# version 01
# baseline (speedup 1.0000x reference)
import os
os.environ.setdefault("NEURON_SCRATCHPAD_PAGE_SIZE", "2048")

import numpy as np
import ml_dtypes
from contextlib import ExitStack

import concourse.bass as bass
import concourse.mybir as mybir
from concourse.bass_utils import run_bass_kernel_spmd

F32 = mybir.dt.float32
BF16 = mybir.dt.bfloat16
AF = mybir.ActivationFunctionType
ALU = mybir.AluOpType
AX = mybir.AxisListType

D = 2048
DI = 4096
DXB = 1024
NST = 16
DCONV = 4
DTR = 128
INTER = 8192
LF = 1024
EXN = 8
HID = [1024, 2048, 3072, 4096, 1024, 2048, 3072, 4096]
COST_LAMBDA = 1e-7
EPS = 1e-5

B = 2
T = 16384
NC = 8
TOKC = 4096
DC = 512
NSUB = 512
SUP = 1024
CH = 512

ENGS = ("sync", "gpsimd", "tensor", "scalar", "vector")
SEMS = ("ld", "gld", "st", "mm", "ev", "sq", "vq", "gq", "cc", "pc")


class KB:
    def __init__(self, nc, block, es):
        self.nc, self.block, self.es = nc, block, es
        self.q = {e: [] for e in ENGS}
        self.sems = {}

    def S(self, name):
        if name not in self.sems:
            h = self.es.enter_context(self.nc.semaphore(name))
            self.sems[name] = [h, 0]
        return self.sems[name]

    def val(self, name):
        return self.S(name)[1]

    def op(self, eng, fn, inc=None, n=1):
        if inc is not None:
            s = self.S(inc)
            s[1] += n

            def run(h, fn=fn, sh=s[0], n=n):
                fn(h).then_inc(sh, n)
        else:
            def run(h, fn=fn):
                fn(h)
        self.q[eng].append(run)
        return self.val(inc) if inc else None

    def raw(self, eng, fn):
        self.q[eng].append(fn)

    def wait(self, eng, name, val=None):
        s = self.S(name)
        v = s[1] if val is None else val
        self.q[eng].append(lambda h, sh=s[0], v=v: h.wait_ge(sh, v))

    def barrier(self, exclude=()):
        for e in ENGS:
            for snm in SEMS:
                if snm in self.sems and snm not in exclude:
                    self.wait(e, snm)

    def flush(self):
        for e in ENGS:
            fns = self.q[e]
            if not fns:
                continue
            self.q[e] = []

            def runner(h, fns=fns):
                for f in fns:
                    f(h)
            getattr(self.block, e)(runner)


def scope_guard(k, engs=("gpsimd", "sync")):
    for e in engs:
        for snm in ("mm", "vq", "sq", "ev", "st"):
            if snm in k.sems:
                k.wait(e, snm)


def v_op(k, fn, dep=False):
    if dep:
        k.wait("vector", "vq")
    k.op("vector", fn, inc="vq")


def s_op(k, fn, dep=False):
    if dep:
        k.wait("scalar", "sq")
    k.op("scalar", fn, inc="sq")


def build_program(stop_after=None, debug=()):
    nc = bass.Bass("TRN2", target_bir_lowering=False)

    def inp(name, shape):
        return nc.dram_tensor(name, list(shape), F32, kind="ExternalInput").ap()

    x0 = inp("x0", (D, TOKC))
    ffn_down_w = inp("ffn_down_w", (D, LF))
    ffn_up_w = inp("ffn_up_w", (LF, D))
    ffn_router_w = inp("ffn_router_w", (D, EXN))
    ffn_shared_w1 = inp("ffn_shared_w1", (LF, LF))
    ffn_shared_w2 = inp("ffn_shared_w2", (LF, LF))
    ffn_core_w = inp("ffn_core_w", (D, D))
    e_w1 = [inp(f"e{i}_w1", (LF, HID[i])) for i in range(EXN)]
    e_w2 = [inp(f"e{i}_w2", (HID[i], LF)) for i in range(EXN)]
    down_b128 = inp("down_b128", (128, LF // 128))
    b1c = [inp(f"b1c{i}", (128, HID[i] // 128)) for i in range(EXN)]
    b2all = inp("b2all", (128, (LF // 128) * EXN))
    sb1_128 = inp("sb1_128", (128, LF // 128))
    sb2_128 = inp("sb2_128", (128, LF // 128))
    upb_128 = inp("upb_128", (128, D // 128))
    coreb_128 = inp("coreb_128", (128, D // 128))
    rb_row = inp("rb_row", (1, EXN))
    cost_row = inp("cost_row", (1, EXN))
    ipwT = inp("ipwT", (D, 1408))
    dtwT = inp("dtwT", (DTR, DC))
    opwT = inp("opwT", (DI, D))
    gatewT = inp("gatewT", (D, INTER))
    upwT = inp("upwT", (D, INTER))
    downwT = inp("downwT", (INTER, D))
    alog128 = inp("alog128", (128, (DC // 128) * NST))
    d128 = inp("d128", (128, DC // 128))
    convw128 = inp("convw128", (128, (DC // 128) * DCONV))
    convb128 = inp("convb128", (128, DC // 128))
    dtb128 = inp("dtb128", (128, DC // 128))
    ln1_128 = inp("ln1_128", (128, D // 128))
    ln2_128 = inp("ln2_128", (128, D // 128))
    ones_col = inp("ones_col", (128, 1))
    ones_row = inp("ones_row", (1, 128))
    ident = inp("ident", (128, 128))
    oh8 = inp("oh8", (EXN, EXN * 128))
    selx = inp("selx", (64, 128))
    selB_all = inp("selB_all", (64, NST * 128))
    selC_all = inp("selC_all", (128, NST * 128))

    def scratch(name, shape, dt=F32):
        return nc.dram_tensor(name, list(shape), dt, kind="Internal").ap()

    g_min_in = scratch("g_min_in", (D, TOKC), BF16)
    g_min_out = scratch("g_min_out", (NC * D, TOKC), BF16)
    g_yg_in = scratch("g_yg_in", (B * DC, T), BF16)
    g_yg_out = scratch("g_yg_out", (NC * B * DC, T), BF16)
    s_ymy = scratch("s_ymy", (DI, TOKC), BF16)
    s_x1 = scratch("s_x1", (D, TOKC))
    s_u = scratch("s_u", (D, TOKC))
    s_xlnbf = scratch("s_xlnbf", (D, TOKC), BF16)
    s_cx = scratch("s_cx", (D, TOKC), BF16)
    s_x2 = scratch("s_x2", (D, TOKC))
    s_m2 = scratch("s_m2", (D, TOKC), BF16)
    s_dbgx = scratch("s_dbgx", (D, NSUB))
    s_s12 = scratch("s_s12", (D, TOKC))
    p_ew1 = [scratch(f"p_e{i}w1", (LF, HID[i]), BF16) for i in range(EXN)]
    p_ew2 = [scratch(f"p_e{i}w2", (HID[i], LF), BF16) for i in range(EXN)]
    p_fup = scratch("p_fup", (LF, D), BF16)
    p_fcore = scratch("p_fcore", (D, D), BF16)
    p_opw = scratch("p_opw", (DI, D), BF16)
    p_gate = scratch("p_gate", (D, INTER), BF16)
    p_up = scratch("p_up", (D, INTER), BF16)
    p_down = scratch("p_down", (INTER, D), BF16)
    xout = nc.dram_tensor("xout", [D, TOKC], F32, kind="ExternalOutput").ap()
    dbg_sb = (nc.dram_tensor("dbg_sb", [128, 2100], F32, kind="ExternalOutput").ap()
              if "p0" in debug else None)

    dbg_outs = []
    srcs = {"x1": s_x1, "u": s_u, "minin": g_min_in, "minout": g_min_out,
            "yg": g_yg_in, "ygout": g_yg_out, "ymy": s_ymy, "x2": s_x2,
            "xlnbf": s_xlnbf, "cx": s_cx, "dbgx": s_dbgx, "m2": s_m2, "s12": s_s12}
    for nm in debug:
        if nm == "p0":
            continue
        srct = srcs[nm]
        dbg_outs.append((nc.dram_tensor("dbg_" + nm, list(srct.shape),
                                        srct.dtype, kind="ExternalOutput").ap(), srct))

    _dbgx_done = []
    es = ExitStack()
    with es:
        block = es.enter_context(nc.Block())
        k = KB(nc, block, es)
        ps = [es.enter_context(nc.psum_tensor(f"ps{i}", [128, 512], F32))
              for i in range(8)]

        def sb(name, p, f, dt=F32, st=es):
            return st.enter_context(nc.sbuf_tensor(name, [p, f], dt))

        c_ones_col = sb("c_ones_col", 128, 1)
        c_ones_row = sb("c_ones_row", 1, 128)
        c_ident = sb("c_ident", 128, 128)
        c_oh8 = sb("c_oh8", EXN, EXN * 128)
        c_selx = sb("c_selx", 64, 128)
        c_selB = sb("c_selB", 64, NST * 128, BF16)
        c_selC = sb("c_selC", 128, NST * 128, BF16)
        c_downb = sb("c_downb", 128, LF // 128)
        c_b1 = [sb(f"c_b1_{i}", 128, HID[i] // 128) for i in range(EXN)]
        c_gb1 = [sb(f"c_gb1_{i}", 128, HID[i] // 128, BF16) for i in range(EXN)]
        c_sb1 = sb("c_sb1", 128, LF // 128)
        c_rbb = sb("c_rbb", 128, EXN)
        c_cc = sb("c_cc", 128, LF // 128)
        c_kneg = sb("c_kneg", EXN, LF)
        c_ucbh = sb("c_ucbh", 128, D // 128)
        c_aneg = sb("c_aneg", 128, (DC // 128) * NST)
        c_d128 = sb("c_d128", 128, DC // 128)
        c_convw = sb("c_convw", 128, (DC // 128) * DCONV)
        c_convb = sb("c_convb", 128, DC // 128)
        c_dtb = sb("c_dtb", 128, DC // 128)
        c_ln1 = sb("c_ln1", 128, D // 128)
        c_ln2 = sb("c_ln2", 128, D // 128)
        c_eps = sb("c_eps", 128, 1)
        c_mhalf = sb("c_mhalf", 128, 1)

        def phase0():
            v_op(k, lambda v: v.memset(c_eps[:, :], EPS))
            v_op(k, lambda v: v.memset(c_mhalf[:, :], -0.5))
            def precast(dst, srcap, rows):
                step = max(1, (4 << 20) // max(1, int(np.prod(srcap.shape[1:]) * 4)))
                r = 0
                while r < rows:
                    rr = min(step, rows - r)
                    k.op("gpsimd", lambda e, d=dst, s=srcap, r=r, rr=rr:
                         e.dma_start(d[r:r + rr, :], s[r:r + rr, :]), inc="pc", n=16)
                    r += rr
            for i in range(EXN):
                precast(p_ew1[i], e_w1[i], LF)
                precast(p_ew2[i], e_w2[i], HID[i])
            precast(p_fup, ffn_up_w, LF)
            precast(p_fcore, ffn_core_w, D)
            precast(p_opw, opwT, DI)
            precast(p_gate, gatewT, D)
            precast(p_up, upwT, D)
            precast(p_down, downwT, INTER)

            loads = [
                (c_ones_col, ones_col), (c_ones_row, ones_row), (c_ident, ident),
                (c_selx, selx), (c_downb, down_b128), (c_sb1, sb1_128),
                (c_d128, d128), (c_convw, convw128), (c_convb, convb128),
                (c_dtb, dtb128), (c_ln1, ln1_128), (c_ln2, ln2_128),
            ]
            for dst, srcd in loads:
                k.op("sync", lambda e, d=dst, s=srcd: e.dma_start(d[:, :], s[:, :]),
                     inc="ld", n=16)
            for i in range(EXN):
                k.op("sync", lambda e, d=c_b1[i], s=b1c[i]: e.dma_start(d[:, :], s[:, :]),
                     inc="ld", n=16)
            for dst, srcd in [(c_selB, selB_all), (c_selC, selC_all), (c_oh8, oh8)]:
                if dst.dtype == BF16:
                    k.op("gpsimd", lambda e, d=dst, s=srcd: e.dma_start(d[:, :], s[:, :]),
                         inc="gld", n=16)
                else:
                    k.op("sync", lambda e, d=dst, s=srcd: e.dma_start(d[:, :], s[:, :]),
                         inc="ld", n=16)
            p0es = ExitStack()
            t_rb = sb("t_rb", 1, EXN, F32, p0es)
            t_cost = sb("t_cost", 1, EXN, F32, p0es)
            t_rbeff = sb("t_rbeff", 1, EXN, F32, p0es)
            t_upb = sb("t_upb", 128, D // 128, F32, p0es)
            t_coreb = sb("t_coreb", 128, D // 128, F32, p0es)
            t_ucs = sb("t_ucs", 128, D // 128, F32, p0es)
            t_b2all = sb("t_b2all", 128, (LF // 128) * EXN, F32, p0es)
            t_sb2 = sb("t_sb2", 128, LF // 128, F32, p0es)
            t_sb2t = sb("t_sb2t", 128, LF // 128, F32, p0es)
            t_ksum = sb("t_ksum", 1, LF, F32, p0es)
            t_krow = sb("t_krow", 1, LF, F32, p0es)
            t_alog = sb("t_alog", 128, (DC // 128) * NST, F32, p0es)
            t_aexp = sb("t_aexp", 128, (DC // 128) * NST, F32, p0es)
            for dst, srcd in [(t_rb, rb_row), (t_cost, cost_row), (t_upb, upb_128),
                              (t_coreb, coreb_128), (t_b2all, b2all),
                              (t_sb2, sb2_128), (t_alog, alog128)]:
                k.op("sync", lambda e, d=dst, s=srcd: e.dma_start(d[:, :], s[:, :]),
                     inc="ld", n=16)

            k.wait("vector", "ld")
            k.wait("vector", "gld")
            v_op(k, lambda v: v.tensor_sub(t_rbeff[:, :], t_rb[:, :], t_cost[:, :]))
            k.wait("tensor", "vq")
            k.op("tensor", lambda t: t.matmul(ps[0][:, 0:EXN], c_ones_row[:, :],
                                              t_rbeff[:, :], start=True, stop=True),
                 inc="mm")
            k.wait("vector", "mm")
            v_op(k, lambda v: v.tensor_copy(c_rbb[:, :], ps[0][:, 0:EXN]))
            v_op(k, lambda v: v.tensor_add(t_ucs[:, :], t_upb[:, :], t_coreb[:, :]))
            k.wait("vector", "vq")
            v_op(k, lambda v: v.tensor_scalar(c_ucbh[:, :], t_ucs[:, :], 0.5,
                                              None, op0=ALU.mult))
            k.wait("scalar", "ld")
            for i in range(EXN):
                s_op(k, lambda s, i=i: s.activation(c_gb1[i][:, :], c_b1[i][:, :],
                                                    AF.Gelu))
            s_op(k, lambda s: s.activation(t_aexp[:, :], t_alog[:, :], AF.Exp))
            s_op(k, lambda s: s.mul(c_aneg[:, :], t_aexp[:, :], -1.0), dep=True)
            v_op(k, lambda v: v.memset(t_ksum[:, :], 0.0))
            wbuf0 = sb("wbuf0", 128, LF, BF16, p0es)
            gb_done = k.val("sq")
            for ei in range(EXN):
                ke = HID[ei] // 128
                for mb in range(ke):
                    gv = k.op("gpsimd", lambda e, ei=ei, mb=mb: e.dma_start(
                        wbuf0[:, :], e_w2[ei][mb * 128:(mb + 1) * 128, :]),
                        inc="gld", n=16)
                    k.wait("tensor", "gld", gv)
                    k.wait("tensor", "sq")
                    last = (mb == ke - 1)
                    for half in range(2):
                        k.op("tensor", lambda t, ei=ei, mb=mb, half=half, ke=ke:
                             t.matmul(ps[half][0:1, :],
                                      c_gb1[ei][:, mb:mb + 1],
                                      wbuf0[:, half * 512:(half + 1) * 512],
                                      start=(mb == 0), stop=(mb == ke - 1)),
                             inc="mm" if (last and half == 1) else None)
                    if not last:
                        k.op("tensor", lambda t: t.matmul(
                            ps[2][0:1, 0:1], c_ones_row[0:1, 0:1],
                            c_ones_row[0:1, 0:1], start=True, stop=True), inc="mm")
                    k.wait("gpsimd", "mm")
                k.wait("scalar", "mm")
                k.wait("scalar", "ld")
                k.wait("scalar", "st")
                s_op(k, lambda s: s.activation(t_krow[0:1, 0:512], ps[0][0:1, :],
                                               AF.Copy, bias=0.0, scale=-1.0))
                s_op(k, lambda s: s.activation(t_krow[0:1, 512:1024], ps[1][0:1, :],
                                               AF.Copy, bias=0.0, scale=-1.0))
                k.wait("vector", "sq")
                v_op(k, lambda v: v.tensor_sub(t_ksum[:, :], t_ksum[:, :],
                                               t_krow[:, :]), dep=True)
                k.wait("sync", "sq")
                k.op("sync", lambda e, ei=ei: e.dma_start(
                    c_kneg[ei:ei + 1, :], t_krow[:, :]), inc="st", n=16)
            v_op(k, lambda v: v.tensor_reduce(
                c_cc[:, :].rearrange("p (m one) -> p m one", one=1),
                t_b2all[:, :].rearrange("p (m e) -> p m e", e=EXN),
                op=ALU.add, axis=AX.X), dep=True)
            v_op(k, lambda v: v.tensor_scalar(t_sb2t[:, :], t_sb2[:, :], 0.1,
                                              None, op0=ALU.mult), dep=True)
            k.wait("vector", "sq")
            v_op(k, lambda v: v.tensor_add(c_cc[:, :], c_cc[:, :], t_sb2t[:, :]),
                 dep=True)
            for mo in range(LF // 128):
                k.wait("tensor", "vq")
                k.op("tensor", lambda t, mo=mo: t.matmul(
                    ps[3][:, 0:1], t_ksum[0:1, mo * 128:(mo + 1) * 128],
                    c_ones_row[0:1, 0:1], start=True, stop=True), inc="mm")
                k.wait("vector", "mm")
                v_op(k, lambda v, mo=mo: v.tensor_add(
                    c_cc[:, mo:mo + 1], c_cc[:, mo:mo + 1], ps[3][:, 0:1]), dep=True)
            k.flush()
            p0es.close()

        def ffn_phase(src, call):
            out_t = s_x1 if call == 1 else xout
            pes = ExitStack()
            with pes:
                for sup in range(TOKC // SUP):
                    spes = ExitStack()
                    with spes:
                        accb = sb(f"f{call}s{sup}_acc", 128, 8 * SUP, F32, spes)
                        wfm = sb(f"f{call}s{sup}_wfm", EXN, SUP, F32, spes)
                        for _e in ("sync", "gpsimd", "tensor", "scalar", "vector"):
                            for _s in ("mm", "vq", "sq", "st"):
                                k.wait(_e, _s)
                        v_op(k, lambda v: v.memset(accb[:, :], 0.0))
                        for m in range(LF // 128):
                            v_op(k, lambda v, m=m: v.tensor_scalar(
                                accb[:, m * SUP:(m + 1) * SUP],
                                accb[:, m * SUP:(m + 1) * SUP],
                                c_cc[:, m:m + 1], None, op0=ALU.add), dep=True)
                        apes = ExitStack()
                        with apes:
                            gHs = sb(f"f{call}s{sup}_gH", 128, 8 * SUP, BF16, apes)
                            a2 = ExitStack()
                            with a2:
                                dwn_sb = sb(f"fA{call}{sup}_dwn", 128, 16 * LF, BF16, a2)
                                rtr_sb = sb(f"fA{call}{sup}_rtr", 128, 16 * EXN, BF16, a2)
                                k.wait("gpsimd", "mm")
                                k.op("gpsimd", lambda e: e.dma_start(
                                    dwn_sb[:, :].rearrange("p (kb m) -> p kb m", m=LF),
                                    ffn_down_w.rearrange("(kb p) m -> p kb m", p=128)),
                                    inc="gld", n=16)
                                k.op("gpsimd", lambda e: e.dma_start(
                                    rtr_sb[:, :].rearrange("p (kb m) -> p kb m", m=EXN),
                                    ffn_router_w.rearrange("(kb p) m -> p kb m", p=128)),
                                    inc="gld", n=16)
                                x0_sb = sb(f"fA{call}{sup}_x0", 128, 16 * NSUB, F32, a2)
                                xln = sb(f"fA{call}{sup}_xln", 128, 16 * NSUB, BF16, a2)
                                cxl = sb(f"fA{call}{sup}_cx", 128, 16 * NSUB, BF16, a2)
                                tmp = sb(f"fA{call}{sup}_tmp", 128, NSUB, F32, a2)
                                xsq = sb(f"fA{call}{sup}_xsq", 128, NSUB, F32, a2)
                                rmu = sb(f"fA{call}{sup}_rmu", 1, NSUB, F32, a2)
                                rms_ = sb(f"fA{call}{sup}_rms", 1, NSUB, F32, a2)
                                rm2 = sb(f"fA{call}{sup}_rm2", 1, NSUB, F32, a2)
                                rva = sb(f"fA{call}{sup}_rva", 1, NSUB, F32, a2)
                                rrs = sb(f"fA{call}{sup}_rrs", 1, NSUB, F32, a2)
                                rmr = sb(f"fA{call}{sup}_rmr", 1, NSUB, F32, a2)
                                rr = [sb(f"fA{call}{sup}_r{i}", 128,
                                         EXN if i > 1 else 1, F32, a2)
                                      for i in range(12)]
                                r_sc = [sb(f"fA{call}{sup}_s{i}", 128, 1, F32, a2)
                                        for i in range(6)]
                                k.barrier() if sup == 0 and call == 2 else None
                                for ncs in range(SUP // NSUB):
                                    tok0 = sup * SUP + ncs * NSUB
                                    emit_ln_stage(k, nc, ps, src, tok0, x0_sb, xln,
                                                  cxl, tmp, xsq, rmu, rms_, rm2, rva,
                                                  rrs, rmr, c_ones_col, c_ones_row,
                                                  s_xlnbf, s_cx, store_ln=True)
                                    k.wait("tensor", "gld")
                                    k.wait("tensor", "vq")
                                    for m in range(LF // 128):
                                        pb = ps[m % 2]
                                        if m >= 2:
                                            k.wait("tensor", "sq")
                                        for kk in range(16):
                                            k.op("tensor", lambda t, m=m, kk=kk, pb=pb:
                                                 t.matmul(pb[:, :],
                                                          dwn_sb[:, kk * LF + m * 128:
                                                                 kk * LF + m * 128 + 128],
                                                          xln[:, kk * NSUB:(kk + 1) * NSUB],
                                                          start=(kk == 0), stop=(kk == 15)),
                                                 inc="mm" if kk == 15 else None)
                                        k.wait("scalar", "mm")
                                        s_op(k, lambda s, m=m, pb=pb, ncs=ncs: s.activation(
                                            gHs[:, m * SUP + ncs * NSUB:
                                                m * SUP + ncs * NSUB + NSUB],
                                            pb[:, :], AF.Gelu,
                                            bias=c_downb[:, m:m + 1], scale=1.0))
                                    emit_router(k, nc, ps, xln, rtr_sb, c_rbb, c_ident,
                                                c_ones_col, wfm, ncs, rr, r_sc)
                            bpes = ExitStack()
                            with bpes:
                                G_sb = sb(f"fB{call}{sup}_G", 128, 64 * NSUB, BF16, bpes)
                                w1b = [sb(f"fB{call}{sup}_w1{i}", 128, 8 * 128, BF16,
                                          bpes) for i in range(2)]
                                w2b = [sb(f"fB{call}{sup}_w2{i}", 128, 32 * 128, BF16,
                                          bpes) for i in range(2)]
                                wbs = [sb(f"fB{call}{sup}_wb{i}", 128, NSUB, F32, bpes)
                                       for i in range(2)]
                                gtmp = sb(f"fB{call}{sup}_gt", 128, NSUB, F32, bpes)
                                k.wait("gpsimd", "pc")
                                scope_guard(k)
                                bank_sq = {0: k.val("sq"), 1: k.val("sq")}
                                w1_slot_mm = {0: 0, 1: 0}
                                w2_slot_mm = {0: 0, 1: 0}
                                for ei in range(EXN):
                                    ke = HID[ei] // 128
                                    k.wait("tensor", "vq")
                                    k.wait("tensor", "sq")
                                    for h2 in range(2):
                                        k.op("tensor", lambda t, ei=ei, h2=h2: t.matmul(
                                            ps[4 + h2][:, :],
                                            c_oh8[:, ei * 128:(ei + 1) * 128],
                                            wfm[:, h2 * NSUB:(h2 + 1) * NSUB],
                                            start=True, stop=True), inc="mm")
                                    k.wait("vector", "mm")
                                    for h2 in range(2):
                                        v_op(k, lambda v, h2=h2: v.tensor_copy(
                                            wbs[h2][:, :], ps[4 + h2][:, :]))
                                    wbv = k.val("vq")
                                    for mb in range(ke):
                                        slot = mb % 2
                                        k.wait("gpsimd", "mm", w1_slot_mm[slot])
                                        gv = k.op("gpsimd", lambda e, ei=ei, mb=mb,
                                                  slot=slot: e.dma_start(
                                                      w1b[slot][:, :].rearrange(
                                                          "p (kb m) -> p kb m", m=128),
                                                      p_ew1[ei][:, mb * 128:(mb + 1) * 128]
                                                      .rearrange("(kb p) m -> p kb m",
                                                                 p=128)),
                                                  inc="gld", n=16)
                                        k.wait("tensor", "gld", gv)
                                        for h2 in range(2):
                                            pb = ps[h2]
                                            k.wait("tensor", "sq", bank_sq[h2])
                                            for kk in range(8):
                                                k.op("tensor", lambda t, kk=kk, slot=slot,
                                                     pb=pb, h2=h2: t.matmul(
                                                         pb[:, :],
                                                         w1b[slot][:, kk * 128:(kk + 1) * 128],
                                                         gHs[:, kk * SUP + h2 * NSUB:
                                                             kk * SUP + h2 * NSUB + NSUB],
                                                         start=(kk == 0), stop=(kk == 7)),
                                                     inc="mm" if kk == 7 else None)
                                            k.wait("scalar", "mm")
                                            k.wait("scalar", "vq")
                                            s_op(k, lambda s, ei=ei, mb=mb, pb=pb:
                                                 s.activation(gtmp[:, :], pb[:, :],
                                                              AF.Gelu,
                                                              bias=c_b1[ei][:, mb:mb + 1],
                                                              scale=1.0), dep=True)
                                            bank_sq[h2] = k.val("sq")
                                            w1_slot_mm[slot] = k.val("mm")
                                            k.wait("vector", "sq")
                                            v_op(k, lambda v, mb=mb, h2=h2: v.tensor_mul(
                                                G_sb[:, (2 * mb + h2) * NSUB:
                                                     (2 * mb + h2 + 1) * NSUB],
                                                gtmp[:, :], wbs[h2][:, :]))
                                    gdone = k.val("vq")
                                    for mo in range(LF // 128):
                                        slot = mo % 2
                                        k.wait("gpsimd", "mm", w2_slot_mm[slot])
                                        wv2 = k.op("gpsimd", lambda e, ei=ei, mo=mo,
                                                   slot=slot, ke=ke: e.dma_start(
                                                       w2b[slot][:, 0:ke * 128].rearrange(
                                                           "p (kb m) -> p kb m", m=128),
                                                       p_ew2[ei][:, mo * 128:(mo + 1) * 128]
                                                       .rearrange("(kb p) m -> p kb m",
                                                                  p=128)),
                                                   inc="gld", n=16)
                                        k.wait("tensor", "gld", wv2)
                                        k.wait("tensor", "vq", gdone)
                                        if mo >= 1:
                                            k.wait("tensor", "vq")
                                        for h2 in range(2):
                                            pb = ps[2 + h2]
                                            for mb in range(ke):
                                                k.op("tensor", lambda t, mb=mb, slot=slot,
                                                     pb=pb, h2=h2, ke=ke: t.matmul(
                                                         pb[:, :],
                                                         w2b[slot][:, mb * 128:(mb + 1) * 128],
                                                         G_sb[:, (2 * mb + h2) * NSUB:
                                                              (2 * mb + h2 + 1) * NSUB],
                                                         start=(mb == 0), stop=(mb == ke - 1)),
                                                     inc="mm" if mb == ke - 1 else None)
                                        w2_slot_mm[slot] = k.val("mm")
                                        k.wait("vector", "mm")
                                        for h2 in range(2):
                                            v_op(k, lambda v, mo=mo, h2=h2: v.tensor_add(
                                                accb[:, mo * SUP + h2 * NSUB:
                                                     mo * SUP + h2 * NSUB + NSUB],
                                                accb[:, mo * SUP + h2 * NSUB:
                                                     mo * SUP + h2 * NSUB + NSUB],
                                                ps[2 + h2][:, :]), dep=(h2 == 0))
                            cpes = ExitStack()
                            with cpes:
                                s1b = sb(f"fC{call}{sup}_s1", 128, 8 * NSUB, BF16, cpes)
                                shb = [sb(f"fC{call}{sup}_sh{i}", 128, 8 * 128, BF16,
                                          cpes) for i in range(2)]
                                scope_guard(k)
                                sh_slot_mm = {0: 0, 1: 0}
                                for h2 in range(SUP // NSUB):
                                    for mo in range(LF // 128):
                                        slot = mo % 2
                                        k.wait("gpsimd", "mm", sh_slot_mm[slot])
                                        sv = k.op("gpsimd", lambda e, mo=mo, slot=slot:
                                                  e.dma_start(
                                                      shb[slot][:, :].rearrange(
                                                          "p (kb m) -> p kb m", m=128),
                                                      ffn_shared_w1[:, mo * 128:(mo + 1) * 128]
                                                      .rearrange("(kb p) m -> p kb m", p=128)),
                                                  inc="gld", n=16)
                                        k.wait("tensor", "gld", sv)
                                        pb = ps[mo % 2]
                                        k.wait("tensor", "sq")
                                        k.wait("tensor", "vq")
                                        for kk in range(8):
                                            k.op("tensor", lambda t, kk=kk, slot=slot,
                                                 pb=pb, h2=h2: t.matmul(
                                                     pb[:, :],
                                                     shb[slot][:, kk * 128:(kk + 1) * 128],
                                                     gHs[:, kk * SUP + h2 * NSUB:
                                                         kk * SUP + h2 * NSUB + NSUB],
                                                     start=(kk == 0), stop=(kk == 7)),
                                                 inc="mm" if kk == 7 else None)
                                        sh_slot_mm[slot] = k.val("mm")
                                        k.wait("scalar", "mm")
                                        s_op(k, lambda s, mo=mo, pb=pb: s.activation(
                                            s1b[:, mo * NSUB:(mo + 1) * NSUB], pb[:, :],
                                            AF.Gelu, bias=c_sb1[:, mo:mo + 1], scale=1.0))
                                    for mo in range(LF // 128):
                                        slot = mo % 2
                                        k.wait("gpsimd", "mm", sh_slot_mm[slot])
                                        sv = k.op("gpsimd", lambda e, mo=mo, slot=slot:
                                                  e.dma_start(
                                                      shb[slot][:, :].rearrange(
                                                          "p (kb m) -> p kb m", m=128),
                                                      ffn_shared_w2[:, mo * 128:(mo + 1) * 128]
                                                      .rearrange("(kb p) m -> p kb m", p=128)),
                                                  inc="gld", n=16)
                                        k.wait("tensor", "gld", sv)
                                        pb = ps[2 + mo % 2]
                                        if mo >= 2:
                                            k.wait("tensor", "vq")
                                        k.wait("tensor", "sq")
                                        for kk in range(8):
                                            k.op("tensor", lambda t, kk=kk, slot=slot,
                                                 pb=pb: t.matmul(
                                                     pb[:, :],
                                                     shb[slot][:, kk * 128:(kk + 1) * 128],
                                                     s1b[:, kk * NSUB:(kk + 1) * NSUB],
                                                     start=(kk == 0), stop=(kk == 7)),
                                                 inc="mm" if kk == 7 else None)
                                        sh_slot_mm[slot] = k.val("mm")
                                        k.wait("vector", "mm")
                                        v_op(k, lambda v, mo=mo, h2=h2, pb=pb:
                                             v.scalar_tensor_tensor(
                                                 accb[:, mo * SUP + h2 * NSUB:
                                                      mo * SUP + h2 * NSUB + NSUB],
                                                 pb[:, :], 0.1,
                                                 accb[:, mo * SUP + h2 * NSUB:
                                                      mo * SUP + h2 * NSUB + NSUB],
                                                 ALU.mult, ALU.add), dep=True)
                                    k.wait("tensor", "st")
                                    for mo in range(LF // 128):
                                        pb = ps[4 + mo % 2]
                                        if mo >= 2:
                                            k.wait("tensor", "vq")
                                        k.op("tensor", lambda t, mo=mo, h2=h2, pb=pb:
                                             t.matmul(pb[:, :],
                                                      c_kneg[:, mo * 128:(mo + 1) * 128],
                                                      wfm[:, h2 * NSUB:(h2 + 1) * NSUB],
                                                      start=True, stop=True), inc="mm")
                                        k.wait("vector", "mm")
                                        v_op(k, lambda v, mo=mo, h2=h2, pb=pb: v.tensor_add(
                                            accb[:, mo * SUP + h2 * NSUB:
                                                 mo * SUP + h2 * NSUB + NSUB],
                                            accb[:, mo * SUP + h2 * NSUB:
                                                 mo * SUP + h2 * NSUB + NSUB],
                                            pb[:, :]), dep=True)
                        dpes = ExitStack()
                        with dpes:
                            xnew = sb(f"fD{call}{sup}_xn", 128, 16 * NSUB, F32, dpes)
                            u_sb = sb(f"fD{call}{sup}_u", 128, 16 * NSUB, F32, dpes)
                            cxl2 = sb(f"fD{call}{sup}_cx", 128, 16 * NSUB, BF16, dpes)
                            minb = sb(f"fD{call}{sup}_mb", 128, NSUB, BF16, dpes)
                            abf = sb(f"fD{call}{sup}_ab", 128, 8 * NSUB, BF16, dpes)
                            upw2 = [sb(f"fD{call}{sup}_up{i}", 128, 8 * 128, BF16, dpes)
                                    for i in range(2)]
                            cow2 = [sb(f"fD{call}{sup}_co{i}", 128, 16 * 128, BF16, dpes)
                                    for i in range(2)]
                            x0t = sb(f"fD{call}{sup}_x0t", 128, NSUB, F32, dpes)
                            t1t = sb(f"fD{call}{sup}_t1", 128, NSUB, F32, dpes)
                            tmp2 = sb(f"fD{call}{sup}_tp", 128, NSUB, F32, dpes)
                            xsq2 = sb(f"fD{call}{sup}_xq", 128, NSUB, F32, dpes)
                            rmu2 = sb(f"fD{call}{sup}_ru", 1, NSUB, F32, dpes)
                            rms2 = sb(f"fD{call}{sup}_rm", 1, NSUB, F32, dpes)
                            rm22 = sb(f"fD{call}{sup}_r2", 1, NSUB, F32, dpes)
                            rva2 = sb(f"fD{call}{sup}_rv", 1, NSUB, F32, dpes)
                            rrs2 = sb(f"fD{call}{sup}_rr", 1, NSUB, F32, dpes)
                            rmr2 = sb(f"fD{call}{sup}_rx", 1, NSUB, F32, dpes)
                            scope_guard(k)
                            k.wait("vector", "mm")
                            k.wait("vector", "st")
                            d_slot_mm = {0: 0, 1: 0}
                            for ncs in range(SUP // NSUB):
                                tok0 = sup * SUP + ncs * NSUB
                                k.wait("vector", "vq")
                                k.wait("vector", "mm")
                                for m in range(LF // 128):
                                    v_op(k, lambda v, m=m, ncs=ncs: v.tensor_copy(
                                        abf[:, m * NSUB:(m + 1) * NSUB],
                                        accb[:, m * SUP + ncs * NSUB:
                                             m * SUP + ncs * NSUB + NSUB]))
                                abv = k.val("vq")
                                cv = None
                                k.wait("sync", "mm")
                                for kk in range(16):
                                    cv = k.op("sync", lambda e, kk=kk, tok0=tok0:
                                              e.dma_start(
                                                  cxl2[:, kk * NSUB:(kk + 1) * NSUB],
                                                  s_cx[kk * 128:(kk + 1) * 128,
                                                       tok0:tok0 + NSUB]),
                                              inc="ld", n=16)
                                for mo in range(D // 128):
                                    slot = mo % 2
                                    k.wait("gpsimd", "mm", d_slot_mm[slot])
                                    uv = k.op("gpsimd", lambda e, mo=mo, slot=slot:
                                              e.dma_start(
                                                  upw2[slot][:, :].rearrange(
                                                      "p (kb m) -> p kb m", m=128),
                                                  p_fup[:, mo * 128:(mo + 1) * 128]
                                                  .rearrange("(kb p) m -> p kb m", p=128)),
                                              inc="gld", n=16)
                                    cv2 = k.op("gpsimd", lambda e, mo=mo, slot=slot:
                                               e.dma_start(
                                                   cow2[slot][:, :].rearrange(
                                                       "p (kb m) -> p kb m", m=128),
                                                   p_fcore[:, mo * 128:(mo + 1) * 128]
                                                   .rearrange("(kb p) m -> p kb m", p=128)),
                                               inc="gld", n=16)
                                    k.wait("tensor", "gld", cv2)
                                    k.wait("tensor", "ld", cv)
                                    k.wait("tensor", "vq", abv)
                                    pb = ps[mo % 2]
                                    if mo >= 2:
                                        k.wait("tensor", "vq")
                                    for kk in range(8):
                                        k.op("tensor", lambda t, kk=kk, slot=slot, pb=pb:
                                             t.matmul(pb[:, :],
                                                      upw2[slot][:, kk * 128:(kk + 1) * 128],
                                                      abf[:, kk * NSUB:(kk + 1) * NSUB],
                                                      start=(kk == 0), stop=False))
                                    for kk in range(16):
                                        k.op("tensor", lambda t, kk=kk, slot=slot, pb=pb:
                                             t.matmul(pb[:, :],
                                                      cow2[slot][:, kk * 128:(kk + 1) * 128],
                                                      cxl2[:, kk * NSUB:(kk + 1) * NSUB],
                                                      start=False, stop=(kk == 15)),
                                             inc="mm" if kk == 15 else None)
                                    d_slot_mm[slot] = k.val("mm")
                                    k.wait("sync", "vq")
                                    xv = k.op("sync", lambda e, mo=mo, tok0=tok0:
                                              e.dma_start(
                                                  x0t[:, :],
                                                  src[mo * 128:(mo + 1) * 128,
                                                      tok0:tok0 + NSUB]),
                                              inc="ld", n=16)
                                    k.wait("vector", "mm")
                                    k.wait("vector", "vq")
                                    v_op(k, lambda v, mo=mo, pb=pb: v.tensor_scalar(
                                        t1t[:, :], pb[:, :], 0.5,
                                        c_ucbh[:, mo:mo + 1], op0=ALU.mult,
                                        op1=ALU.add))
                                    k.wait("vector", "ld", xv)
                                    v_op(k, lambda v, mo=mo: v.tensor_add(
                                        xnew[:, mo * NSUB:(mo + 1) * NSUB],
                                        x0t[:, :], t1t[:, :]), dep=True)
                                xnv = k.val("vq")
                                k.wait("sync", "vq", xnv)
                                for kk in range(16):
                                    k.op("sync", lambda e, kk=kk, tok0=tok0:
                                         e.dma_start(
                                             out_t[kk * 128:(kk + 1) * 128,
                                                   tok0:tok0 + NSUB],
                                             xnew[:, kk * NSUB:(kk + 1) * NSUB]),
                                         inc="st", n=16)
                                if call == 1:
                                    emit_ln_simple(k, nc, ps, xnew, u_sb, tmp2, xsq2,
                                                   rmu2, rms2, rm22, rva2, rrs2, rmr2,
                                                   c_ones_col, c_ones_row, D)
                                    uv2 = k.val("vq")
                                    k.wait("sync", "vq", uv2)
                                    for kk in range(16):
                                        k.op("sync", lambda e, kk=kk, tok0=tok0:
                                             e.dma_start(
                                                 s_u[kk * 128:(kk + 1) * 128,
                                                     tok0:tok0 + NSUB],
                                                 u_sb[:, kk * NSUB:(kk + 1) * NSUB]),
                                             inc="st", n=16)
                                    emit_rms_prep(k, nc, ps, u_sb, xsq2, rms2, rrs2,
                                                  c_ones_col, c_ones_row, D)
                                    for kk in range(16):
                                        k.wait("vector", "st")
                                        v_op(k, lambda v, kk=kk: v.scalar_tensor_tensor(
                                            minb[:, :],
                                            u_sb[:, kk * NSUB:(kk + 1) * NSUB],
                                            c_ln1[:, kk:kk + 1], ps[4][:, :],
                                            ALU.mult, ALU.mult), dep=True)
                                        mv = k.val("vq")
                                        k.wait("sync", "vq", mv)
                                        k.op("sync", lambda e, kk=kk, tok0=tok0:
                                             e.dma_start(
                                                 g_min_in[kk * 128:(kk + 1) * 128,
                                                          tok0:tok0 + NSUB],
                                                 minb[:, :]), inc="st", n=16)
                                k.wait("vector", "st")
                    k.flush()

        def emit_ln_stage(k, nc, ps, src, tok0, x0_sb, xln, cxl, tmp, xsq,
                          rmu, rms_, rm2, rva, rrs, rmr, ones_c, ones_r,
                          st_ln, st_cx, store_ln):
            k.wait("sync", "vq")
            k.wait("sync", "sq")
            k.wait("sync", "mm")
            k.wait("vector", "st")
            k.wait("scalar", "st")
            ldv = None
            for kk in range(16):
                ldv = k.op("sync", lambda e, kk=kk: e.dma_start(
                    x0_sb[:, kk * NSUB:(kk + 1) * NSUB],
                    src[kk * 128:(kk + 1) * 128, tok0:tok0 + NSUB]), inc="ld", n=16)
            k.wait("tensor", "ld", ldv)
            k.wait("tensor", "vq")
            k.wait("tensor", "sq")
            for kk in range(16):
                k.op("tensor", lambda t, kk=kk: t.matmul(
                    ps[6][0:1, :], ones_c[:, :], x0_sb[:, kk * NSUB:(kk + 1) * NSUB],
                    start=(kk == 0), stop=(kk == 15)), inc="mm" if kk == 15 else None)
            k.wait("scalar", "ld", ldv)
            for kk in range(16):
                s_op(k, lambda s, kk=kk: s.activation(
                    xsq[:, :], x0_sb[:, kk * NSUB:(kk + 1) * NSUB], AF.Square),
                    dep=True)
                k.wait("tensor", "sq")
                k.op("tensor", lambda t, kk=kk: t.matmul(
                    ps[7][0:1, :], ones_c[:, :], xsq[:, :],
                    start=(kk == 0), stop=(kk == 15)), inc="mm" if kk == 15 else None)
                if kk < 15:
                    k.op("tensor", lambda t: t.matmul(
                        ps[5][0:1, 0:1], ones_r[0:1, 0:1], ones_r[0:1, 0:1],
                        start=True, stop=True), inc="mm")
                    k.wait("scalar", "mm")
            k.wait("vector", "mm")
            v_op(k, lambda v: v.tensor_scalar(rmu[:, :], ps[6][0:1, :], 1.0 / D,
                                              None, op0=ALU.mult))
            v_op(k, lambda v: v.tensor_scalar(rms_[:, :], ps[7][0:1, :], 1.0 / D,
                                              None, op0=ALU.mult))
            k.wait("vector", "vq")
            v_op(k, lambda v: v.tensor_mul(rm2[:, :], rmu[:, :], rmu[:, :]))
            k.wait("vector", "vq")
            v_op(k, lambda v: v.tensor_sub(rva[:, :], rms_[:, :], rm2[:, :]))
            k.wait("scalar", "vq")
            s_op(k, lambda s: s.activation(rva[:, :], rva[:, :], AF.Ln, bias=c_eps[0:1, :],
                                           scale=1.0), dep=True)
            s_op(k, lambda s: s.activation(rrs[:, :], rva[:, :], AF.Exp, bias=0.0,
                                           scale=c_mhalf[0:1, :]), dep=True)
            k.wait("vector", "sq")
            v_op(k, lambda v: v.tensor_mul(rmr[:, :], rmu[:, :], rrs[:, :]))
            k.wait("tensor", "vq")
            k.op("tensor", lambda t: t.matmul(ps[4][:, :], ones_r[:, :], rrs[:, :],
                                              start=True, stop=True), inc="mm")
            k.op("tensor", lambda t: t.matmul(ps[5][:, :], ones_r[:, :], rmr[:, :],
                                              start=True, stop=True), inc="mm")
            k.wait("vector", "mm")
            k.wait("scalar", "mm")
            if tok0 == 0 and not _dbgx_done:
                _dbgx_done.append(1)
                k.wait("sync", "mm")
                for kk in range(16):
                    k.op("sync", lambda e, kk=kk: e.dma_start(
                        s_dbgx[kk * 128:(kk + 1) * 128, :],
                        x0_sb[:, kk * NSUB:(kk + 1) * NSUB]), inc="st", n=16)
            for kk in range(16):
                v_op(k, lambda v, kk=kk: v.tensor_mul(
                    tmp[:, :], x0_sb[:, kk * NSUB:(kk + 1) * NSUB], ps[4][:, :]),
                    dep=True)
                k.wait("vector", "vq")
                v_op(k, lambda v, kk=kk: v.tensor_sub(
                    xln[:, kk * NSUB:(kk + 1) * NSUB], tmp[:, :], ps[5][:, :]))
                k.wait("scalar", "vq")
                s_op(k, lambda s, kk=kk: s.activation(
                    cxl[:, kk * NSUB:(kk + 1) * NSUB],
                    xln[:, kk * NSUB:(kk + 1) * NSUB], AF.Gelu))
            xlv = k.val("vq")
            clv = k.val("sq")
            if store_ln:
                k.wait("sync", "vq", xlv)
                k.wait("sync", "sq", clv)
                for kk in range(16):
                    k.op("sync", lambda e, kk=kk: e.dma_start(
                        st_ln[kk * 128:(kk + 1) * 128, tok0:tok0 + NSUB],
                        xln[:, kk * NSUB:(kk + 1) * NSUB]), inc="st", n=16)
                    k.op("sync", lambda e, kk=kk: e.dma_start(
                        st_cx[kk * 128:(kk + 1) * 128, tok0:tok0 + NSUB],
                        cxl[:, kk * NSUB:(kk + 1) * NSUB]), inc="st", n=16)
                k.wait("sync", "st")

        def emit_ln_simple(k, nc, ps, xin_sb, out_sb, tmp, xsq, rmu, rms_, rm2,
                           rva, rrs, rmr, ones_c, ones_r, DD):
            nt = 16
            k.wait("tensor", "vq")
            k.wait("tensor", "sq")
            k.wait("scalar", "vq")
            for kk in range(nt):
                k.op("tensor", lambda t, kk=kk: t.matmul(
                    ps[6][0:1, :], ones_c[:, :], xin_sb[:, kk * NSUB:(kk + 1) * NSUB],
                    start=(kk == 0), stop=(kk == nt - 1)),
                    inc="mm" if kk == nt - 1 else None)
            for kk in range(nt):
                s_op(k, lambda s, kk=kk: s.activation(
                    xsq[:, :], xin_sb[:, kk * NSUB:(kk + 1) * NSUB], AF.Square),
                    dep=True)
                k.wait("tensor", "sq")
                k.op("tensor", lambda t, kk=kk: t.matmul(
                    ps[7][0:1, :], ones_c[:, :], xsq[:, :],
                    start=(kk == 0), stop=(kk == nt - 1)),
                    inc="mm" if kk == nt - 1 else None)
                if kk < nt - 1:
                    k.op("tensor", lambda t: t.matmul(
                        ps[5][0:1, 0:1], ones_r[0:1, 0:1], ones_r[0:1, 0:1],
                        start=True, stop=True), inc="mm")
                    k.wait("scalar", "mm")
            k.wait("vector", "mm")
            v_op(k, lambda v: v.tensor_scalar(rmu[:, :], ps[6][0:1, :], 1.0 / DD,
                                              None, op0=ALU.mult))
            v_op(k, lambda v: v.tensor_scalar(rms_[:, :], ps[7][0:1, :], 1.0 / DD,
                                              None, op0=ALU.mult))
            k.wait("vector", "vq")
            v_op(k, lambda v: v.tensor_mul(rm2[:, :], rmu[:, :], rmu[:, :]))
            k.wait("vector", "vq")
            v_op(k, lambda v: v.tensor_sub(rva[:, :], rms_[:, :], rm2[:, :]))
            k.wait("scalar", "vq")
            s_op(k, lambda s: s.activation(rva[:, :], rva[:, :], AF.Ln, bias=c_eps[0:1, :],
                                           scale=1.0), dep=True)
            s_op(k, lambda s: s.activation(rrs[:, :], rva[:, :], AF.Exp, bias=0.0,
                                           scale=c_mhalf[0:1, :]), dep=True)
            k.wait("vector", "sq")
            v_op(k, lambda v: v.tensor_mul(rmr[:, :], rmu[:, :], rrs[:, :]))
            k.wait("tensor", "vq")
            k.op("tensor", lambda t: t.matmul(ps[4][:, :], ones_r[:, :], rrs[:, :],
                                              start=True, stop=True), inc="mm")
            k.op("tensor", lambda t: t.matmul(ps[5][:, :], ones_r[:, :], rmr[:, :],
                                              start=True, stop=True), inc="mm")
            k.wait("vector", "mm")
            for kk in range(nt):
                v_op(k, lambda v, kk=kk: v.tensor_mul(
                    tmp[:, :], xin_sb[:, kk * NSUB:(kk + 1) * NSUB], ps[4][:, :]),
                    dep=True)
                k.wait("vector", "vq")
                v_op(k, lambda v, kk=kk: v.tensor_sub(
                    out_sb[:, kk * NSUB:(kk + 1) * NSUB], tmp[:, :], ps[5][:, :]))

        def emit_rms_prep(k, nc, ps, xin_sb, xsq, rms_, rrs, ones_c, ones_r, DD):
            nt = 16
            k.wait("tensor", "vq")
            k.wait("tensor", "sq")
            k.wait("scalar", "vq")
            for kk in range(nt):
                s_op(k, lambda s, kk=kk: s.activation(
                    xsq[:, :], xin_sb[:, kk * NSUB:(kk + 1) * NSUB], AF.Square),
                    dep=True)
                k.wait("tensor", "sq")
                k.op("tensor", lambda t, kk=kk: t.matmul(
                    ps[7][0:1, :], ones_c[:, :], xsq[:, :],
                    start=(kk == 0), stop=(kk == nt - 1)),
                    inc="mm" if kk == nt - 1 else None)
                if kk < nt - 1:
                    k.op("tensor", lambda t: t.matmul(
                        ps[5][0:1, 0:1], ones_r[0:1, 0:1], ones_r[0:1, 0:1],
                        start=True, stop=True), inc="mm")
                    k.wait("scalar", "mm")
            k.wait("vector", "mm")
            v_op(k, lambda v: v.tensor_scalar(rms_[:, :], ps[7][0:1, :], 1.0 / DD,
                                              None, op0=ALU.mult))
            k.wait("scalar", "vq")
            s_op(k, lambda s: s.activation(rms_[:, :], rms_[:, :], AF.Ln, bias=c_eps[0:1, :],
                                           scale=1.0), dep=True)
            s_op(k, lambda s: s.activation(rrs[:, :], rms_[:, :], AF.Exp, bias=0.0,
                                           scale=c_mhalf[0:1, :]), dep=True)
            k.wait("tensor", "sq")
            k.op("tensor", lambda t: t.matmul(ps[4][:, :], ones_r[:, :], rrs[:, :],
                                              start=True, stop=True), inc="mm")
            k.wait("vector", "mm")

        def emit_rms_scale(k, nc, ps, xin_sb, out_sb, xsq, rms_, rrs, ones_c,
                           ones_r, lnw, DD):
            emit_rms_prep(k, nc, ps, xin_sb, xsq, rms_, rrs, ones_c, ones_r, DD)
            for kk in range(16):
                v_op(k, lambda v, kk=kk: v.scalar_tensor_tensor(
                    out_sb[:, kk * NSUB:(kk + 1) * NSUB],
                    xin_sb[:, kk * NSUB:(kk + 1) * NSUB],
                    lnw[:, kk:kk + 1], ps[4][:, :], ALU.mult, ALU.mult))

        def emit_router(k, nc, ps, xln, rtr_sb, rbb, ident_sb, ones_c, wfm, ncs,
                        rr, r_sc):
            (r_pm, r_npm, r_ex, r_pr, r_k1, r_pk, r_k2, r_w0, r_w, r_lg,
             _a, _b) = rr
            (r_m1, r_mm2, r_dd, r_e2, r_g1, r_g2) = r_sc
            for ts in range(4):
                k.wait("tensor", "vq")
                k.wait("tensor", "sq")
                for kk in range(16):
                    k.op("tensor", lambda t, ts=ts, kk=kk: t.matmul(
                        ps[2][0:128, 0:EXN],
                        xln[:, kk * NSUB + ts * 128: kk * NSUB + ts * 128 + 128],
                        rtr_sb[:, kk * EXN:(kk + 1) * EXN],
                        start=(kk == 0), stop=(kk == 15)),
                        inc="mm" if kk == 15 else None)
                k.wait("vector", "mm")
                v_op(k, lambda v: v.tensor_add(r_lg[:, :], ps[2][0:128, 0:EXN],
                                               rbb[:, :]))
                v_op(k, lambda v: v.reduce_max(r_pm[:, 0:1], r_lg[:, :], axis=AX.X),
                     dep=True)
                k.wait("scalar", "vq")
                s_op(k, lambda s: s.mul(r_npm[:, 0:1], r_pm[:, 0:1], -1.0))
                k.wait("scalar", "sq")
                s_op(k, lambda s: s.activation(r_ex[:, :], r_lg[:, :], AF.Exp,
                                               bias=r_npm[:, 0:1], scale=1.0))
                k.wait("vector", "sq")
                v_op(k, lambda v: v.reduce_sum(r_m1[:, :], r_ex[:, :], axis=AX.X))
                v_op(k, lambda v: v.reciprocal(r_mm2[:, :], r_m1[:, :]), dep=True)
                v_op(k, lambda v: v.tensor_scalar(r_pr[:, :], r_ex[:, :],
                                                  r_mm2[:, :], None, op0=ALU.mult),
                     dep=True)
                v_op(k, lambda v: v.reduce_max(r_m1[:, :], r_pr[:, :], axis=AX.X),
                     dep=True)
                v_op(k, lambda v: v.tensor_scalar(r_k1[:, :], r_pr[:, :],
                                                  r_m1[:, :], None, op0=ALU.is_ge),
                     dep=True)
                v_op(k, lambda v: v.scalar_tensor_tensor(
                    r_pk[:, :], r_k1[:, :], -1e30, r_pr[:, :], ALU.mult, ALU.add),
                    dep=True)
                v_op(k, lambda v: v.reduce_max(r_mm2[:, :], r_pk[:, :], axis=AX.X),
                     dep=True)
                v_op(k, lambda v: v.tensor_scalar(r_k2[:, :], r_pk[:, :],
                                                  r_mm2[:, :], None, op0=ALU.is_ge),
                     dep=True)
                v_op(k, lambda v: v.tensor_sub(r_dd[:, :], r_mm2[:, :], r_m1[:, :]),
                     dep=True)
                k.wait("scalar", "vq")
                s_op(k, lambda s: s.activation(r_e2[:, :], r_dd[:, :], AF.Exp))
                k.wait("vector", "sq")
                v_op(k, lambda v: v.tensor_scalar(r_g2[:, :], r_e2[:, :], 1.0,
                                                  None, op0=ALU.add))
                v_op(k, lambda v: v.reciprocal(r_g1[:, :], r_g2[:, :]), dep=True)
                v_op(k, lambda v: v.scalar_tensor_tensor(
                    r_g2[:, :], r_g1[:, :], -1.0, ones_c[:, :], ALU.mult, ALU.add),
                    dep=True)
                v_op(k, lambda v: v.tensor_scalar(r_w0[:, :], r_k1[:, :],
                                                  r_g1[:, :], None, op0=ALU.mult))
                k.wait("vector", "vq")
                v_op(k, lambda v: v.scalar_tensor_tensor(
                    r_w[:, :], r_k2[:, :], r_g2[:, :], r_w0[:, :],
                    ALU.mult, ALU.add))
                k.wait("tensor", "vq")
                k.op("tensor", lambda t: t.transpose(
                    ps[3][0:8, 0:128], r_w[:, :], ident_sb[:, :]), inc="mm")
                k.wait("vector", "mm")
                v_op(k, lambda v, ncs=ncs, ts=ts: v.tensor_copy(
                    wfm[:, ncs * NSUB + ts * 128: ncs * NSUB + ts * 128 + 128],
                    ps[3][0:8, 0:128]))

        def mixer_phase():
            pes = ExitStack()
            with pes:
                ipw_sb = sb("mx_ipw", 128, 16 * 1408, BF16, pes)
                dtw_sb = sb("mx_dtw", 128, DC, BF16, pes)
                rhs_sb = [sb(f"mx_rhs{i}", 128, 16 * CH, BF16, pes) for i in range(2)]
                siluz = sb("mx_siluz", 128, 4 * CH, F32, pes)
                xsrcA = sb("mx_xsA", 64, CH, F32, pes)
                xsrcB = sb("mx_xsB", 64, CH, F32, pes)
                BsbA = sb("mx_BA", 64, CH, BF16, pes)
                BsbB = sb("mx_BB", 64, CH, BF16, pes)
                Csb = [sb(f"mx_C{j}", 128, CH, BF16, pes) for j in range(4)]
                dtp = sb("mx_dtp", 128, CH, BF16, pes)
                dtf = sb("mx_dtf", 128, 4 * CH, F32, pes)
                relu_t = sb("mx_rel", 128, CH, F32, pes)
                ab_t = sb("mx_ab", 128, CH, F32, pes)
                xs_sb = sb("mx_xs", 128, 4 * CH, F32, pes)
                dtx = sb("mx_dtx", 128, 4 * CH, F32, pes)
                xrep = [sb(f"mx_xr{j}", 128, CH + 4, F32, pes) for j in range(4)]
                xc_t = sb("mx_xc", 128, CH, F32, pes)
                dA = [sb(f"mx_dA{i}", 128, CH, F32, pes) for i in range(2)]
                dbx = [sb(f"mx_db{i}", 128, CH, F32, pes) for i in range(2)]
                hbuf = [sb(f"mx_h{i}", 128, CH, F32, pes) for i in range(2)]
                prod = [sb(f"mx_p{i}", 128, CH, F32, pes) for i in range(2)]
                yac = sb("mx_yac", 128, CH, F32, pes)
                states = sb("mx_st", 128, 4 * NST, F32, pes)
                ygb = [sb(f"mx_yg{i}", 128, CH, BF16, pes) for i in range(2)]

                k.op("gpsimd", lambda e: e.dma_start(
                    ipw_sb[:, :].rearrange("p (kb m) -> p kb m", m=1408),
                    ipwT.rearrange("(kb p) m -> p kb m", p=128)), inc="gld", n=16)
                k.op("gpsimd", lambda e: e.dma_start(
                    dtw_sb[:, :], dtwT[:, :]), inc="gld", n=16)
                k.wait("gpsimd", "st")
                k.op("gpsimd", lambda e: e.collective_compute(
                    "AllGather", ALU.bypass, replica_groups=[list(range(NC))],
                    ins=[g_min_in.opt()], outs=[g_min_out.opt()]), inc="cc")
                k.wait("sync", "cc")
                k.wait("tensor", "gld")

                NCHUNK = (B * T) // CH
                rhs_slot_mm = {0: 0, 1: 0}
                for ch in range(NCHUNK):
                    bb = ch // (T // CH)
                    t0 = (ch % (T // CH)) * CH
                    rank = bb * 4 + t0 // TOKC
                    col0 = t0 % TOKC
                    slot = ch % 2
                    k.wait("sync", "mm", rhs_slot_mm[slot])
                    rv = k.op("sync", lambda e, slot=slot, rank=rank, col0=col0:
                              e.dma_start(
                                  rhs_sb[slot][:, :].rearrange(
                                      "p (kb m) -> p kb m", m=CH),
                                  g_min_out[rank * D:(rank + 1) * D, col0:col0 + CH]
                                  .rearrange("(kb p) m -> p kb m", p=128)),
                              inc="ld", n=16)
                    k.wait("tensor", "ld", rv)
                    k.wait("scalar", "vq")
                    k.wait("scalar", "mm")
                    for m in range(11):
                        pb = ps[m % 3]
                        k.wait("tensor", "ev")
                        for kk in range(16):
                            k.op("tensor", lambda t, m=m, kk=kk, pb=pb, slot=slot:
                                 t.matmul(pb[:, :],
                                          ipw_sb[:, kk * 1408 + m * 128:
                                                 kk * 1408 + m * 128 + 128],
                                          rhs_sb[slot][:, kk * CH:(kk + 1) * CH],
                                          start=(kk == 0), stop=(kk == 15)),
                                 inc="mm" if kk == 15 else None)
                        k.wait("scalar", "mm")
                        if m < 4:
                            k.op("scalar", lambda s, m=m, pb=pb: s.activation(
                                siluz[:, m * CH:(m + 1) * CH], pb[:, :], AF.Silu),
                                inc="ev")
                        elif m == 4:
                            k.op("scalar", lambda s, pb=pb: s.copy(
                                xsrcA[:, :], pb[0:64, :]), inc="ev")
                            k.op("scalar", lambda s, pb=pb: s.copy(
                                xsrcB[:, :], pb[64:128, :]), inc="ev")
                        elif m == 5:
                            k.op("scalar", lambda s, pb=pb: s.copy(
                                BsbA[:, :], pb[0:64, :]), inc="ev")
                            k.op("scalar", lambda s, pb=pb: s.copy(
                                BsbB[:, :], pb[64:128, :]), inc="ev")
                        elif m < 10:
                            k.op("scalar", lambda s, m=m, pb=pb: s.copy(
                                Csb[m - 6][:, :], pb[:, :]), inc="ev")
                        else:
                            k.op("scalar", lambda s, pb=pb: s.copy(
                                dtp[:, :], pb[:, :]), inc="ev")
                    rhs_slot_mm[slot] = k.val("mm")
                    k.wait("tensor", "ev")
                    for j in range(4):
                        pb = ps[4 + j % 2]
                        if j >= 2:
                            k.wait("tensor", "sq")
                        k.op("tensor", lambda t, j=j, pb=pb: t.matmul(
                            pb[:, :], dtw_sb[:, j * 128:(j + 1) * 128], dtp[:, :],
                            start=True, stop=True), inc="mm")
                        k.wait("scalar", "mm")
                        k.wait("scalar", "vq")
                        s_op(k, lambda s, j=j, pb=pb: s.activation(
                            relu_t[:, :], pb[:, :], AF.Relu,
                            bias=c_dtb[:, j:j + 1], scale=1.0), dep=True)
                        s_op(k, lambda s, j=j, pb=pb: s.activation(
                            ab_t[:, :], pb[:, :], AF.Abs,
                            bias=c_dtb[:, j:j + 1], scale=1.0))
                        s_op(k, lambda s: s.activation(
                            ab_t[:, :], ab_t[:, :], AF.Exp, bias=0.0, scale=-1.0),
                            dep=True)
                        s_op(k, lambda s: s.activation(
                            ab_t[:, :], ab_t[:, :], AF.Ln, bias=1.0, scale=1.0),
                            dep=True)
                        k.wait("vector", "sq")
                        v_op(k, lambda v, j=j: v.tensor_add(
                            dtf[:, j * CH:(j + 1) * CH], relu_t[:, :], ab_t[:, :]),
                            dep=True)
                    for j in range(4):
                        pb = ps[6]
                        k.wait("tensor", "vq")
                        src_t = xsrcA if j < 2 else xsrcB
                        k.op("tensor", lambda t, j=j, pb=pb, src_t=src_t: t.matmul(
                            pb[:, :],
                            c_selx[32 * (j % 2):32 * (j % 2) + 32, :],
                            src_t[32 * (j % 2):32 * (j % 2) + 32, :],
                            start=True, stop=True), inc="mm")
                        if t0 == 0:
                            v_op(k, lambda v, j=j: v.memset(xrep[j][:, 0:4], 0.0),
                                 dep=True)
                        else:
                            v_op(k, lambda v, j=j: v.tensor_copy(
                                xrep[j][:, 0:4], xrep[j][:, CH:CH + 4]), dep=True)
                        k.wait("vector", "mm")
                        k.wait("vector", "vq")
                        v_op(k, lambda v, j=j, pb=pb: v.tensor_copy(
                            xrep[j][:, 4:CH + 4], pb[:, :]))
                        k.wait("vector", "vq")
                        v_op(k, lambda v, j=j: v.tensor_scalar(
                            xc_t[:, :], xrep[j][:, 1:CH + 1],
                            c_convw[:, j * 4:j * 4 + 1], None, op0=ALU.mult))
                        for tau in range(1, 4):
                            k.wait("vector", "vq")
                            v_op(k, lambda v, j=j, tau=tau: v.scalar_tensor_tensor(
                                xc_t[:, :], xrep[j][:, 1 + tau:CH + 1 + tau],
                                c_convw[:, j * 4 + tau:j * 4 + tau + 1], xc_t[:, :],
                                ALU.mult, ALU.add))
                        k.wait("scalar", "vq")
                        s_op(k, lambda s, j=j: s.activation(
                            xs_sb[:, j * CH:(j + 1) * CH], xc_t[:, :], AF.Silu,
                            bias=c_convb[:, j:j + 1], scale=1.0), dep=True)
                        k.wait("vector", "sq")
                        k.wait("vector", "vq")
                        v_op(k, lambda v, j=j: v.tensor_mul(
                            dtx[:, j * CH:(j + 1) * CH],
                            dtf[:, j * CH:(j + 1) * CH],
                            xs_sb[:, j * CH:(j + 1) * CH]))
                    if t0 == 0:
                        k.wait("vector", "vq")
                        v_op(k, lambda v: v.memset(states[:, :], 0.0))
                    k.wait("tensor", "sq")
                    for j in range(4):
                        Bs = BsbA if j < 2 else BsbB
                        for n in range(NST):
                            sl = n % 2
                            pb = ps[4 + sl]
                            pc = ps[6 + sl]
                            k.wait("scalar", "vq")
                            s_op(k, lambda s, j=j, n=n, sl=sl: s.activation(
                                dA[sl][:, :], dtf[:, j * CH:(j + 1) * CH], AF.Exp,
                                bias=0.0,
                                scale=c_aneg[:, j * NST + n:j * NST + n + 1]),
                                dep=True)
                            k.wait("tensor", "vq")
                            k.op("tensor", lambda t, j=j, n=n, pb=pb, Bs=Bs: t.matmul(
                                pb[:, :],
                                c_selB[32 * (j % 2):32 * (j % 2) + 32,
                                       n * 128:(n + 1) * 128],
                                Bs[32 * (j % 2):32 * (j % 2) + 32, :],
                                start=True, stop=True), inc="mm")
                            k.wait("vector", "mm")
                            v_op(k, lambda v, j=j, sl=sl, pb=pb: v.tensor_mul(
                                dbx[sl][:, :], dtx[:, j * CH:(j + 1) * CH], pb[:, :]),
                                dep=True)
                            k.wait("vector", "sq")
                            k.wait("vector", "vq")
                            v_op(k, lambda v, j=j, n=n, sl=sl: v.tensor_tensor_scan(
                                hbuf[sl][:, :], dA[sl][:, :], dbx[sl][:, :],
                                states[:, j * NST + n:j * NST + n + 1],
                                ALU.mult, ALU.add))
                            k.wait("vector", "vq")
                            v_op(k, lambda v, j=j, n=n, sl=sl: v.tensor_copy(
                                states[:, j * NST + n:j * NST + n + 1],
                                hbuf[sl][:, CH - 1:CH]))
                            k.op("tensor", lambda t, j=j, n=n, pc=pc: t.matmul(
                                pc[:, :], c_selC[:, n * 128:(n + 1) * 128],
                                Csb[j][:, :], start=True, stop=True), inc="mm")
                            k.wait("vector", "mm")
                            v_op(k, lambda v, sl=sl, pc=pc: v.tensor_mul(
                                prod[sl][:, :], hbuf[sl][:, :], pc[:, :]))
                            k.wait("vector", "vq")
                            if n == 0:
                                v_op(k, lambda v, sl=sl: v.tensor_copy(
                                    yac[:, :], prod[sl][:, :]))
                            else:
                                v_op(k, lambda v, sl=sl: v.tensor_add(
                                    yac[:, :], yac[:, :], prod[sl][:, :]))
                        k.wait("vector", "ev")
                        k.wait("vector", "vq")
                        v_op(k, lambda v, j=j: v.scalar_tensor_tensor(
                            yac[:, :], xs_sb[:, j * CH:(j + 1) * CH],
                            c_d128[:, j:j + 1], yac[:, :], ALU.mult, ALU.add))
                        k.wait("vector", "vq")
                        k.wait("vector", "st")
                        v_op(k, lambda v, j=j, ch=ch: v.tensor_mul(
                            ygb[ch % 2][:, :], yac[:, :],
                            siluz[:, j * CH:(j + 1) * CH]))
                        yv = k.val("vq")
                        k.wait("sync", "vq", yv)
                        k.op("sync", lambda e, j=j, bb=bb, t0=t0, ch=ch: e.dma_start(
                            g_yg_in[bb * DC + j * 128: bb * DC + (j + 1) * 128,
                                    t0:t0 + CH], ygb[ch % 2][:, :]), inc="st", n=16)
                    if ch % 8 == 7:
                        k.flush()
                k.flush()

        def outproj_phase():
            pes = ExitStack()
            with pes:
                yrhs = sb("op_yrhs", 128, 32 * NSUB, BF16, pes)
                wop = [sb(f"op_w{i}", 128, 32 * 128, BF16, pes) for i in range(2)]
                h1 = sb("op_h1", 128, 16 * NSUB, F32, pes)
                m2 = sb("op_m2", 128, 16 * NSUB, BF16, pes)
                s12t = sb("op_s12t", 128, NSUB, F32, pes)
                xsq = sb("op_xsq", 128, NSUB, F32, pes)
                rms_ = sb("op_rms", 1, NSUB, F32, pes)
                rrs = sb("op_rrs", 1, NSUB, F32, pes)
                u_t = sb("op_ut", 128, NSUB, F32, pes)
                x1_t = sb("op_x1t", 128, NSUB, F32, pes)

                k.wait("gpsimd", "st")
                k.op("gpsimd", lambda e: e.collective_compute(
                    "AllGather", ALU.bypass, replica_groups=[list(range(NC))],
                    ins=[g_yg_in.opt()], outs=[g_yg_out.opt()]), inc="cc")
                k.wait("sync", "cc")
                k.S("ld")[1] += 16

                def sw_copy(e):
                    for arm in nc.switch_core_id(engines=[nc.sync], n=128):
                        c = arm.logical if hasattr(arm, "logical") else int(arm)
                        c = c % NC
                        bbq, ttq = c // 4, (c % 4) * TOKC
                        srcap = bass.AP(
                            g_yg_out.tensor, (bbq * DC) * T + ttq,
                            [[B * DC * T, NC], [T, DC], [1, TOKC]])
                        dstap = s_ymy.rearrange("(r d) t -> r d t", d=DC)
                        e.dma_start(dstap, srcap).then_inc(k.S("ld")[0], 16)
                k.raw("sync", sw_copy)
                swv = k.val("ld")
                k.wait("sync", "ld", swv)

                op_slot_mm = {0: 0, 1: 0}
                for ncs in range(TOKC // NSUB):
                    tok0 = ncs * NSUB
                    rv = None
                    k.wait("sync", "mm")
                    for kb_ in range(32):
                        rv = k.op("sync", lambda e, kb_=kb_, tok0=tok0: e.dma_start(
                            yrhs[:, kb_ * NSUB:(kb_ + 1) * NSUB],
                            s_ymy[kb_ * 128:(kb_ + 1) * 128, tok0:tok0 + NSUB]),
                            inc="ld", n=16)
                    k.wait("tensor", "ld", rv)
                    for mo in range(D // 128):
                        slot = mo % 2
                        k.wait("gpsimd", "mm", op_slot_mm[slot])
                        wv = k.op("gpsimd", lambda e, mo=mo, slot=slot: e.dma_start(
                            wop[slot][:, :].rearrange("p (kb m) -> p kb m", m=128),
                            p_opw[:, mo * 128:(mo + 1) * 128]
                            .rearrange("(kb p) m -> p kb m", p=128)),
                            inc="gld", n=16)
                        k.wait("tensor", "gld", wv)
                        pb = ps[mo % 2]
                        if mo >= 2:
                            k.wait("tensor", "vq")
                        for kb_ in range(32):
                            k.op("tensor", lambda t, kb_=kb_, slot=slot, pb=pb:
                                 t.matmul(pb[:, :],
                                          wop[slot][:, kb_ * 128:(kb_ + 1) * 128],
                                          yrhs[:, kb_ * NSUB:(kb_ + 1) * NSUB],
                                          start=(kb_ == 0), stop=(kb_ == 31)),
                                 inc="mm" if kb_ == 31 else None)
                        op_slot_mm[slot] = k.val("mm")
                        k.wait("sync", "vq")
                        uv = k.op("sync", lambda e, mo=mo, tok0=tok0: e.dma_start(
                            u_t[:, :], s_u[mo * 128:(mo + 1) * 128, tok0:tok0 + NSUB]),
                            inc="ld", n=16)
                        xv = k.op("sync", lambda e, mo=mo, tok0=tok0: e.dma_start(
                            x1_t[:, :], s_x1[mo * 128:(mo + 1) * 128,
                                             tok0:tok0 + NSUB]), inc="ld", n=16)
                        k.wait("vector", "mm")
                        k.wait("vector", "ld", xv)
                        v_op(k, lambda v, mo=mo, pb=pb: v.tensor_add(
                            h1[:, mo * NSUB:(mo + 1) * NSUB], pb[:, :], u_t[:, :]),
                            dep=True)
                        k.wait("vector", "vq")
                        k.wait("vector", "st")
                        v_op(k, lambda v, mo=mo: v.tensor_add(
                            s12t[:, :], h1[:, mo * NSUB:(mo + 1) * NSUB], x1_t[:, :]))
                        sv = k.val("vq")
                        k.wait("sync", "vq", sv)
                        k.op("sync", lambda e, mo=mo, tok0=tok0: e.dma_start(
                            s_s12[mo * 128:(mo + 1) * 128, tok0:tok0 + NSUB],
                            s12t[:, :]), inc="st", n=16)
                    emit_rms_scale(k, nc, ps, h1, m2, xsq, rms_, rrs, c_ones_col,
                                   c_ones_row, c_ln2, D)
                    m2v = k.val("vq")
                    k.wait("sync", "vq", m2v)
                    for kk in range(16):
                        k.op("sync", lambda e, kk=kk, tok0=tok0: e.dma_start(
                            s_m2[kk * 128:(kk + 1) * 128, tok0:tok0 + NSUB],
                            m2[:, kk * NSUB:(kk + 1) * NSUB]), inc="st", n=16)
                    k.wait("vector", "st")
                k.flush()

        def mlp_phase():
            pes = ExitStack()
            with pes:
                m2 = sb("ml_m2", 128, 16 * NSUB, BF16, pes)
                wg = [sb(f"ml_wg{i}", 128, 16 * 128, BF16, pes) for i in range(2)]
                wu = [sb(f"ml_wu{i}", 128, 16 * 128, BF16, pes) for i in range(2)]
                wd = [sb(f"ml_wd{i}", 128, 64 * 128, BF16, pes) for i in range(2)]
                i_sb = sb("ml_i", 128, 64 * NSUB, BF16, pes)
                sg = sb("ml_sg", 128, NSUB, F32, pes)
                s12t = sb("ml_s12t", 128, NSUB, F32, pes)
                x2t = [sb(f"ml_x2t{i}", 128, NSUB, F32, pes) for i in range(2)]

                gu_slot_mm = {0: 0, 1: 0}
                wd_slot_mm = {0: 0, 1: 0}
                for ncs in range(TOKC // NSUB):
                    tok0 = ncs * NSUB
                    mv = None
                    k.wait("sync", "vq")
                    k.wait("sync", "mm")
                    for kk in range(16):
                        mv = k.op("sync", lambda e, kk=kk, tok0=tok0: e.dma_start(
                            m2[:, kk * NSUB:(kk + 1) * NSUB],
                            s_m2[kk * 128:(kk + 1) * 128, tok0:tok0 + NSUB]),
                            inc="ld", n=16)
                    k.wait("tensor", "ld", mv)
                    k.wait("vector", "mm")
                    for mi in range(INTER // 128):
                        slot = mi % 2
                        k.wait("gpsimd", "mm", gu_slot_mm[slot])
                        gv = k.op("gpsimd", lambda e, mi=mi, slot=slot: e.dma_start(
                            wg[slot][:, :].rearrange("p (kb m) -> p kb m", m=128),
                            p_gate[:, mi * 128:(mi + 1) * 128]
                            .rearrange("(kb p) m -> p kb m", p=128)),
                            inc="gld", n=16)
                        uv2 = k.op("gpsimd", lambda e, mi=mi, slot=slot: e.dma_start(
                            wu[slot][:, :].rearrange("p (kb m) -> p kb m", m=128),
                            p_up[:, mi * 128:(mi + 1) * 128]
                            .rearrange("(kb p) m -> p kb m", p=128)),
                            inc="gld", n=16)
                        k.wait("tensor", "gld", uv2)
                        pg = ps[mi % 2]
                        pu = ps[2 + mi % 2]
                        if mi >= 2:
                            k.wait("tensor", "vq")
                        for kk in range(16):
                            k.op("tensor", lambda t, kk=kk, slot=slot, pg=pg: t.matmul(
                                pg[:, :], wg[slot][:, kk * 128:(kk + 1) * 128],
                                m2[:, kk * NSUB:(kk + 1) * NSUB],
                                start=(kk == 0), stop=(kk == 15)),
                                inc="mm" if kk == 15 else None)
                        for kk in range(16):
                            k.op("tensor", lambda t, kk=kk, slot=slot, pu=pu: t.matmul(
                                pu[:, :], wu[slot][:, kk * 128:(kk + 1) * 128],
                                m2[:, kk * NSUB:(kk + 1) * NSUB],
                                start=(kk == 0), stop=(kk == 15)),
                                inc="mm" if kk == 15 else None)
                        gu_slot_mm[slot] = k.val("mm")
                        k.wait("scalar", "mm")
                        k.wait("scalar", "vq")
                        s_op(k, lambda s, pg=pg: s.activation(sg[:, :], pg[:, :],
                                                              AF.Silu), dep=True)
                        k.wait("vector", "sq")
                        v_op(k, lambda v, mi=mi, pu=pu: v.tensor_mul(
                            i_sb[:, mi * NSUB:(mi + 1) * NSUB], sg[:, :], pu[:, :]))
                    iv = k.val("vq")
                    k.wait("tensor", "vq", iv)
                    for mo2 in range(D // 128):
                        slot = mo2 % 2
                        k.wait("gpsimd", "mm", wd_slot_mm[slot])
                        dv = k.op("gpsimd", lambda e, mo2=mo2, slot=slot: e.dma_start(
                            wd[slot][:, :].rearrange("p (kb m) -> p kb m", m=128),
                            p_down[:, mo2 * 128:(mo2 + 1) * 128]
                            .rearrange("(kb p) m -> p kb m", p=128)),
                            inc="gld", n=16)
                        k.wait("tensor", "gld", dv)
                        pb = ps[4 + mo2 % 2]
                        if mo2 >= 2:
                            k.wait("tensor", "vq")
                        for mi in range(64):
                            k.op("tensor", lambda t, mi=mi, slot=slot, pb=pb: t.matmul(
                                pb[:, :], wd[slot][:, mi * 128:(mi + 1) * 128],
                                i_sb[:, mi * NSUB:(mi + 1) * NSUB],
                                start=(mi == 0), stop=(mi == 63)),
                                inc="mm" if mi == 63 else None)
                        wd_slot_mm[slot] = k.val("mm")
                        k.wait("sync", "vq")
                        sv12 = k.op("sync", lambda e, mo2=mo2, tok0=tok0: e.dma_start(
                            s12t[:, :], s_s12[mo2 * 128:(mo2 + 1) * 128,
                                              tok0:tok0 + NSUB]), inc="ld", n=16)
                        k.wait("vector", "mm")
                        k.wait("vector", "ld", sv12)
                        k.wait("vector", "st")
                        v_op(k, lambda v, mo2=mo2, pb=pb: v.tensor_add(
                            x2t[mo2 % 2][:, :], s12t[:, :], pb[:, :]), dep=True)
                        xv2 = k.val("vq")
                        k.wait("sync", "vq", xv2)
                        k.op("sync", lambda e, mo2=mo2, tok0=tok0: e.dma_start(
                            s_x2[mo2 * 128:(mo2 + 1) * 128, tok0:tok0 + NSUB],
                            x2t[mo2 % 2][:, :]), inc="st", n=16)
                k.flush()

        phase0()
        if dbg_sb is not None:
            k.barrier(exclude=("pc",))
            k.op("sync", lambda e: e.dma_start(dbg_sb[0:128, 0:8], c_cc[:, :]),
                 inc="st", n=16)
            k.op("sync", lambda e: e.dma_start(dbg_sb[0:8, 8:1032], c_kneg[:, :]),
                 inc="st", n=16)
            k.op("sync", lambda e: e.dma_start(dbg_sb[0:128, 1032:1040], c_rbb[:, :]),
                 inc="st", n=16)
            k.op("sync", lambda e: e.dma_start(dbg_sb[0:128, 1040:1056], c_ucbh[:, :]),
                 inc="st", n=16)
            k.op("sync", lambda e: e.dma_start(dbg_sb[0:128, 1056:1120],
                                               c_aneg[:, :]), inc="st", n=16)
        k.barrier(exclude=("pc",))
        ffn_phase(x0, call=1)
        if stop_after != "p1":
            k.barrier()
            mixer_phase()
            if stop_after != "p3":
                k.barrier()
                outproj_phase()
                k.barrier()
                mlp_phase()
                if stop_after != "p5":
                    k.barrier()
                    ffn_phase(s_x2, call=2)
        for dst, srct in dbg_outs:
            k.barrier()
            rows = srct.shape[0]
            r = 0
            while r < rows:
                rr = min(2048, rows - r)
                k.op("sync", lambda e, d=dst, s=srct, r=r, rr=rr: e.dma_start(
                    d[r:r + rr, :], s[r:r + rr, :]), inc="st", n=16)
                r += rr
        k.wait("sync", "st")
        k.wait("sync", "ld")
        k.wait("gpsimd", "gld")
        k.wait("gpsimd", "pc")
        k.flush()

    return nc


def _host_prep(inputs):
    gi = lambda name: np.asarray(inputs[name])
    x = gi("L0_attn_hidden_states")

    base = {}
    base["ffn_down_w"] = np.ascontiguousarray(gi("L0_ffn_down_w"))
    base["ffn_up_w"] = np.ascontiguousarray(gi("L0_ffn_up_w"))
    base["ffn_router_w"] = np.ascontiguousarray(gi("L0_ffn_router_w"))
    base["ffn_shared_w1"] = np.ascontiguousarray(gi("L0_ffn_shared_w1"))
    base["ffn_shared_w2"] = np.ascontiguousarray(gi("L0_ffn_shared_w2"))
    base["ffn_core_w"] = np.ascontiguousarray(gi("L0_ffn_core_w"))
    for i in range(EXN):
        base[f"e{i}_w1"] = np.ascontiguousarray(gi(f"L0_ffn_e{i}_w1"))
        base[f"e{i}_w2"] = np.ascontiguousarray(gi(f"L0_ffn_e{i}_w2"))

    def col128(vec):
        return np.ascontiguousarray(np.asarray(vec).reshape(-1, 128).T)

    base["down_b128"] = col128(gi("L0_ffn_down_b"))
    for i in range(EXN):
        base[f"b1c{i}"] = col128(gi(f"L0_ffn_e{i}_b1"))
    b2all = np.zeros((128, (LF // 128) * EXN), np.float32)
    for e in range(EXN):
        c = col128(gi(f"L0_ffn_e{e}_b2"))
        for m in range(LF // 128):
            b2all[:, m * EXN + e] = c[:, m]
    base["b2all"] = b2all
    base["sb1_128"] = col128(gi("L0_ffn_shared_b1"))
    base["sb2_128"] = col128(gi("L0_ffn_shared_b2"))
    base["upb_128"] = col128(gi("L0_ffn_up_b"))
    base["coreb_128"] = col128(gi("L0_ffn_core_b"))
    base["rb_row"] = np.ascontiguousarray(
        gi("L0_ffn_router_b").reshape(1, EXN).astype(np.float32))
    cost = np.array([2 * HID[i] * LF for i in range(EXN)], np.float32)
    base["cost_row"] = (COST_LAMBDA * cost).reshape(1, EXN).astype(np.float32)

    base["gatewT"] = np.ascontiguousarray(gi("L0_attn_gate_w").T)
    base["upwT"] = np.ascontiguousarray(gi("L0_attn_up_w").T)
    base["downwT"] = np.ascontiguousarray(gi("L0_attn_down_w").T)
    base["opwT"] = np.ascontiguousarray(gi("L0_attn_out_proj_w").T)
    base["ln1_128"] = col128(gi("L0_attn_ln1_w"))
    base["ln2_128"] = col128(gi("L0_attn_ln2_w"))

    base["ones_col"] = np.ones((128, 1), np.float32)
    base["ones_row"] = np.ones((1, 128), np.float32)
    base["ident"] = np.eye(128, dtype=np.float32)
    oh8 = np.zeros((EXN, EXN * 128), np.float32)
    for e in range(EXN):
        oh8[e, e * 128:(e + 1) * 128] = 1.0
    base["oh8"] = oh8
    selx = np.zeros((32, 128), np.float32)
    for p in range(128):
        selx[(p // 64) * 16 + (p % 16), p] = 1.0
    base["selx"] = np.concatenate([selx, selx], axis=0)
    selB = np.zeros((32, NST * 128), np.float32)
    selC = np.zeros((128, NST * 128), np.float32)
    for n in range(NST):
        for p in range(128):
            selB[(p // 64) * 16 + n, n * 128 + p] = 1.0
            selC[(p // 16) * 16 + n, n * 128 + p] = 1.0
    base["selB_all"] = np.concatenate([selB, selB], axis=0)
    base["selC_all"] = selC

    ipw = gi("L0_attn_in_proj_w")
    dtw = gi("L0_attn_dt_proj_w")
    alog = gi("L0_attn_A_log")
    dpar = gi("L0_attn_D_param")
    cw = gi("L0_attn_conv_w")
    cb = gi("L0_attn_conv_b")
    dtb = gi("L0_attn_dt_proj_b")

    in_maps = []
    for c in range(NC):
        m = dict(base)
        bb, tt = c // 4, (c % 4) * TOKC
        m["x0"] = np.ascontiguousarray(x[bb, tt:tt + TOKC, :].T)
        d0 = c * DC
        rows = np.concatenate([
            np.arange(d0, d0 + DC),
            4096 + np.arange(c * 128, (c + 1) * 128),
            5120 + np.arange(c * 128, (c + 1) * 128),
            6144 + np.arange(d0, d0 + DC),
            np.arange(10240, 10368),
        ])
        m["ipwT"] = np.ascontiguousarray(ipw[rows, :].T)
        m["dtwT"] = np.ascontiguousarray(dtw[d0:d0 + DC, :].T)

        def loc128(vec2):
            v = vec2.reshape(DC // 128, 128, -1)
            return np.ascontiguousarray(v.transpose(1, 0, 2).reshape(128, -1))
        m["alog128"] = loc128(alog[d0:d0 + DC, :])
        m["d128"] = loc128(dpar[d0:d0 + DC].reshape(-1, 1))
        m["convw128"] = loc128(cw[d0:d0 + DC, 0, :])
        m["convb128"] = loc128(cb[d0:d0 + DC].reshape(-1, 1))
        m["dtb128"] = loc128(dtb[d0:d0 + DC].reshape(-1, 1))
        in_maps.append(m)
    return in_maps


def kernel(**inputs):
    in_maps = _host_prep(inputs)
    nc = build_program()
    res = run_bass_kernel_spmd(nc, in_maps, core_ids=list(range(NC)))
    out = np.zeros((B, T, D), np.float32)
    for c in range(NC):
        bb, tt = c // 4, (c % 4) * TOKC
        out[bb, tt:tt + TOKC, :] = np.asarray(res.results[c]["xout"]).T
    return out


# version 02
# speedup vs baseline: 1.0412x; 1.0412x over previous
import os
os.environ.setdefault("NEURON_SCRATCHPAD_PAGE_SIZE", "2048")

import numpy as np
import ml_dtypes
from contextlib import ExitStack

import concourse.bass as bass
import concourse.mybir as mybir
from concourse.bass_utils import run_bass_kernel_spmd

F32 = mybir.dt.float32
BF16 = mybir.dt.bfloat16
AF = mybir.ActivationFunctionType
ALU = mybir.AluOpType
AX = mybir.AxisListType

D = 2048
DI = 4096
DXB = 1024
NST = 16
DCONV = 4
DTR = 128
INTER = 8192
LF = 1024
EXN = 8
HID = [1024, 2048, 3072, 4096, 1024, 2048, 3072, 4096]
COST_LAMBDA = 1e-7
EPS = 1e-5

B = 2
T = 16384
NC = 8
TOKC = 4096
DC = 512
NSUB = 512
SUP = 1024
CH = 512

ENGS = ("sync", "gpsimd", "tensor", "scalar", "vector")
SEMS = ("ld", "gld", "st", "mm", "ev", "sq", "vq", "gq", "cc", "pc")


class KB:
    def __init__(self, nc, block, es):
        self.nc, self.block, self.es = nc, block, es
        self.q = {e: [] for e in ENGS}
        self.sems = {}

    def S(self, name):
        if name not in self.sems:
            h = self.es.enter_context(self.nc.semaphore(name))
            self.sems[name] = [h, 0]
        return self.sems[name]

    def val(self, name):
        return self.S(name)[1]

    def op(self, eng, fn, inc=None, n=1):
        if inc is not None:
            s = self.S(inc)
            s[1] += n

            def run(h, fn=fn, sh=s[0], n=n):
                fn(h).then_inc(sh, n)
        else:
            def run(h, fn=fn):
                fn(h)
        self.q[eng].append(run)
        return self.val(inc) if inc else None

    def raw(self, eng, fn):
        self.q[eng].append(fn)

    def wait(self, eng, name, val=None):
        s = self.S(name)
        v = s[1] if val is None else val
        self.q[eng].append(lambda h, sh=s[0], v=v: h.wait_ge(sh, v))

    def barrier(self, exclude=()):
        for e in ENGS:
            for snm in SEMS:
                if snm in self.sems and snm not in exclude:
                    self.wait(e, snm)

    def flush(self):
        for e in ENGS:
            fns = self.q[e]
            if not fns:
                continue
            self.q[e] = []

            def runner(h, fns=fns):
                for f in fns:
                    f(h)
            getattr(self.block, e)(runner)


def scope_guard(k, engs=("gpsimd", "sync")):
    for e in engs:
        for snm in ("mm", "vq", "sq", "ev", "st"):
            if snm in k.sems:
                k.wait(e, snm)


def v_op(k, fn, dep=False):
    if dep:
        k.wait("vector", "vq")
    k.op("vector", fn, inc="vq")


def s_op(k, fn, dep=False):
    if dep:
        k.wait("scalar", "sq")
    k.op("scalar", fn, inc="sq")


def build_program(stop_after=None, debug=()):
    nc = bass.Bass("TRN2", target_bir_lowering=False)

    def inp(name, shape):
        return nc.dram_tensor(name, list(shape), F32, kind="ExternalInput").ap()

    x0 = inp("x0", (D, TOKC))
    ffn_down_w = inp("ffn_down_w", (D, LF))
    ffn_up_w = inp("ffn_up_w", (LF, D))
    ffn_router_w = inp("ffn_router_w", (D, EXN))
    ffn_shared_w1 = inp("ffn_shared_w1", (LF, LF))
    ffn_shared_w2 = inp("ffn_shared_w2", (LF, LF))
    ffn_core_w = inp("ffn_core_w", (D, D))
    e_w1 = [inp(f"e{i}_w1", (LF, HID[i])) for i in range(EXN)]
    e_w2 = [inp(f"e{i}_w2", (HID[i], LF)) for i in range(EXN)]
    down_b128 = inp("down_b128", (128, LF // 128))
    b1c = [inp(f"b1c{i}", (128, HID[i] // 128)) for i in range(EXN)]
    b2all = inp("b2all", (128, (LF // 128) * EXN))
    sb1_128 = inp("sb1_128", (128, LF // 128))
    sb2_128 = inp("sb2_128", (128, LF // 128))
    upb_128 = inp("upb_128", (128, D // 128))
    coreb_128 = inp("coreb_128", (128, D // 128))
    rb_row = inp("rb_row", (1, EXN))
    cost_row = inp("cost_row", (1, EXN))
    ipwT = inp("ipwT", (D, 1408))
    dtwT = inp("dtwT", (DTR, DC))
    opwT = inp("opwT", (DI, D))
    gatewT = inp("gatewT", (D, INTER))
    upwT = inp("upwT", (D, INTER))
    downwT = inp("downwT", (INTER, D))
    alog128 = inp("alog128", (128, (DC // 128) * NST))
    d128 = inp("d128", (128, DC // 128))
    convw128 = inp("convw128", (128, (DC // 128) * DCONV))
    convb128 = inp("convb128", (128, DC // 128))
    dtb128 = inp("dtb128", (128, DC // 128))
    ln1_128 = inp("ln1_128", (128, D // 128))
    ln2_128 = inp("ln2_128", (128, D // 128))
    ones_col = inp("ones_col", (128, 1))
    ones_row = inp("ones_row", (1, 128))
    ident = inp("ident", (128, 128))
    oh8 = inp("oh8", (EXN, EXN * 128))
    selx = inp("selx", (64, 128))
    selB_all = inp("selB_all", (64, NST * 128))
    selC_all = inp("selC_all", (128, NST * 128))

    def scratch(name, shape, dt=F32):
        return nc.dram_tensor(name, list(shape), dt, kind="Internal").ap()

    g_min_in = scratch("g_min_in", (D, TOKC), BF16)
    g_min_out = nc.dram_tensor("g_min_out", [NC * D, TOKC], BF16, kind="Internal",
                               addr_space="Shared").ap()
    g_yg_in = scratch("g_yg_in", (B * DC, T), BF16)
    g_yg_out = nc.dram_tensor("g_yg_out", [NC * B * DC, T], BF16, kind="Internal",
                              addr_space="Shared").ap()
    s_ymy = scratch("s_ymy", (DI, TOKC), BF16)
    s_x1 = scratch("s_x1", (D, TOKC))
    s_u = scratch("s_u", (D, TOKC))
    s_xlnbf = scratch("s_xlnbf", (D, TOKC), BF16)
    s_cx = scratch("s_cx", (D, TOKC), BF16)
    s_x2 = scratch("s_x2", (D, TOKC))
    s_m2 = scratch("s_m2", (D, TOKC), BF16)
    s_dbgx = scratch("s_dbgx", (D, NSUB))
    s_s12 = scratch("s_s12", (D, TOKC))
    p_ew1 = [scratch(f"p_e{i}w1", (LF, HID[i]), BF16) for i in range(EXN)]
    p_ew2 = [scratch(f"p_e{i}w2", (HID[i], LF), BF16) for i in range(EXN)]
    p_fup = scratch("p_fup", (LF, D), BF16)
    p_fcore = scratch("p_fcore", (D, D), BF16)
    p_opw = scratch("p_opw", (DI, D), BF16)
    p_gate = scratch("p_gate", (D, INTER), BF16)
    p_up = scratch("p_up", (D, INTER), BF16)
    p_down = scratch("p_down", (INTER, D), BF16)
    xout = nc.dram_tensor("xout", [D, TOKC], F32, kind="ExternalOutput").ap()
    dbg_sb = (nc.dram_tensor("dbg_sb", [128, 2100], F32, kind="ExternalOutput").ap()
              if "p0" in debug else None)

    dbg_outs = []
    srcs = {"x1": s_x1, "u": s_u, "minin": g_min_in, "minout": g_min_out,
            "yg": g_yg_in, "ygout": g_yg_out, "ymy": s_ymy, "x2": s_x2,
            "xlnbf": s_xlnbf, "cx": s_cx, "dbgx": s_dbgx, "m2": s_m2, "s12": s_s12}
    for nm in debug:
        if nm == "p0":
            continue
        srct = srcs[nm]
        dbg_outs.append((nc.dram_tensor("dbg_" + nm, list(srct.shape),
                                        srct.dtype, kind="ExternalOutput").ap(), srct))

    _dbgx_done = []
    es = ExitStack()
    with es:
        block = es.enter_context(nc.Block())
        k = KB(nc, block, es)
        ps = [es.enter_context(nc.psum_tensor(f"ps{i}", [128, 512], F32))
              for i in range(8)]

        def sb(name, p, f, dt=F32, st=es):
            return st.enter_context(nc.sbuf_tensor(name, [p, f], dt))

        c_ones_col = sb("c_ones_col", 128, 1)
        c_ones_row = sb("c_ones_row", 1, 128)
        c_ident = sb("c_ident", 128, 128)
        c_oh8 = sb("c_oh8", EXN, EXN * 128)
        c_selx = sb("c_selx", 64, 128)
        c_selB = sb("c_selB", 64, NST * 128, BF16)
        c_selC = sb("c_selC", 128, NST * 128, BF16)
        c_downb = sb("c_downb", 128, LF // 128)
        c_b1 = [sb(f"c_b1_{i}", 128, HID[i] // 128) for i in range(EXN)]
        c_gb1 = [sb(f"c_gb1_{i}", 128, HID[i] // 128, BF16) for i in range(EXN)]
        c_sb1 = sb("c_sb1", 128, LF // 128)
        c_rbb = sb("c_rbb", 128, EXN)
        c_cc = sb("c_cc", 128, LF // 128)
        c_kneg = sb("c_kneg", EXN, LF)
        c_ucbh = sb("c_ucbh", 128, D // 128)
        c_aneg = sb("c_aneg", 128, (DC // 128) * NST)
        c_d128 = sb("c_d128", 128, DC // 128)
        c_convw = sb("c_convw", 128, (DC // 128) * DCONV)
        c_convb = sb("c_convb", 128, DC // 128)
        c_dtb = sb("c_dtb", 128, DC // 128)
        c_ln1 = sb("c_ln1", 128, D // 128)
        c_ln2 = sb("c_ln2", 128, D // 128)
        c_eps = sb("c_eps", 128, 1)
        c_mhalf = sb("c_mhalf", 128, 1)

        def phase0():
            v_op(k, lambda v: v.memset(c_eps[:, :], EPS))
            v_op(k, lambda v: v.memset(c_mhalf[:, :], -0.5))
            def precast(dst, srcap, rows):
                step = max(1, (4 << 20) // max(1, int(np.prod(srcap.shape[1:]) * 4)))
                r = 0
                while r < rows:
                    rr = min(step, rows - r)
                    k.op("gpsimd", lambda e, d=dst, s=srcap, r=r, rr=rr:
                         e.dma_start(d[r:r + rr, :], s[r:r + rr, :]), inc="pc", n=16)
                    r += rr
            for i in range(EXN):
                precast(p_ew1[i], e_w1[i], LF)
                precast(p_ew2[i], e_w2[i], HID[i])
            precast(p_fup, ffn_up_w, LF)
            precast(p_fcore, ffn_core_w, D)
            precast(p_opw, opwT, DI)
            precast(p_gate, gatewT, D)
            precast(p_up, upwT, D)
            precast(p_down, downwT, INTER)

            loads = [
                (c_ones_col, ones_col), (c_ones_row, ones_row), (c_ident, ident),
                (c_selx, selx), (c_downb, down_b128), (c_sb1, sb1_128),
                (c_d128, d128), (c_convw, convw128), (c_convb, convb128),
                (c_dtb, dtb128), (c_ln1, ln1_128), (c_ln2, ln2_128),
            ]
            for dst, srcd in loads:
                k.op("sync", lambda e, d=dst, s=srcd: e.dma_start(d[:, :], s[:, :]),
                     inc="ld", n=16)
            for i in range(EXN):
                k.op("sync", lambda e, d=c_b1[i], s=b1c[i]: e.dma_start(d[:, :], s[:, :]),
                     inc="ld", n=16)
            for dst, srcd in [(c_selB, selB_all), (c_selC, selC_all), (c_oh8, oh8)]:
                if dst.dtype == BF16:
                    k.op("gpsimd", lambda e, d=dst, s=srcd: e.dma_start(d[:, :], s[:, :]),
                         inc="gld", n=16)
                else:
                    k.op("sync", lambda e, d=dst, s=srcd: e.dma_start(d[:, :], s[:, :]),
                         inc="ld", n=16)
            p0es = ExitStack()
            t_rb = sb("t_rb", 1, EXN, F32, p0es)
            t_cost = sb("t_cost", 1, EXN, F32, p0es)
            t_rbeff = sb("t_rbeff", 1, EXN, F32, p0es)
            t_upb = sb("t_upb", 128, D // 128, F32, p0es)
            t_coreb = sb("t_coreb", 128, D // 128, F32, p0es)
            t_ucs = sb("t_ucs", 128, D // 128, F32, p0es)
            t_b2all = sb("t_b2all", 128, (LF // 128) * EXN, F32, p0es)
            t_sb2 = sb("t_sb2", 128, LF // 128, F32, p0es)
            t_sb2t = sb("t_sb2t", 128, LF // 128, F32, p0es)
            t_ksum = sb("t_ksum", 1, LF, F32, p0es)
            t_krow = sb("t_krow", 1, LF, F32, p0es)
            t_alog = sb("t_alog", 128, (DC // 128) * NST, F32, p0es)
            t_aexp = sb("t_aexp", 128, (DC // 128) * NST, F32, p0es)
            for dst, srcd in [(t_rb, rb_row), (t_cost, cost_row), (t_upb, upb_128),
                              (t_coreb, coreb_128), (t_b2all, b2all),
                              (t_sb2, sb2_128), (t_alog, alog128)]:
                k.op("sync", lambda e, d=dst, s=srcd: e.dma_start(d[:, :], s[:, :]),
                     inc="ld", n=16)

            k.wait("vector", "ld")
            k.wait("vector", "gld")
            v_op(k, lambda v: v.tensor_sub(t_rbeff[:, :], t_rb[:, :], t_cost[:, :]))
            k.wait("tensor", "vq")
            k.op("tensor", lambda t: t.matmul(ps[0][:, 0:EXN], c_ones_row[:, :],
                                              t_rbeff[:, :], start=True, stop=True),
                 inc="mm")
            k.wait("vector", "mm")
            v_op(k, lambda v: v.tensor_copy(c_rbb[:, :], ps[0][:, 0:EXN]))
            v_op(k, lambda v: v.tensor_add(t_ucs[:, :], t_upb[:, :], t_coreb[:, :]))
            k.wait("vector", "vq")
            v_op(k, lambda v: v.tensor_scalar(c_ucbh[:, :], t_ucs[:, :], 0.5,
                                              None, op0=ALU.mult))
            k.wait("scalar", "ld")
            for i in range(EXN):
                s_op(k, lambda s, i=i: s.activation(c_gb1[i][:, :], c_b1[i][:, :],
                                                    AF.Gelu))
            s_op(k, lambda s: s.activation(t_aexp[:, :], t_alog[:, :], AF.Exp))
            s_op(k, lambda s: s.mul(c_aneg[:, :], t_aexp[:, :], -1.0), dep=True)
            v_op(k, lambda v: v.memset(t_ksum[:, :], 0.0))
            wbuf0 = sb("wbuf0", 128, LF, BF16, p0es)
            gb_done = k.val("sq")
            for ei in range(EXN):
                ke = HID[ei] // 128
                for mb in range(ke):
                    gv = k.op("gpsimd", lambda e, ei=ei, mb=mb: e.dma_start(
                        wbuf0[:, :], e_w2[ei][mb * 128:(mb + 1) * 128, :]),
                        inc="gld", n=16)
                    k.wait("tensor", "gld", gv)
                    k.wait("tensor", "sq")
                    last = (mb == ke - 1)
                    for half in range(2):
                        k.op("tensor", lambda t, ei=ei, mb=mb, half=half, ke=ke:
                             t.matmul(ps[half][0:1, :],
                                      c_gb1[ei][:, mb:mb + 1],
                                      wbuf0[:, half * 512:(half + 1) * 512],
                                      start=(mb == 0), stop=(mb == ke - 1)),
                             inc="mm" if (last and half == 1) else None)
                    if not last:
                        k.op("tensor", lambda t: t.matmul(
                            ps[2][0:1, 0:1], c_ones_row[0:1, 0:1],
                            c_ones_row[0:1, 0:1], start=True, stop=True), inc="mm")
                    k.wait("gpsimd", "mm")
                k.wait("scalar", "mm")
                k.wait("scalar", "ld")
                k.wait("scalar", "st")
                s_op(k, lambda s: s.activation(t_krow[0:1, 0:512], ps[0][0:1, :],
                                               AF.Copy, bias=0.0, scale=-1.0))
                s_op(k, lambda s: s.activation(t_krow[0:1, 512:1024], ps[1][0:1, :],
                                               AF.Copy, bias=0.0, scale=-1.0))
                k.wait("vector", "sq")
                v_op(k, lambda v: v.tensor_sub(t_ksum[:, :], t_ksum[:, :],
                                               t_krow[:, :]), dep=True)
                k.wait("sync", "sq")
                k.op("sync", lambda e, ei=ei: e.dma_start(
                    c_kneg[ei:ei + 1, :], t_krow[:, :]), inc="st", n=16)
            v_op(k, lambda v: v.tensor_reduce(
                c_cc[:, :].rearrange("p (m one) -> p m one", one=1),
                t_b2all[:, :].rearrange("p (m e) -> p m e", e=EXN),
                op=ALU.add, axis=AX.X), dep=True)
            v_op(k, lambda v: v.tensor_scalar(t_sb2t[:, :], t_sb2[:, :], 0.1,
                                              None, op0=ALU.mult), dep=True)
            k.wait("vector", "sq")
            v_op(k, lambda v: v.tensor_add(c_cc[:, :], c_cc[:, :], t_sb2t[:, :]),
                 dep=True)
            for mo in range(LF // 128):
                k.wait("tensor", "vq")
                k.op("tensor", lambda t, mo=mo: t.matmul(
                    ps[3][:, 0:1], t_ksum[0:1, mo * 128:(mo + 1) * 128],
                    c_ones_row[0:1, 0:1], start=True, stop=True), inc="mm")
                k.wait("vector", "mm")
                v_op(k, lambda v, mo=mo: v.tensor_add(
                    c_cc[:, mo:mo + 1], c_cc[:, mo:mo + 1], ps[3][:, 0:1]), dep=True)
            k.flush()
            p0es.close()

        def ffn_phase(src, call):
            out_t = s_x1 if call == 1 else xout
            pes = ExitStack()
            with pes:
                for sup in range(TOKC // SUP):
                    spes = ExitStack()
                    with spes:
                        accb = sb(f"f{call}s{sup}_acc", 128, 8 * SUP, F32, spes)
                        wfm = sb(f"f{call}s{sup}_wfm", EXN, SUP, F32, spes)
                        for _e in ("sync", "gpsimd", "tensor", "scalar", "vector"):
                            for _s in ("mm", "vq", "sq", "st"):
                                k.wait(_e, _s)
                        v_op(k, lambda v: v.memset(accb[:, :], 0.0))
                        for m in range(LF // 128):
                            v_op(k, lambda v, m=m: v.tensor_scalar(
                                accb[:, m * SUP:(m + 1) * SUP],
                                accb[:, m * SUP:(m + 1) * SUP],
                                c_cc[:, m:m + 1], None, op0=ALU.add), dep=True)
                        apes = ExitStack()
                        with apes:
                            gHs = sb(f"f{call}s{sup}_gH", 128, 8 * SUP, BF16, apes)
                            a2 = ExitStack()
                            with a2:
                                dwn_sb = sb(f"fA{call}{sup}_dwn", 128, 16 * LF, BF16, a2)
                                rtr_sb = sb(f"fA{call}{sup}_rtr", 128, 16 * EXN, BF16, a2)
                                k.wait("gpsimd", "mm")
                                k.op("gpsimd", lambda e: e.dma_start(
                                    dwn_sb[:, :].rearrange("p (kb m) -> p kb m", m=LF),
                                    ffn_down_w.rearrange("(kb p) m -> p kb m", p=128)),
                                    inc="gld", n=16)
                                k.op("gpsimd", lambda e: e.dma_start(
                                    rtr_sb[:, :].rearrange("p (kb m) -> p kb m", m=EXN),
                                    ffn_router_w.rearrange("(kb p) m -> p kb m", p=128)),
                                    inc="gld", n=16)
                                x0_sb = sb(f"fA{call}{sup}_x0", 128, 16 * NSUB, F32, a2)
                                xln = sb(f"fA{call}{sup}_xln", 128, 16 * NSUB, BF16, a2)
                                cxl = sb(f"fA{call}{sup}_cx", 128, 16 * NSUB, BF16, a2)
                                tmp = sb(f"fA{call}{sup}_tmp", 128, NSUB, F32, a2)
                                xsq = sb(f"fA{call}{sup}_xsq", 128, NSUB, F32, a2)
                                rmu = sb(f"fA{call}{sup}_rmu", 1, NSUB, F32, a2)
                                rms_ = sb(f"fA{call}{sup}_rms", 1, NSUB, F32, a2)
                                rm2 = sb(f"fA{call}{sup}_rm2", 1, NSUB, F32, a2)
                                rva = sb(f"fA{call}{sup}_rva", 1, NSUB, F32, a2)
                                rrs = sb(f"fA{call}{sup}_rrs", 1, NSUB, F32, a2)
                                rmr = sb(f"fA{call}{sup}_rmr", 1, NSUB, F32, a2)
                                rr = [sb(f"fA{call}{sup}_r{i}", 128,
                                         EXN if i > 1 else 1, F32, a2)
                                      for i in range(12)]
                                r_sc = [sb(f"fA{call}{sup}_s{i}", 128, 1, F32, a2)
                                        for i in range(6)]
                                k.barrier() if sup == 0 and call == 2 else None
                                for ncs in range(SUP // NSUB):
                                    tok0 = sup * SUP + ncs * NSUB
                                    emit_ln_stage(k, nc, ps, src, tok0, x0_sb, xln,
                                                  cxl, tmp, xsq, rmu, rms_, rm2, rva,
                                                  rrs, rmr, c_ones_col, c_ones_row,
                                                  s_xlnbf, s_cx, store_ln=True)
                                    k.wait("tensor", "gld")
                                    k.wait("tensor", "vq")
                                    for m in range(LF // 128):
                                        pb = ps[m % 2]
                                        if m >= 2:
                                            k.wait("tensor", "sq")
                                        for kk in range(16):
                                            k.op("tensor", lambda t, m=m, kk=kk, pb=pb:
                                                 t.matmul(pb[:, :],
                                                          dwn_sb[:, kk * LF + m * 128:
                                                                 kk * LF + m * 128 + 128],
                                                          xln[:, kk * NSUB:(kk + 1) * NSUB],
                                                          start=(kk == 0), stop=(kk == 15)),
                                                 inc="mm" if kk == 15 else None)
                                        k.wait("scalar", "mm")
                                        s_op(k, lambda s, m=m, pb=pb, ncs=ncs: s.activation(
                                            gHs[:, m * SUP + ncs * NSUB:
                                                m * SUP + ncs * NSUB + NSUB],
                                            pb[:, :], AF.Gelu,
                                            bias=c_downb[:, m:m + 1], scale=1.0))
                                    emit_router(k, nc, ps, xln, rtr_sb, c_rbb, c_ident,
                                                c_ones_col, wfm, ncs, rr, r_sc)
                            bpes = ExitStack()
                            with bpes:
                                G_sb = sb(f"fB{call}{sup}_G", 128, 64 * NSUB, BF16, bpes)
                                w1b = [sb(f"fB{call}{sup}_w1{i}", 128, 8 * 128, BF16,
                                          bpes) for i in range(2)]
                                w2b = [sb(f"fB{call}{sup}_w2{i}", 128, 32 * 128, BF16,
                                          bpes) for i in range(2)]
                                wbs = [sb(f"fB{call}{sup}_wb{i}", 128, NSUB, F32, bpes)
                                       for i in range(2)]
                                gtmp = sb(f"fB{call}{sup}_gt", 128, NSUB, F32, bpes)
                                k.wait("gpsimd", "pc")
                                scope_guard(k)
                                bank_sq = {0: k.val("sq"), 1: k.val("sq")}
                                w1_slot_mm = {0: 0, 1: 0}
                                w2_slot_mm = {0: 0, 1: 0}
                                for ei in range(EXN):
                                    ke = HID[ei] // 128
                                    k.wait("tensor", "vq")
                                    k.wait("tensor", "sq")
                                    for h2 in range(2):
                                        k.op("tensor", lambda t, ei=ei, h2=h2: t.matmul(
                                            ps[4 + h2][:, :],
                                            c_oh8[:, ei * 128:(ei + 1) * 128],
                                            wfm[:, h2 * NSUB:(h2 + 1) * NSUB],
                                            start=True, stop=True), inc="mm")
                                    k.wait("vector", "mm")
                                    for h2 in range(2):
                                        v_op(k, lambda v, h2=h2: v.tensor_copy(
                                            wbs[h2][:, :], ps[4 + h2][:, :]))
                                    wbv = k.val("vq")
                                    for mb in range(ke):
                                        slot = mb % 2
                                        k.wait("gpsimd", "mm", w1_slot_mm[slot])
                                        gv = k.op("gpsimd", lambda e, ei=ei, mb=mb,
                                                  slot=slot: e.dma_start(
                                                      w1b[slot][:, :].rearrange(
                                                          "p (kb m) -> p kb m", m=128),
                                                      p_ew1[ei][:, mb * 128:(mb + 1) * 128]
                                                      .rearrange("(kb p) m -> p kb m",
                                                                 p=128)),
                                                  inc="gld", n=16)
                                        k.wait("tensor", "gld", gv)
                                        for h2 in range(2):
                                            pb = ps[h2]
                                            k.wait("tensor", "sq", bank_sq[h2])
                                            for kk in range(8):
                                                k.op("tensor", lambda t, kk=kk, slot=slot,
                                                     pb=pb, h2=h2: t.matmul(
                                                         pb[:, :],
                                                         w1b[slot][:, kk * 128:(kk + 1) * 128],
                                                         gHs[:, kk * SUP + h2 * NSUB:
                                                             kk * SUP + h2 * NSUB + NSUB],
                                                         start=(kk == 0), stop=(kk == 7)),
                                                     inc="mm" if kk == 7 else None)
                                            k.wait("scalar", "mm")
                                            k.wait("scalar", "vq")
                                            s_op(k, lambda s, ei=ei, mb=mb, pb=pb:
                                                 s.activation(gtmp[:, :], pb[:, :],
                                                              AF.Gelu,
                                                              bias=c_b1[ei][:, mb:mb + 1],
                                                              scale=1.0), dep=True)
                                            bank_sq[h2] = k.val("sq")
                                            w1_slot_mm[slot] = k.val("mm")
                                            k.wait("vector", "sq")
                                            v_op(k, lambda v, mb=mb, h2=h2: v.tensor_mul(
                                                G_sb[:, (2 * mb + h2) * NSUB:
                                                     (2 * mb + h2 + 1) * NSUB],
                                                gtmp[:, :], wbs[h2][:, :]))
                                    gdone = k.val("vq")
                                    for mo in range(LF // 128):
                                        slot = mo % 2
                                        k.wait("gpsimd", "mm", w2_slot_mm[slot])
                                        wv2 = k.op("gpsimd", lambda e, ei=ei, mo=mo,
                                                   slot=slot, ke=ke: e.dma_start(
                                                       w2b[slot][:, 0:ke * 128].rearrange(
                                                           "p (kb m) -> p kb m", m=128),
                                                       p_ew2[ei][:, mo * 128:(mo + 1) * 128]
                                                       .rearrange("(kb p) m -> p kb m",
                                                                  p=128)),
                                                   inc="gld", n=16)
                                        k.wait("tensor", "gld", wv2)
                                        k.wait("tensor", "vq", gdone)
                                        if mo >= 1:
                                            k.wait("tensor", "vq")
                                        for h2 in range(2):
                                            pb = ps[2 + h2]
                                            for mb in range(ke):
                                                k.op("tensor", lambda t, mb=mb, slot=slot,
                                                     pb=pb, h2=h2, ke=ke: t.matmul(
                                                         pb[:, :],
                                                         w2b[slot][:, mb * 128:(mb + 1) * 128],
                                                         G_sb[:, (2 * mb + h2) * NSUB:
                                                              (2 * mb + h2 + 1) * NSUB],
                                                         start=(mb == 0), stop=(mb == ke - 1)),
                                                     inc="mm" if mb == ke - 1 else None)
                                        w2_slot_mm[slot] = k.val("mm")
                                        k.wait("vector", "mm")
                                        for h2 in range(2):
                                            v_op(k, lambda v, mo=mo, h2=h2: v.tensor_add(
                                                accb[:, mo * SUP + h2 * NSUB:
                                                     mo * SUP + h2 * NSUB + NSUB],
                                                accb[:, mo * SUP + h2 * NSUB:
                                                     mo * SUP + h2 * NSUB + NSUB],
                                                ps[2 + h2][:, :]), dep=(h2 == 0))
                            cpes = ExitStack()
                            with cpes:
                                s1b = sb(f"fC{call}{sup}_s1", 128, 8 * NSUB, BF16, cpes)
                                shb = [sb(f"fC{call}{sup}_sh{i}", 128, 8 * 128, BF16,
                                          cpes) for i in range(2)]
                                scope_guard(k)
                                sh_slot_mm = {0: 0, 1: 0}
                                for h2 in range(SUP // NSUB):
                                    for mo in range(LF // 128):
                                        slot = mo % 2
                                        k.wait("gpsimd", "mm", sh_slot_mm[slot])
                                        sv = k.op("gpsimd", lambda e, mo=mo, slot=slot:
                                                  e.dma_start(
                                                      shb[slot][:, :].rearrange(
                                                          "p (kb m) -> p kb m", m=128),
                                                      ffn_shared_w1[:, mo * 128:(mo + 1) * 128]
                                                      .rearrange("(kb p) m -> p kb m", p=128)),
                                                  inc="gld", n=16)
                                        k.wait("tensor", "gld", sv)
                                        pb = ps[mo % 2]
                                        k.wait("tensor", "sq")
                                        k.wait("tensor", "vq")
                                        for kk in range(8):
                                            k.op("tensor", lambda t, kk=kk, slot=slot,
                                                 pb=pb, h2=h2: t.matmul(
                                                     pb[:, :],
                                                     shb[slot][:, kk * 128:(kk + 1) * 128],
                                                     gHs[:, kk * SUP + h2 * NSUB:
                                                         kk * SUP + h2 * NSUB + NSUB],
                                                     start=(kk == 0), stop=(kk == 7)),
                                                 inc="mm" if kk == 7 else None)
                                        sh_slot_mm[slot] = k.val("mm")
                                        k.wait("scalar", "mm")
                                        s_op(k, lambda s, mo=mo, pb=pb: s.activation(
                                            s1b[:, mo * NSUB:(mo + 1) * NSUB], pb[:, :],
                                            AF.Gelu, bias=c_sb1[:, mo:mo + 1], scale=1.0))
                                    for mo in range(LF // 128):
                                        slot = mo % 2
                                        k.wait("gpsimd", "mm", sh_slot_mm[slot])
                                        sv = k.op("gpsimd", lambda e, mo=mo, slot=slot:
                                                  e.dma_start(
                                                      shb[slot][:, :].rearrange(
                                                          "p (kb m) -> p kb m", m=128),
                                                      ffn_shared_w2[:, mo * 128:(mo + 1) * 128]
                                                      .rearrange("(kb p) m -> p kb m", p=128)),
                                                  inc="gld", n=16)
                                        k.wait("tensor", "gld", sv)
                                        pb = ps[2 + mo % 2]
                                        if mo >= 2:
                                            k.wait("tensor", "vq")
                                        k.wait("tensor", "sq")
                                        for kk in range(8):
                                            k.op("tensor", lambda t, kk=kk, slot=slot,
                                                 pb=pb: t.matmul(
                                                     pb[:, :],
                                                     shb[slot][:, kk * 128:(kk + 1) * 128],
                                                     s1b[:, kk * NSUB:(kk + 1) * NSUB],
                                                     start=(kk == 0), stop=(kk == 7)),
                                                 inc="mm" if kk == 7 else None)
                                        sh_slot_mm[slot] = k.val("mm")
                                        k.wait("vector", "mm")
                                        v_op(k, lambda v, mo=mo, h2=h2, pb=pb:
                                             v.scalar_tensor_tensor(
                                                 accb[:, mo * SUP + h2 * NSUB:
                                                      mo * SUP + h2 * NSUB + NSUB],
                                                 pb[:, :], 0.1,
                                                 accb[:, mo * SUP + h2 * NSUB:
                                                      mo * SUP + h2 * NSUB + NSUB],
                                                 ALU.mult, ALU.add), dep=True)
                                    k.wait("tensor", "st")
                                    for mo in range(LF // 128):
                                        pb = ps[4 + mo % 2]
                                        if mo >= 2:
                                            k.wait("tensor", "vq")
                                        k.op("tensor", lambda t, mo=mo, h2=h2, pb=pb:
                                             t.matmul(pb[:, :],
                                                      c_kneg[:, mo * 128:(mo + 1) * 128],
                                                      wfm[:, h2 * NSUB:(h2 + 1) * NSUB],
                                                      start=True, stop=True), inc="mm")
                                        k.wait("vector", "mm")
                                        v_op(k, lambda v, mo=mo, h2=h2, pb=pb: v.tensor_add(
                                            accb[:, mo * SUP + h2 * NSUB:
                                                 mo * SUP + h2 * NSUB + NSUB],
                                            accb[:, mo * SUP + h2 * NSUB:
                                                 mo * SUP + h2 * NSUB + NSUB],
                                            pb[:, :]), dep=True)
                        dpes = ExitStack()
                        with dpes:
                            xnew = sb(f"fD{call}{sup}_xn", 128, 16 * NSUB, F32, dpes)
                            u_sb = sb(f"fD{call}{sup}_u", 128, 16 * NSUB, F32, dpes)
                            cxl2 = sb(f"fD{call}{sup}_cx", 128, 16 * NSUB, BF16, dpes)
                            minb = sb(f"fD{call}{sup}_mb", 128, NSUB, BF16, dpes)
                            abf = sb(f"fD{call}{sup}_ab", 128, 8 * NSUB, BF16, dpes)
                            upw2 = [sb(f"fD{call}{sup}_up{i}", 128, 8 * 128, BF16, dpes)
                                    for i in range(2)]
                            cow2 = [sb(f"fD{call}{sup}_co{i}", 128, 16 * 128, BF16, dpes)
                                    for i in range(2)]
                            x0t = sb(f"fD{call}{sup}_x0t", 128, NSUB, F32, dpes)
                            t1t = sb(f"fD{call}{sup}_t1", 128, NSUB, F32, dpes)
                            tmp2 = sb(f"fD{call}{sup}_tp", 128, NSUB, F32, dpes)
                            xsq2 = sb(f"fD{call}{sup}_xq", 128, NSUB, F32, dpes)
                            rmu2 = sb(f"fD{call}{sup}_ru", 1, NSUB, F32, dpes)
                            rms2 = sb(f"fD{call}{sup}_rm", 1, NSUB, F32, dpes)
                            rm22 = sb(f"fD{call}{sup}_r2", 1, NSUB, F32, dpes)
                            rva2 = sb(f"fD{call}{sup}_rv", 1, NSUB, F32, dpes)
                            rrs2 = sb(f"fD{call}{sup}_rr", 1, NSUB, F32, dpes)
                            rmr2 = sb(f"fD{call}{sup}_rx", 1, NSUB, F32, dpes)
                            scope_guard(k)
                            k.wait("vector", "mm")
                            k.wait("vector", "st")
                            d_slot_mm = {0: 0, 1: 0}
                            for ncs in range(SUP // NSUB):
                                tok0 = sup * SUP + ncs * NSUB
                                k.wait("vector", "vq")
                                k.wait("vector", "mm")
                                for m in range(LF // 128):
                                    v_op(k, lambda v, m=m, ncs=ncs: v.tensor_copy(
                                        abf[:, m * NSUB:(m + 1) * NSUB],
                                        accb[:, m * SUP + ncs * NSUB:
                                             m * SUP + ncs * NSUB + NSUB]))
                                abv = k.val("vq")
                                cv = None
                                k.wait("sync", "mm")
                                for kk in range(16):
                                    cv = k.op("sync", lambda e, kk=kk, tok0=tok0:
                                              e.dma_start(
                                                  cxl2[:, kk * NSUB:(kk + 1) * NSUB],
                                                  s_cx[kk * 128:(kk + 1) * 128,
                                                       tok0:tok0 + NSUB]),
                                              inc="ld", n=16)
                                for mo in range(D // 128):
                                    slot = mo % 2
                                    k.wait("gpsimd", "mm", d_slot_mm[slot])
                                    uv = k.op("gpsimd", lambda e, mo=mo, slot=slot:
                                              e.dma_start(
                                                  upw2[slot][:, :].rearrange(
                                                      "p (kb m) -> p kb m", m=128),
                                                  p_fup[:, mo * 128:(mo + 1) * 128]
                                                  .rearrange("(kb p) m -> p kb m", p=128)),
                                              inc="gld", n=16)
                                    cv2 = k.op("gpsimd", lambda e, mo=mo, slot=slot:
                                               e.dma_start(
                                                   cow2[slot][:, :].rearrange(
                                                       "p (kb m) -> p kb m", m=128),
                                                   p_fcore[:, mo * 128:(mo + 1) * 128]
                                                   .rearrange("(kb p) m -> p kb m", p=128)),
                                               inc="gld", n=16)
                                    k.wait("tensor", "gld", cv2)
                                    k.wait("tensor", "ld", cv)
                                    k.wait("tensor", "vq", abv)
                                    pb = ps[mo % 2]
                                    if mo >= 2:
                                        k.wait("tensor", "vq")
                                    for kk in range(8):
                                        k.op("tensor", lambda t, kk=kk, slot=slot, pb=pb:
                                             t.matmul(pb[:, :],
                                                      upw2[slot][:, kk * 128:(kk + 1) * 128],
                                                      abf[:, kk * NSUB:(kk + 1) * NSUB],
                                                      start=(kk == 0), stop=False))
                                    for kk in range(16):
                                        k.op("tensor", lambda t, kk=kk, slot=slot, pb=pb:
                                             t.matmul(pb[:, :],
                                                      cow2[slot][:, kk * 128:(kk + 1) * 128],
                                                      cxl2[:, kk * NSUB:(kk + 1) * NSUB],
                                                      start=False, stop=(kk == 15)),
                                             inc="mm" if kk == 15 else None)
                                    d_slot_mm[slot] = k.val("mm")
                                    k.wait("sync", "vq")
                                    xv = k.op("sync", lambda e, mo=mo, tok0=tok0:
                                              e.dma_start(
                                                  x0t[:, :],
                                                  src[mo * 128:(mo + 1) * 128,
                                                      tok0:tok0 + NSUB]),
                                              inc="ld", n=16)
                                    k.wait("vector", "mm")
                                    k.wait("vector", "vq")
                                    v_op(k, lambda v, mo=mo, pb=pb: v.tensor_scalar(
                                        t1t[:, :], pb[:, :], 0.5,
                                        c_ucbh[:, mo:mo + 1], op0=ALU.mult,
                                        op1=ALU.add))
                                    k.wait("vector", "ld", xv)
                                    v_op(k, lambda v, mo=mo: v.tensor_add(
                                        xnew[:, mo * NSUB:(mo + 1) * NSUB],
                                        x0t[:, :], t1t[:, :]), dep=True)
                                xnv = k.val("vq")
                                k.wait("sync", "vq", xnv)
                                for kk in range(16):
                                    k.op("sync", lambda e, kk=kk, tok0=tok0:
                                         e.dma_start(
                                             out_t[kk * 128:(kk + 1) * 128,
                                                   tok0:tok0 + NSUB],
                                             xnew[:, kk * NSUB:(kk + 1) * NSUB]),
                                         inc="st", n=16)
                                if call == 1:
                                    emit_ln_simple(k, nc, ps, xnew, u_sb, tmp2, xsq2,
                                                   rmu2, rms2, rm22, rva2, rrs2, rmr2,
                                                   c_ones_col, c_ones_row, D)
                                    uv2 = k.val("vq")
                                    k.wait("sync", "vq", uv2)
                                    for kk in range(16):
                                        k.op("sync", lambda e, kk=kk, tok0=tok0:
                                             e.dma_start(
                                                 s_u[kk * 128:(kk + 1) * 128,
                                                     tok0:tok0 + NSUB],
                                                 u_sb[:, kk * NSUB:(kk + 1) * NSUB]),
                                             inc="st", n=16)
                                    emit_rms_prep(k, nc, ps, u_sb, xsq2, rms2, rrs2,
                                                  c_ones_col, c_ones_row, D)
                                    for kk in range(16):
                                        k.wait("vector", "st")
                                        v_op(k, lambda v, kk=kk: v.scalar_tensor_tensor(
                                            minb[:, :],
                                            u_sb[:, kk * NSUB:(kk + 1) * NSUB],
                                            c_ln1[:, kk:kk + 1], ps[4][:, :],
                                            ALU.mult, ALU.mult), dep=True)
                                        mv = k.val("vq")
                                        k.wait("sync", "vq", mv)
                                        k.op("sync", lambda e, kk=kk, tok0=tok0:
                                             e.dma_start(
                                                 g_min_in[kk * 128:(kk + 1) * 128,
                                                          tok0:tok0 + NSUB],
                                                 minb[:, :]), inc="st", n=16)
                                k.wait("vector", "st")
                    k.flush()

        def emit_ln_stage(k, nc, ps, src, tok0, x0_sb, xln, cxl, tmp, xsq,
                          rmu, rms_, rm2, rva, rrs, rmr, ones_c, ones_r,
                          st_ln, st_cx, store_ln):
            k.wait("sync", "vq")
            k.wait("sync", "sq")
            k.wait("sync", "mm")
            k.wait("vector", "st")
            k.wait("scalar", "st")
            ldv = None
            for kk in range(16):
                ldv = k.op("sync", lambda e, kk=kk: e.dma_start(
                    x0_sb[:, kk * NSUB:(kk + 1) * NSUB],
                    src[kk * 128:(kk + 1) * 128, tok0:tok0 + NSUB]), inc="ld", n=16)
            k.wait("tensor", "ld", ldv)
            k.wait("tensor", "vq")
            k.wait("tensor", "sq")
            for kk in range(16):
                k.op("tensor", lambda t, kk=kk: t.matmul(
                    ps[6][0:1, :], ones_c[:, :], x0_sb[:, kk * NSUB:(kk + 1) * NSUB],
                    start=(kk == 0), stop=(kk == 15)), inc="mm" if kk == 15 else None)
            k.wait("scalar", "ld", ldv)
            for kk in range(16):
                s_op(k, lambda s, kk=kk: s.activation(
                    xsq[:, :], x0_sb[:, kk * NSUB:(kk + 1) * NSUB], AF.Square),
                    dep=True)
                k.wait("tensor", "sq")
                k.op("tensor", lambda t, kk=kk: t.matmul(
                    ps[7][0:1, :], ones_c[:, :], xsq[:, :],
                    start=(kk == 0), stop=(kk == 15)), inc="mm" if kk == 15 else None)
                if kk < 15:
                    k.op("tensor", lambda t: t.matmul(
                        ps[5][0:1, 0:1], ones_r[0:1, 0:1], ones_r[0:1, 0:1],
                        start=True, stop=True), inc="mm")
                    k.wait("scalar", "mm")
            k.wait("vector", "mm")
            v_op(k, lambda v: v.tensor_scalar(rmu[:, :], ps[6][0:1, :], 1.0 / D,
                                              None, op0=ALU.mult))
            v_op(k, lambda v: v.tensor_scalar(rms_[:, :], ps[7][0:1, :], 1.0 / D,
                                              None, op0=ALU.mult))
            k.wait("vector", "vq")
            v_op(k, lambda v: v.tensor_mul(rm2[:, :], rmu[:, :], rmu[:, :]))
            k.wait("vector", "vq")
            v_op(k, lambda v: v.tensor_sub(rva[:, :], rms_[:, :], rm2[:, :]))
            k.wait("scalar", "vq")
            s_op(k, lambda s: s.activation(rva[:, :], rva[:, :], AF.Ln, bias=c_eps[0:1, :],
                                           scale=1.0), dep=True)
            s_op(k, lambda s: s.activation(rrs[:, :], rva[:, :], AF.Exp, bias=0.0,
                                           scale=c_mhalf[0:1, :]), dep=True)
            k.wait("vector", "sq")
            v_op(k, lambda v: v.tensor_mul(rmr[:, :], rmu[:, :], rrs[:, :]))
            k.wait("tensor", "vq")
            k.op("tensor", lambda t: t.matmul(ps[4][:, :], ones_r[:, :], rrs[:, :],
                                              start=True, stop=True), inc="mm")
            k.op("tensor", lambda t: t.matmul(ps[5][:, :], ones_r[:, :], rmr[:, :],
                                              start=True, stop=True), inc="mm")
            k.wait("vector", "mm")
            k.wait("scalar", "mm")
            if tok0 == 0 and not _dbgx_done:
                _dbgx_done.append(1)
                k.wait("sync", "mm")
                for kk in range(16):
                    k.op("sync", lambda e, kk=kk: e.dma_start(
                        s_dbgx[kk * 128:(kk + 1) * 128, :],
                        x0_sb[:, kk * NSUB:(kk + 1) * NSUB]), inc="st", n=16)
            for kk in range(16):
                v_op(k, lambda v, kk=kk: v.tensor_mul(
                    tmp[:, :], x0_sb[:, kk * NSUB:(kk + 1) * NSUB], ps[4][:, :]),
                    dep=True)
                k.wait("vector", "vq")
                v_op(k, lambda v, kk=kk: v.tensor_sub(
                    xln[:, kk * NSUB:(kk + 1) * NSUB], tmp[:, :], ps[5][:, :]))
                k.wait("scalar", "vq")
                s_op(k, lambda s, kk=kk: s.activation(
                    cxl[:, kk * NSUB:(kk + 1) * NSUB],
                    xln[:, kk * NSUB:(kk + 1) * NSUB], AF.Gelu))
            xlv = k.val("vq")
            clv = k.val("sq")
            if store_ln:
                k.wait("sync", "vq", xlv)
                k.wait("sync", "sq", clv)
                for kk in range(16):
                    k.op("sync", lambda e, kk=kk: e.dma_start(
                        st_ln[kk * 128:(kk + 1) * 128, tok0:tok0 + NSUB],
                        xln[:, kk * NSUB:(kk + 1) * NSUB]), inc="st", n=16)
                    k.op("sync", lambda e, kk=kk: e.dma_start(
                        st_cx[kk * 128:(kk + 1) * 128, tok0:tok0 + NSUB],
                        cxl[:, kk * NSUB:(kk + 1) * NSUB]), inc="st", n=16)

        def emit_ln_simple(k, nc, ps, xin_sb, out_sb, tmp, xsq, rmu, rms_, rm2,
                           rva, rrs, rmr, ones_c, ones_r, DD):
            nt = 16
            k.wait("tensor", "vq")
            k.wait("tensor", "sq")
            k.wait("scalar", "vq")
            for kk in range(nt):
                k.op("tensor", lambda t, kk=kk: t.matmul(
                    ps[6][0:1, :], ones_c[:, :], xin_sb[:, kk * NSUB:(kk + 1) * NSUB],
                    start=(kk == 0), stop=(kk == nt - 1)),
                    inc="mm" if kk == nt - 1 else None)
            for kk in range(nt):
                s_op(k, lambda s, kk=kk: s.activation(
                    xsq[:, :], xin_sb[:, kk * NSUB:(kk + 1) * NSUB], AF.Square),
                    dep=True)
                k.wait("tensor", "sq")
                k.op("tensor", lambda t, kk=kk: t.matmul(
                    ps[7][0:1, :], ones_c[:, :], xsq[:, :],
                    start=(kk == 0), stop=(kk == nt - 1)),
                    inc="mm" if kk == nt - 1 else None)
                if kk < nt - 1:
                    k.op("tensor", lambda t: t.matmul(
                        ps[5][0:1, 0:1], ones_r[0:1, 0:1], ones_r[0:1, 0:1],
                        start=True, stop=True), inc="mm")
                    k.wait("scalar", "mm")
            k.wait("vector", "mm")
            v_op(k, lambda v: v.tensor_scalar(rmu[:, :], ps[6][0:1, :], 1.0 / DD,
                                              None, op0=ALU.mult))
            v_op(k, lambda v: v.tensor_scalar(rms_[:, :], ps[7][0:1, :], 1.0 / DD,
                                              None, op0=ALU.mult))
            k.wait("vector", "vq")
            v_op(k, lambda v: v.tensor_mul(rm2[:, :], rmu[:, :], rmu[:, :]))
            k.wait("vector", "vq")
            v_op(k, lambda v: v.tensor_sub(rva[:, :], rms_[:, :], rm2[:, :]))
            k.wait("scalar", "vq")
            s_op(k, lambda s: s.activation(rva[:, :], rva[:, :], AF.Ln, bias=c_eps[0:1, :],
                                           scale=1.0), dep=True)
            s_op(k, lambda s: s.activation(rrs[:, :], rva[:, :], AF.Exp, bias=0.0,
                                           scale=c_mhalf[0:1, :]), dep=True)
            k.wait("vector", "sq")
            v_op(k, lambda v: v.tensor_mul(rmr[:, :], rmu[:, :], rrs[:, :]))
            k.wait("tensor", "vq")
            k.op("tensor", lambda t: t.matmul(ps[4][:, :], ones_r[:, :], rrs[:, :],
                                              start=True, stop=True), inc="mm")
            k.op("tensor", lambda t: t.matmul(ps[5][:, :], ones_r[:, :], rmr[:, :],
                                              start=True, stop=True), inc="mm")
            k.wait("vector", "mm")
            for kk in range(nt):
                v_op(k, lambda v, kk=kk: v.tensor_mul(
                    tmp[:, :], xin_sb[:, kk * NSUB:(kk + 1) * NSUB], ps[4][:, :]),
                    dep=True)
                k.wait("vector", "vq")
                v_op(k, lambda v, kk=kk: v.tensor_sub(
                    out_sb[:, kk * NSUB:(kk + 1) * NSUB], tmp[:, :], ps[5][:, :]))

        def emit_rms_prep(k, nc, ps, xin_sb, xsq, rms_, rrs, ones_c, ones_r, DD):
            nt = 16
            k.wait("tensor", "vq")
            k.wait("tensor", "sq")
            k.wait("scalar", "vq")
            for kk in range(nt):
                s_op(k, lambda s, kk=kk: s.activation(
                    xsq[:, :], xin_sb[:, kk * NSUB:(kk + 1) * NSUB], AF.Square),
                    dep=True)
                k.wait("tensor", "sq")
                k.op("tensor", lambda t, kk=kk: t.matmul(
                    ps[7][0:1, :], ones_c[:, :], xsq[:, :],
                    start=(kk == 0), stop=(kk == nt - 1)),
                    inc="mm" if kk == nt - 1 else None)
                if kk < nt - 1:
                    k.op("tensor", lambda t: t.matmul(
                        ps[5][0:1, 0:1], ones_r[0:1, 0:1], ones_r[0:1, 0:1],
                        start=True, stop=True), inc="mm")
                    k.wait("scalar", "mm")
            k.wait("vector", "mm")
            v_op(k, lambda v: v.tensor_scalar(rms_[:, :], ps[7][0:1, :], 1.0 / DD,
                                              None, op0=ALU.mult))
            k.wait("scalar", "vq")
            s_op(k, lambda s: s.activation(rms_[:, :], rms_[:, :], AF.Ln, bias=c_eps[0:1, :],
                                           scale=1.0), dep=True)
            s_op(k, lambda s: s.activation(rrs[:, :], rms_[:, :], AF.Exp, bias=0.0,
                                           scale=c_mhalf[0:1, :]), dep=True)
            k.wait("tensor", "sq")
            k.op("tensor", lambda t: t.matmul(ps[4][:, :], ones_r[:, :], rrs[:, :],
                                              start=True, stop=True), inc="mm")
            k.wait("vector", "mm")

        def emit_rms_scale(k, nc, ps, xin_sb, out_sb, xsq, rms_, rrs, ones_c,
                           ones_r, lnw, DD):
            emit_rms_prep(k, nc, ps, xin_sb, xsq, rms_, rrs, ones_c, ones_r, DD)
            for kk in range(16):
                v_op(k, lambda v, kk=kk: v.scalar_tensor_tensor(
                    out_sb[:, kk * NSUB:(kk + 1) * NSUB],
                    xin_sb[:, kk * NSUB:(kk + 1) * NSUB],
                    lnw[:, kk:kk + 1], ps[4][:, :], ALU.mult, ALU.mult))

        def emit_router(k, nc, ps, xln, rtr_sb, rbb, ident_sb, ones_c, wfm, ncs,
                        rr, r_sc):
            (r_pm, r_npm, r_ex, r_pr, r_k1, r_pk, r_k2, r_w0, r_w, r_lg,
             _a, _b) = rr
            (r_m1, r_mm2, r_dd, r_e2, r_g1, r_g2) = r_sc
            for ts in range(4):
                k.wait("tensor", "vq")
                k.wait("tensor", "sq")
                for kk in range(16):
                    k.op("tensor", lambda t, ts=ts, kk=kk: t.matmul(
                        ps[2][0:128, 0:EXN],
                        xln[:, kk * NSUB + ts * 128: kk * NSUB + ts * 128 + 128],
                        rtr_sb[:, kk * EXN:(kk + 1) * EXN],
                        start=(kk == 0), stop=(kk == 15)),
                        inc="mm" if kk == 15 else None)
                k.wait("vector", "mm")
                v_op(k, lambda v: v.tensor_add(r_lg[:, :], ps[2][0:128, 0:EXN],
                                               rbb[:, :]))
                v_op(k, lambda v: v.reduce_max(r_pm[:, 0:1], r_lg[:, :], axis=AX.X),
                     dep=True)
                k.wait("scalar", "vq")
                s_op(k, lambda s: s.mul(r_npm[:, 0:1], r_pm[:, 0:1], -1.0))
                k.wait("scalar", "sq")
                s_op(k, lambda s: s.activation(r_ex[:, :], r_lg[:, :], AF.Exp,
                                               bias=r_npm[:, 0:1], scale=1.0))
                k.wait("vector", "sq")
                v_op(k, lambda v: v.reduce_sum(r_m1[:, :], r_ex[:, :], axis=AX.X))
                v_op(k, lambda v: v.reciprocal(r_mm2[:, :], r_m1[:, :]), dep=True)
                v_op(k, lambda v: v.tensor_scalar(r_pr[:, :], r_ex[:, :],
                                                  r_mm2[:, :], None, op0=ALU.mult),
                     dep=True)
                v_op(k, lambda v: v.reduce_max(r_m1[:, :], r_pr[:, :], axis=AX.X),
                     dep=True)
                v_op(k, lambda v: v.tensor_scalar(r_k1[:, :], r_pr[:, :],
                                                  r_m1[:, :], None, op0=ALU.is_ge),
                     dep=True)
                v_op(k, lambda v: v.scalar_tensor_tensor(
                    r_pk[:, :], r_k1[:, :], -1e30, r_pr[:, :], ALU.mult, ALU.add),
                    dep=True)
                v_op(k, lambda v: v.reduce_max(r_mm2[:, :], r_pk[:, :], axis=AX.X),
                     dep=True)
                v_op(k, lambda v: v.tensor_scalar(r_k2[:, :], r_pk[:, :],
                                                  r_mm2[:, :], None, op0=ALU.is_ge),
                     dep=True)
                v_op(k, lambda v: v.tensor_sub(r_dd[:, :], r_mm2[:, :], r_m1[:, :]),
                     dep=True)
                k.wait("scalar", "vq")
                s_op(k, lambda s: s.activation(r_e2[:, :], r_dd[:, :], AF.Exp))
                k.wait("vector", "sq")
                v_op(k, lambda v: v.tensor_scalar(r_g2[:, :], r_e2[:, :], 1.0,
                                                  None, op0=ALU.add))
                v_op(k, lambda v: v.reciprocal(r_g1[:, :], r_g2[:, :]), dep=True)
                v_op(k, lambda v: v.scalar_tensor_tensor(
                    r_g2[:, :], r_g1[:, :], -1.0, ones_c[:, :], ALU.mult, ALU.add),
                    dep=True)
                v_op(k, lambda v: v.tensor_scalar(r_w0[:, :], r_k1[:, :],
                                                  r_g1[:, :], None, op0=ALU.mult))
                k.wait("vector", "vq")
                v_op(k, lambda v: v.scalar_tensor_tensor(
                    r_w[:, :], r_k2[:, :], r_g2[:, :], r_w0[:, :],
                    ALU.mult, ALU.add))
                k.wait("tensor", "vq")
                k.op("tensor", lambda t: t.transpose(
                    ps[3][0:8, 0:128], r_w[:, :], ident_sb[:, :]), inc="mm")
                k.wait("vector", "mm")
                v_op(k, lambda v, ncs=ncs, ts=ts: v.tensor_copy(
                    wfm[:, ncs * NSUB + ts * 128: ncs * NSUB + ts * 128 + 128],
                    ps[3][0:8, 0:128]))

        def mixer_phase():
            pes = ExitStack()
            with pes:
                ipw_sb = sb("mx_ipw", 128, 16 * 1408, BF16, pes)
                dtw_sb = sb("mx_dtw", 128, DC, BF16, pes)
                rhs_sb = [sb(f"mx_rhs{i}", 128, 16 * CH, BF16, pes) for i in range(2)]
                siluz = sb("mx_siluz", 128, 4 * CH, F32, pes)
                xsrcA = sb("mx_xsA", 64, CH, F32, pes)
                xsrcB = sb("mx_xsB", 64, CH, F32, pes)
                BsbA = sb("mx_BA", 64, CH, BF16, pes)
                BsbB = sb("mx_BB", 64, CH, BF16, pes)
                Csb = [sb(f"mx_C{j}", 128, CH, BF16, pes) for j in range(4)]
                dtp = sb("mx_dtp", 128, CH, BF16, pes)
                dtf = sb("mx_dtf", 128, 4 * CH, F32, pes)
                relu_t = sb("mx_rel", 128, CH, F32, pes)
                ab_t = sb("mx_ab", 128, CH, F32, pes)
                xs_sb = sb("mx_xs", 128, 4 * CH, F32, pes)
                dtx = sb("mx_dtx", 128, 4 * CH, F32, pes)
                xrep = [sb(f"mx_xr{j}", 128, CH + 4, F32, pes) for j in range(4)]
                xc_t = sb("mx_xc", 128, CH, F32, pes)
                dA = [sb(f"mx_dA{i}", 128, CH, F32, pes) for i in range(2)]
                dbx = [sb(f"mx_db{i}", 128, CH, F32, pes) for i in range(2)]
                hbuf = [sb(f"mx_h{i}", 128, CH, F32, pes) for i in range(2)]
                prodb = [sb(f"mx_p{i}", 128, CH, F32, pes) for i in range(2)]
                crepb = [sb(f"mx_cr{i}", 128, CH, F32, pes) for i in range(2)]
                yac2 = [sb(f"mx_y2{i}", 128, CH, F32, pes) for i in range(2)]
                yac = sb("mx_yac", 128, CH, F32, pes)
                states = sb("mx_st", 128, 4 * NST, F32, pes)
                ygb = [sb(f"mx_yg{i}", 128, CH, BF16, pes) for i in range(2)]

                k.op("gpsimd", lambda e: e.dma_start(
                    ipw_sb[:, :].rearrange("p (kb m) -> p kb m", m=1408),
                    ipwT.rearrange("(kb p) m -> p kb m", p=128)), inc="gld", n=16)
                k.op("gpsimd", lambda e: e.dma_start(
                    dtw_sb[:, :], dtwT[:, :]), inc="gld", n=16)
                k.wait("gpsimd", "st")
                k.op("gpsimd", lambda e: e.collective_compute(
                    "AllGather", ALU.bypass, replica_groups=[list(range(NC))],
                    ins=[g_min_in.opt()], outs=[g_min_out.opt()]), inc="cc")
                k.wait("sync", "cc")
                k.wait("tensor", "gld")

                NCHUNK = (B * T) // CH
                rhs_slot_mm = {0: 0, 1: 0}
                mxw = {"pb": [0, 0], "pc": [0, 0], "dA": [0, 0], "dbx": [0, 0],
                       "hb": [0, 0], "crep": [0, 0], "prodb": [0, 0],
                       "prodv": [0, 0], "acc": [0, 0], "ygst": [0, 0],
                       "state": [0] * (4 * NST)}
                for ch in range(NCHUNK):
                    bb = ch // (T // CH)
                    t0 = (ch % (T // CH)) * CH
                    rank = bb * 4 + t0 // TOKC
                    col0 = t0 % TOKC
                    slot = ch % 2
                    k.wait("sync", "mm", rhs_slot_mm[slot])
                    rv = k.op("sync", lambda e, slot=slot, rank=rank, col0=col0:
                              e.dma_start(
                                  rhs_sb[slot][:, :].rearrange(
                                      "p (kb m) -> p kb m", m=CH),
                                  g_min_out[rank * D:(rank + 1) * D, col0:col0 + CH]
                                  .rearrange("(kb p) m -> p kb m", p=128)),
                              inc="ld", n=16)
                    k.wait("tensor", "ld", rv)
                    k.wait("scalar", "vq")
                    k.wait("scalar", "mm")
                    for m in range(11):
                        pb = ps[m % 3]
                        k.wait("tensor", "ev")
                        for kk in range(16):
                            k.op("tensor", lambda t, m=m, kk=kk, pb=pb, slot=slot:
                                 t.matmul(pb[:, :],
                                          ipw_sb[:, kk * 1408 + m * 128:
                                                 kk * 1408 + m * 128 + 128],
                                          rhs_sb[slot][:, kk * CH:(kk + 1) * CH],
                                          start=(kk == 0), stop=(kk == 15)),
                                 inc="mm" if kk == 15 else None)
                        k.wait("scalar", "mm")
                        if m < 4:
                            k.op("scalar", lambda s, m=m, pb=pb: s.activation(
                                siluz[:, m * CH:(m + 1) * CH], pb[:, :], AF.Silu),
                                inc="ev")
                        elif m == 4:
                            k.op("scalar", lambda s, pb=pb: s.copy(
                                xsrcA[:, :], pb[0:64, :]), inc="ev")
                            k.op("scalar", lambda s, pb=pb: s.copy(
                                xsrcB[:, :], pb[64:128, :]), inc="ev")
                        elif m == 5:
                            k.op("scalar", lambda s, pb=pb: s.copy(
                                BsbA[:, :], pb[0:64, :]), inc="ev")
                            k.op("scalar", lambda s, pb=pb: s.copy(
                                BsbB[:, :], pb[64:128, :]), inc="ev")
                        elif m < 10:
                            k.op("scalar", lambda s, m=m, pb=pb: s.copy(
                                Csb[m - 6][:, :], pb[:, :]), inc="ev")
                        else:
                            k.op("scalar", lambda s, pb=pb: s.copy(
                                dtp[:, :], pb[:, :]), inc="ev")
                    rhs_slot_mm[slot] = k.val("mm")
                    k.wait("tensor", "ev")
                    for j in range(4):
                        pb = ps[4 + j % 2]
                        if j >= 2:
                            k.wait("tensor", "sq")
                        k.op("tensor", lambda t, j=j, pb=pb: t.matmul(
                            pb[:, :], dtw_sb[:, j * 128:(j + 1) * 128], dtp[:, :],
                            start=True, stop=True), inc="mm")
                        k.wait("scalar", "mm")
                        k.wait("scalar", "vq")
                        s_op(k, lambda s, j=j, pb=pb: s.activation(
                            relu_t[:, :], pb[:, :], AF.Relu,
                            bias=c_dtb[:, j:j + 1], scale=1.0), dep=True)
                        s_op(k, lambda s, j=j, pb=pb: s.activation(
                            ab_t[:, :], pb[:, :], AF.Abs,
                            bias=c_dtb[:, j:j + 1], scale=1.0))
                        s_op(k, lambda s: s.activation(
                            ab_t[:, :], ab_t[:, :], AF.Exp, bias=0.0, scale=-1.0),
                            dep=True)
                        s_op(k, lambda s: s.activation(
                            ab_t[:, :], ab_t[:, :], AF.Ln, bias=1.0, scale=1.0),
                            dep=True)
                        k.wait("vector", "sq")
                        v_op(k, lambda v, j=j: v.tensor_add(
                            dtf[:, j * CH:(j + 1) * CH], relu_t[:, :], ab_t[:, :]),
                            dep=True)
                    k.wait("tensor", "sq")
                    for j in range(4):
                        pb = ps[6]
                        k.wait("tensor", "vq")
                        src_t = xsrcA if j < 2 else xsrcB
                        k.op("tensor", lambda t, j=j, pb=pb, src_t=src_t: t.matmul(
                            pb[:, :],
                            c_selx[32 * (j % 2):32 * (j % 2) + 32, :],
                            src_t[32 * (j % 2):32 * (j % 2) + 32, :],
                            start=True, stop=True), inc="mm")
                        if t0 == 0:
                            v_op(k, lambda v, j=j: v.memset(xrep[j][:, 0:4], 0.0),
                                 dep=True)
                        else:
                            v_op(k, lambda v, j=j: v.tensor_copy(
                                xrep[j][:, 0:4], xrep[j][:, CH:CH + 4]), dep=True)
                        k.wait("vector", "mm")
                        k.wait("vector", "vq")
                        v_op(k, lambda v, j=j, pb=pb: v.tensor_copy(
                            xrep[j][:, 4:CH + 4], pb[:, :]))
                        k.wait("vector", "vq")
                        v_op(k, lambda v, j=j: v.tensor_scalar(
                            xc_t[:, :], xrep[j][:, 1:CH + 1],
                            c_convw[:, j * 4:j * 4 + 1], None, op0=ALU.mult))
                        for tau in range(1, 4):
                            k.wait("vector", "vq")
                            v_op(k, lambda v, j=j, tau=tau: v.scalar_tensor_tensor(
                                xc_t[:, :], xrep[j][:, 1 + tau:CH + 1 + tau],
                                c_convw[:, j * 4 + tau:j * 4 + tau + 1], xc_t[:, :],
                                ALU.mult, ALU.add))
                        k.wait("scalar", "vq")
                        s_op(k, lambda s, j=j: s.activation(
                            xs_sb[:, j * CH:(j + 1) * CH], xc_t[:, :], AF.Silu,
                            bias=c_convb[:, j:j + 1], scale=1.0), dep=True)
                        k.wait("vector", "sq")
                        k.wait("vector", "vq")
                        v_op(k, lambda v, j=j: v.tensor_mul(
                            dtx[:, j * CH:(j + 1) * CH],
                            dtf[:, j * CH:(j + 1) * CH],
                            xs_sb[:, j * CH:(j + 1) * CH]))
                    if t0 == 0:
                        k.wait("vector", "vq")
                        v_op(k, lambda v: v.memset(states[:, :], 0.0))
                    k.wait("tensor", "sq")
                    k.wait("tensor", "vq")
                    for j in range(4):
                        Bs = BsbA if j < 2 else BsbB
                        for n in range(NST):
                            q = n % 2
                            pb = ps[4 + q]
                            pc = ps[6 + q]
                            k.wait("tensor", "vq", mxw["pb"][q])
                            k.op("tensor", lambda t, j=j, n=n, pb=pb, Bs=Bs: t.matmul(
                                pb[:, :],
                                c_selB[32 * (j % 2):32 * (j % 2) + 32,
                                       n * 128:(n + 1) * 128],
                                Bs[32 * (j % 2):32 * (j % 2) + 32, :],
                                start=True, stop=True), inc="mm")
                            selb_mm = k.val("mm")
                            k.wait("scalar", "vq", mxw["dA"][q])
                            s_op(k, lambda sc_, j=j, n=n, q=q: sc_.activation(
                                dA[q][:, :], dtf[:, j * CH:(j + 1) * CH], AF.Exp,
                                bias=0.0,
                                scale=c_aneg[:, j * NST + n:j * NST + n + 1]))
                            dA_sq = k.val("sq")
                            k.wait("tensor", "sq", mxw["pc"][q])
                            k.op("tensor", lambda t, j=j, n=n, pc=pc: t.matmul(
                                pc[:, :], c_selC[:, n * 128:(n + 1) * 128],
                                Csb[j][:, :], start=True, stop=True), inc="mm")
                            selc_mm = k.val("mm")
                            k.wait("scalar", "mm", selc_mm)
                            k.wait("scalar", "gq", mxw["crep"][q])
                            s_op(k, lambda sc_, q=q, pc=pc: sc_.copy(
                                crepb[q][:, :], pc[:, :]))
                            crep_sq = k.val("sq")
                            mxw["pc"][q] = crep_sq
                            if n > 0:
                                qp = (n - 1) % 2
                                k.wait("vector", "gq", mxw["prodv"][qp])
                                if n - 1 < 2:
                                    v_op(k, lambda v, qp=qp: v.tensor_copy(
                                        yac2[qp][:, :], prodb[qp][:, :]))
                                else:
                                    k.wait("vector", "vq", mxw["acc"][qp])
                                    v_op(k, lambda v, qp=qp: v.tensor_add(
                                        yac2[qp][:, :], yac2[qp][:, :],
                                        prodb[qp][:, :]))
                                mxw["acc"][qp] = k.val("vq")
                                mxw["prodb"][qp] = k.val("vq")
                            k.wait("vector", "mm", selb_mm)
                            k.wait("vector", "vq", mxw["dbx"][q])
                            v_op(k, lambda v, j=j, q=q, pb=pb: v.tensor_mul(
                                dbx[q][:, :], dtx[:, j * CH:(j + 1) * CH], pb[:, :]))
                            mxw["pb"][q] = k.val("vq")
                            k.wait("vector", "sq", dA_sq)
                            k.wait("vector", "gq", mxw["hb"][q])
                            k.wait("vector", "sq", mxw["state"][j * NST + n])
                            k.wait("vector", "vq")
                            v_op(k, lambda v, j=j, n=n, q=q: v.tensor_tensor_scan(
                                hbuf[q][:, :], dA[q][:, :], dbx[q][:, :],
                                states[:, j * NST + n:j * NST + n + 1],
                                ALU.mult, ALU.add))
                            scan_vq = k.val("vq")
                            mxw["dA"][q] = scan_vq
                            mxw["dbx"][q] = scan_vq
                            k.wait("scalar", "vq", scan_vq)
                            s_op(k, lambda sc_, j=j, n=n, q=q: sc_.copy(
                                states[:, j * NST + n:j * NST + n + 1],
                                hbuf[q][:, CH - 1:CH]))
                            mxw["state"][j * NST + n] = k.val("sq")
                            k.wait("gpsimd", "vq", scan_vq)
                            k.wait("gpsimd", "sq", crep_sq)
                            k.wait("gpsimd", "vq", mxw["prodb"][q])
                            k.op("gpsimd", lambda g, q=q: g.tensor_mul(
                                prodb[q][:, :], hbuf[q][:, :], crepb[q][:, :]),
                                inc="gq")
                            gq_v = k.val("gq")
                            mxw["prodv"][q] = gq_v
                            mxw["hb"][q] = gq_v
                            mxw["crep"][q] = gq_v
                        qp = 1
                        k.wait("vector", "gq", mxw["prodv"][qp])
                        k.wait("vector", "vq", mxw["acc"][qp])
                        v_op(k, lambda v, qp=qp: v.tensor_add(
                            yac2[qp][:, :], yac2[qp][:, :], prodb[qp][:, :]))
                        mxw["acc"][qp] = k.val("vq")
                        mxw["prodb"][qp] = k.val("vq")
                        k.wait("vector", "vq")
                        v_op(k, lambda v: v.tensor_add(yac[:, :], yac2[0][:, :],
                                                       yac2[1][:, :]))
                        k.wait("vector", "vq")
                        v_op(k, lambda v, j=j: v.scalar_tensor_tensor(
                            yac[:, :], xs_sb[:, j * CH:(j + 1) * CH],
                            c_d128[:, j:j + 1], yac[:, :], ALU.mult, ALU.add))
                        k.wait("vector", "ev")
                        k.wait("vector", "vq")
                        k.wait("vector", "st", mxw["ygst"][j % 2])
                        v_op(k, lambda v, j=j: v.tensor_mul(
                            ygb[j % 2][:, :], yac[:, :],
                            siluz[:, j * CH:(j + 1) * CH]))
                        yv = k.val("vq")
                        k.wait("sync", "vq", yv)
                        k.op("sync", lambda e, j=j, bb=bb, t0=t0: e.dma_start(
                            g_yg_in[bb * DC + j * 128: bb * DC + (j + 1) * 128,
                                    t0:t0 + CH], ygb[j % 2][:, :]), inc="st", n=16)
                        mxw["ygst"][j % 2] = k.val("st")
                    if ch % 8 == 7:
                        k.flush()
                k.flush()

        def outproj_phase():
            pes = ExitStack()
            with pes:
                yrhs = sb("op_yrhs", 128, 32 * NSUB, BF16, pes)
                wop = [sb(f"op_w{i}", 128, 32 * 128, BF16, pes) for i in range(2)]
                h1 = sb("op_h1", 128, 16 * NSUB, F32, pes)
                m2 = sb("op_m2", 128, 16 * NSUB, BF16, pes)
                s12t = sb("op_s12t", 128, NSUB, F32, pes)
                xsq = sb("op_xsq", 128, NSUB, F32, pes)
                rms_ = sb("op_rms", 1, NSUB, F32, pes)
                rrs = sb("op_rrs", 1, NSUB, F32, pes)
                u_t = sb("op_ut", 128, NSUB, F32, pes)
                x1_t = sb("op_x1t", 128, NSUB, F32, pes)

                k.wait("gpsimd", "st")
                k.op("gpsimd", lambda e: e.collective_compute(
                    "AllGather", ALU.bypass, replica_groups=[list(range(NC))],
                    ins=[g_yg_in.opt()], outs=[g_yg_out.opt()]), inc="cc")
                k.wait("sync", "cc")
                k.S("ld")[1] += 16

                def sw_copy(e):
                    for arm in nc.switch_core_id(engines=[nc.sync], n=128):
                        c = arm.logical if hasattr(arm, "logical") else int(arm)
                        c = c % NC
                        bbq, ttq = c // 4, (c % 4) * TOKC
                        srcap = bass.AP(
                            g_yg_out.tensor, (bbq * DC) * T + ttq,
                            [[B * DC * T, NC], [T, DC], [1, TOKC]])
                        dstap = s_ymy.rearrange("(r d) t -> r d t", d=DC)
                        e.dma_start(dstap, srcap).then_inc(k.S("ld")[0], 16)
                k.raw("sync", sw_copy)
                swv = k.val("ld")
                k.wait("sync", "ld", swv)

                op_slot_mm = {0: 0, 1: 0}
                for ncs in range(TOKC // NSUB):
                    tok0 = ncs * NSUB
                    rv = None
                    k.wait("sync", "mm")
                    for kb_ in range(32):
                        rv = k.op("sync", lambda e, kb_=kb_, tok0=tok0: e.dma_start(
                            yrhs[:, kb_ * NSUB:(kb_ + 1) * NSUB],
                            s_ymy[kb_ * 128:(kb_ + 1) * 128, tok0:tok0 + NSUB]),
                            inc="ld", n=16)
                    k.wait("tensor", "ld", rv)
                    for mo in range(D // 128):
                        slot = mo % 2
                        k.wait("gpsimd", "mm", op_slot_mm[slot])
                        wv = k.op("gpsimd", lambda e, mo=mo, slot=slot: e.dma_start(
                            wop[slot][:, :].rearrange("p (kb m) -> p kb m", m=128),
                            p_opw[:, mo * 128:(mo + 1) * 128]
                            .rearrange("(kb p) m -> p kb m", p=128)),
                            inc="gld", n=16)
                        k.wait("tensor", "gld", wv)
                        pb = ps[mo % 2]
                        if mo >= 2:
                            k.wait("tensor", "vq")
                        for kb_ in range(32):
                            k.op("tensor", lambda t, kb_=kb_, slot=slot, pb=pb:
                                 t.matmul(pb[:, :],
                                          wop[slot][:, kb_ * 128:(kb_ + 1) * 128],
                                          yrhs[:, kb_ * NSUB:(kb_ + 1) * NSUB],
                                          start=(kb_ == 0), stop=(kb_ == 31)),
                                 inc="mm" if kb_ == 31 else None)
                        op_slot_mm[slot] = k.val("mm")
                        k.wait("sync", "vq")
                        uv = k.op("sync", lambda e, mo=mo, tok0=tok0: e.dma_start(
                            u_t[:, :], s_u[mo * 128:(mo + 1) * 128, tok0:tok0 + NSUB]),
                            inc="ld", n=16)
                        xv = k.op("sync", lambda e, mo=mo, tok0=tok0: e.dma_start(
                            x1_t[:, :], s_x1[mo * 128:(mo + 1) * 128,
                                             tok0:tok0 + NSUB]), inc="ld", n=16)
                        k.wait("vector", "mm")
                        k.wait("vector", "ld", xv)
                        v_op(k, lambda v, mo=mo, pb=pb: v.tensor_add(
                            h1[:, mo * NSUB:(mo + 1) * NSUB], pb[:, :], u_t[:, :]),
                            dep=True)
                        k.wait("vector", "vq")
                        k.wait("vector", "st")
                        v_op(k, lambda v, mo=mo: v.tensor_add(
                            s12t[:, :], h1[:, mo * NSUB:(mo + 1) * NSUB], x1_t[:, :]))
                        sv = k.val("vq")
                        k.wait("sync", "vq", sv)
                        k.op("sync", lambda e, mo=mo, tok0=tok0: e.dma_start(
                            s_s12[mo * 128:(mo + 1) * 128, tok0:tok0 + NSUB],
                            s12t[:, :]), inc="st", n=16)
                    emit_rms_scale(k, nc, ps, h1, m2, xsq, rms_, rrs, c_ones_col,
                                   c_ones_row, c_ln2, D)
                    m2v = k.val("vq")
                    k.wait("sync", "vq", m2v)
                    for kk in range(16):
                        k.op("sync", lambda e, kk=kk, tok0=tok0: e.dma_start(
                            s_m2[kk * 128:(kk + 1) * 128, tok0:tok0 + NSUB],
                            m2[:, kk * NSUB:(kk + 1) * NSUB]), inc="st", n=16)
                    k.wait("vector", "st")
                k.flush()

        def mlp_phase():
            pes = ExitStack()
            with pes:
                m2 = sb("ml_m2", 128, 16 * NSUB, BF16, pes)
                wg = [sb(f"ml_wg{i}", 128, 16 * 128, BF16, pes) for i in range(2)]
                wu = [sb(f"ml_wu{i}", 128, 16 * 128, BF16, pes) for i in range(2)]
                wd = [sb(f"ml_wd{i}", 128, 64 * 128, BF16, pes) for i in range(2)]
                i_sb = sb("ml_i", 128, 64 * NSUB, BF16, pes)
                sg = sb("ml_sg", 128, NSUB, F32, pes)
                s12t = sb("ml_s12t", 128, NSUB, F32, pes)
                x2t = [sb(f"ml_x2t{i}", 128, NSUB, F32, pes) for i in range(2)]

                gu_slot_mm = {0: 0, 1: 0}
                wd_slot_mm = {0: 0, 1: 0}
                for ncs in range(TOKC // NSUB):
                    tok0 = ncs * NSUB
                    mv = None
                    k.wait("sync", "vq")
                    k.wait("sync", "mm")
                    for kk in range(16):
                        mv = k.op("sync", lambda e, kk=kk, tok0=tok0: e.dma_start(
                            m2[:, kk * NSUB:(kk + 1) * NSUB],
                            s_m2[kk * 128:(kk + 1) * 128, tok0:tok0 + NSUB]),
                            inc="ld", n=16)
                    k.wait("tensor", "ld", mv)
                    k.wait("vector", "mm")
                    for mi in range(INTER // 128):
                        slot = mi % 2
                        k.wait("gpsimd", "mm", gu_slot_mm[slot])
                        gv = k.op("gpsimd", lambda e, mi=mi, slot=slot: e.dma_start(
                            wg[slot][:, :].rearrange("p (kb m) -> p kb m", m=128),
                            p_gate[:, mi * 128:(mi + 1) * 128]
                            .rearrange("(kb p) m -> p kb m", p=128)),
                            inc="gld", n=16)
                        uv2 = k.op("gpsimd", lambda e, mi=mi, slot=slot: e.dma_start(
                            wu[slot][:, :].rearrange("p (kb m) -> p kb m", m=128),
                            p_up[:, mi * 128:(mi + 1) * 128]
                            .rearrange("(kb p) m -> p kb m", p=128)),
                            inc="gld", n=16)
                        k.wait("tensor", "gld", uv2)
                        pg = ps[mi % 2]
                        pu = ps[2 + mi % 2]
                        if mi >= 2:
                            k.wait("tensor", "vq")
                        for kk in range(16):
                            k.op("tensor", lambda t, kk=kk, slot=slot, pg=pg: t.matmul(
                                pg[:, :], wg[slot][:, kk * 128:(kk + 1) * 128],
                                m2[:, kk * NSUB:(kk + 1) * NSUB],
                                start=(kk == 0), stop=(kk == 15)),
                                inc="mm" if kk == 15 else None)
                        for kk in range(16):
                            k.op("tensor", lambda t, kk=kk, slot=slot, pu=pu: t.matmul(
                                pu[:, :], wu[slot][:, kk * 128:(kk + 1) * 128],
                                m2[:, kk * NSUB:(kk + 1) * NSUB],
                                start=(kk == 0), stop=(kk == 15)),
                                inc="mm" if kk == 15 else None)
                        gu_slot_mm[slot] = k.val("mm")
                        k.wait("scalar", "mm")
                        k.wait("scalar", "vq")
                        s_op(k, lambda s, pg=pg: s.activation(sg[:, :], pg[:, :],
                                                              AF.Silu), dep=True)
                        k.wait("vector", "sq")
                        v_op(k, lambda v, mi=mi, pu=pu: v.tensor_mul(
                            i_sb[:, mi * NSUB:(mi + 1) * NSUB], sg[:, :], pu[:, :]))
                    iv = k.val("vq")
                    k.wait("tensor", "vq", iv)
                    for mo2 in range(D // 128):
                        slot = mo2 % 2
                        k.wait("gpsimd", "mm", wd_slot_mm[slot])
                        dv = k.op("gpsimd", lambda e, mo2=mo2, slot=slot: e.dma_start(
                            wd[slot][:, :].rearrange("p (kb m) -> p kb m", m=128),
                            p_down[:, mo2 * 128:(mo2 + 1) * 128]
                            .rearrange("(kb p) m -> p kb m", p=128)),
                            inc="gld", n=16)
                        k.wait("tensor", "gld", dv)
                        pb = ps[4 + mo2 % 2]
                        if mo2 >= 2:
                            k.wait("tensor", "vq")
                        for mi in range(64):
                            k.op("tensor", lambda t, mi=mi, slot=slot, pb=pb: t.matmul(
                                pb[:, :], wd[slot][:, mi * 128:(mi + 1) * 128],
                                i_sb[:, mi * NSUB:(mi + 1) * NSUB],
                                start=(mi == 0), stop=(mi == 63)),
                                inc="mm" if mi == 63 else None)
                        wd_slot_mm[slot] = k.val("mm")
                        k.wait("sync", "vq")
                        sv12 = k.op("sync", lambda e, mo2=mo2, tok0=tok0: e.dma_start(
                            s12t[:, :], s_s12[mo2 * 128:(mo2 + 1) * 128,
                                              tok0:tok0 + NSUB]), inc="ld", n=16)
                        k.wait("vector", "mm")
                        k.wait("vector", "ld", sv12)
                        k.wait("vector", "st")
                        v_op(k, lambda v, mo2=mo2, pb=pb: v.tensor_add(
                            x2t[mo2 % 2][:, :], s12t[:, :], pb[:, :]), dep=True)
                        xv2 = k.val("vq")
                        k.wait("sync", "vq", xv2)
                        k.op("sync", lambda e, mo2=mo2, tok0=tok0: e.dma_start(
                            s_x2[mo2 * 128:(mo2 + 1) * 128, tok0:tok0 + NSUB],
                            x2t[mo2 % 2][:, :]), inc="st", n=16)
                k.flush()

        phase0()
        if dbg_sb is not None:
            k.barrier(exclude=("pc",))
            k.op("sync", lambda e: e.dma_start(dbg_sb[0:128, 0:8], c_cc[:, :]),
                 inc="st", n=16)
            k.op("sync", lambda e: e.dma_start(dbg_sb[0:8, 8:1032], c_kneg[:, :]),
                 inc="st", n=16)
            k.op("sync", lambda e: e.dma_start(dbg_sb[0:128, 1032:1040], c_rbb[:, :]),
                 inc="st", n=16)
            k.op("sync", lambda e: e.dma_start(dbg_sb[0:128, 1040:1056], c_ucbh[:, :]),
                 inc="st", n=16)
            k.op("sync", lambda e: e.dma_start(dbg_sb[0:128, 1056:1120],
                                               c_aneg[:, :]), inc="st", n=16)
        k.barrier(exclude=("pc",))
        ffn_phase(x0, call=1)
        if stop_after != "p1":
            k.barrier()
            mixer_phase()
            if stop_after != "p3":
                k.barrier()
                outproj_phase()
                k.barrier()
                mlp_phase()
                if stop_after != "p5":
                    k.barrier()
                    ffn_phase(s_x2, call=2)
        for dst, srct in dbg_outs:
            k.barrier()
            rows = srct.shape[0]
            r = 0
            while r < rows:
                rr = min(2048, rows - r)
                k.op("sync", lambda e, d=dst, s=srct, r=r, rr=rr: e.dma_start(
                    d[r:r + rr, :], s[r:r + rr, :]), inc="st", n=16)
                r += rr
        k.wait("sync", "st")
        k.wait("sync", "ld")
        k.wait("gpsimd", "gld")
        k.wait("gpsimd", "pc")
        k.flush()

    return nc


def _host_prep(inputs):
    gi = lambda name: np.asarray(inputs[name])
    x = gi("L0_attn_hidden_states")

    base = {}
    base["ffn_down_w"] = np.ascontiguousarray(gi("L0_ffn_down_w"))
    base["ffn_up_w"] = np.ascontiguousarray(gi("L0_ffn_up_w"))
    base["ffn_router_w"] = np.ascontiguousarray(gi("L0_ffn_router_w"))
    base["ffn_shared_w1"] = np.ascontiguousarray(gi("L0_ffn_shared_w1"))
    base["ffn_shared_w2"] = np.ascontiguousarray(gi("L0_ffn_shared_w2"))
    base["ffn_core_w"] = np.ascontiguousarray(gi("L0_ffn_core_w"))
    for i in range(EXN):
        base[f"e{i}_w1"] = np.ascontiguousarray(gi(f"L0_ffn_e{i}_w1"))
        base[f"e{i}_w2"] = np.ascontiguousarray(gi(f"L0_ffn_e{i}_w2"))

    def col128(vec):
        return np.ascontiguousarray(np.asarray(vec).reshape(-1, 128).T)

    base["down_b128"] = col128(gi("L0_ffn_down_b"))
    for i in range(EXN):
        base[f"b1c{i}"] = col128(gi(f"L0_ffn_e{i}_b1"))
    b2all = np.zeros((128, (LF // 128) * EXN), np.float32)
    for e in range(EXN):
        c = col128(gi(f"L0_ffn_e{e}_b2"))
        for m in range(LF // 128):
            b2all[:, m * EXN + e] = c[:, m]
    base["b2all"] = b2all
    base["sb1_128"] = col128(gi("L0_ffn_shared_b1"))
    base["sb2_128"] = col128(gi("L0_ffn_shared_b2"))
    base["upb_128"] = col128(gi("L0_ffn_up_b"))
    base["coreb_128"] = col128(gi("L0_ffn_core_b"))
    base["rb_row"] = np.ascontiguousarray(
        gi("L0_ffn_router_b").reshape(1, EXN).astype(np.float32))
    cost = np.array([2 * HID[i] * LF for i in range(EXN)], np.float32)
    base["cost_row"] = (COST_LAMBDA * cost).reshape(1, EXN).astype(np.float32)

    base["gatewT"] = np.ascontiguousarray(gi("L0_attn_gate_w").T)
    base["upwT"] = np.ascontiguousarray(gi("L0_attn_up_w").T)
    base["downwT"] = np.ascontiguousarray(gi("L0_attn_down_w").T)
    base["opwT"] = np.ascontiguousarray(gi("L0_attn_out_proj_w").T)
    base["ln1_128"] = col128(gi("L0_attn_ln1_w"))
    base["ln2_128"] = col128(gi("L0_attn_ln2_w"))

    base["ones_col"] = np.ones((128, 1), np.float32)
    base["ones_row"] = np.ones((1, 128), np.float32)
    base["ident"] = np.eye(128, dtype=np.float32)
    oh8 = np.zeros((EXN, EXN * 128), np.float32)
    for e in range(EXN):
        oh8[e, e * 128:(e + 1) * 128] = 1.0
    base["oh8"] = oh8
    selx = np.zeros((32, 128), np.float32)
    for p in range(128):
        selx[(p // 64) * 16 + (p % 16), p] = 1.0
    base["selx"] = np.concatenate([selx, selx], axis=0)
    selB = np.zeros((32, NST * 128), np.float32)
    selC = np.zeros((128, NST * 128), np.float32)
    for n in range(NST):
        for p in range(128):
            selB[(p // 64) * 16 + n, n * 128 + p] = 1.0
            selC[(p // 16) * 16 + n, n * 128 + p] = 1.0
    base["selB_all"] = np.concatenate([selB, selB], axis=0)
    base["selC_all"] = selC

    ipw = gi("L0_attn_in_proj_w")
    dtw = gi("L0_attn_dt_proj_w")
    alog = gi("L0_attn_A_log")
    dpar = gi("L0_attn_D_param")
    cw = gi("L0_attn_conv_w")
    cb = gi("L0_attn_conv_b")
    dtb = gi("L0_attn_dt_proj_b")

    in_maps = []
    for c in range(NC):
        m = dict(base)
        bb, tt = c // 4, (c % 4) * TOKC
        m["x0"] = np.ascontiguousarray(x[bb, tt:tt + TOKC, :].T)
        d0 = c * DC
        rows = np.concatenate([
            np.arange(d0, d0 + DC),
            4096 + np.arange(c * 128, (c + 1) * 128),
            5120 + np.arange(c * 128, (c + 1) * 128),
            6144 + np.arange(d0, d0 + DC),
            np.arange(10240, 10368),
        ])
        m["ipwT"] = np.ascontiguousarray(ipw[rows, :].T)
        m["dtwT"] = np.ascontiguousarray(dtw[d0:d0 + DC, :].T)

        def loc128(vec2):
            v = vec2.reshape(DC // 128, 128, -1)
            return np.ascontiguousarray(v.transpose(1, 0, 2).reshape(128, -1))
        m["alog128"] = loc128(alog[d0:d0 + DC, :])
        m["d128"] = loc128(dpar[d0:d0 + DC].reshape(-1, 1))
        m["convw128"] = loc128(cw[d0:d0 + DC, 0, :])
        m["convb128"] = loc128(cb[d0:d0 + DC].reshape(-1, 1))
        m["dtb128"] = loc128(dtb[d0:d0 + DC].reshape(-1, 1))
        in_maps.append(m)
    return in_maps


def kernel(**inputs):
    in_maps = _host_prep(inputs)
    nc = build_program()
    res = run_bass_kernel_spmd(nc, in_maps, core_ids=list(range(NC)))
    out = np.zeros((B, T, D), np.float32)
    for c in range(NC):
        bb, tt = c // 4, (c % 4) * TOKC
        out[bb, tt:tt + TOKC, :] = np.asarray(res.results[c]["xout"]).T
    return out


# version 03
# speedup vs baseline: 1.1610x; 1.1151x over previous
import os
os.environ.setdefault("NEURON_SCRATCHPAD_PAGE_SIZE", "2048")

import numpy as np
import ml_dtypes
from contextlib import ExitStack

import concourse.bass as bass
import concourse.mybir as mybir
from concourse.bass_utils import run_bass_kernel_spmd

F32 = mybir.dt.float32
BF16 = mybir.dt.bfloat16
AF = mybir.ActivationFunctionType
ALU = mybir.AluOpType
AX = mybir.AxisListType

D = 2048
DI = 4096
DXB = 1024
NST = 16
DCONV = 4
DTR = 128
INTER = 8192
LF = 1024
EXN = 8
HID = [1024, 2048, 3072, 4096, 1024, 2048, 3072, 4096]
COST_LAMBDA = 1e-7
EPS = 1e-5

B = 2
T = 16384
NC = 8
TOKC = 4096
DC = 512
NSUB = 512
SUP = 1024
CH = 512

ENGS = ("sync", "gpsimd", "tensor", "scalar", "vector")
SEMS = ("ld", "gld", "st", "mm", "ev", "sq", "vq", "gq", "cc", "pc")


class KB:
    def __init__(self, nc, block, es):
        self.nc, self.block, self.es = nc, block, es
        self.q = {e: [] for e in ENGS}
        self.sems = {}

    def S(self, name):
        if name not in self.sems:
            h = self.es.enter_context(self.nc.semaphore(name))
            self.sems[name] = [h, 0]
        return self.sems[name]

    def val(self, name):
        return self.S(name)[1]

    def op(self, eng, fn, inc=None, n=1):
        if inc is not None:
            s = self.S(inc)
            s[1] += n

            def run(h, fn=fn, sh=s[0], n=n):
                fn(h).then_inc(sh, n)
        else:
            def run(h, fn=fn):
                fn(h)
        self.q[eng].append(run)
        return self.val(inc) if inc else None

    def raw(self, eng, fn):
        self.q[eng].append(fn)

    def wait(self, eng, name, val=None):
        s = self.S(name)
        v = s[1] if val is None else val
        self.q[eng].append(lambda h, sh=s[0], v=v: h.wait_ge(sh, v))

    def barrier(self, exclude=()):
        for e in ENGS:
            for snm in SEMS:
                if snm in self.sems and snm not in exclude:
                    self.wait(e, snm)

    def flush(self):
        for e in ENGS:
            fns = self.q[e]
            if not fns:
                continue
            self.q[e] = []

            def runner(h, fns=fns):
                for f in fns:
                    f(h)
            getattr(self.block, e)(runner)


def scope_guard(k, engs=("gpsimd", "sync")):
    for e in engs:
        for snm in ("mm", "vq", "sq", "ev", "st"):
            if snm in k.sems:
                k.wait(e, snm)


def v_op(k, fn, dep=False):
    if dep:
        k.wait("vector", "vq")
    k.op("vector", fn, inc="vq")


def s_op(k, fn, dep=False):
    if dep:
        k.wait("scalar", "sq")
    k.op("scalar", fn, inc="sq")


def build_program(stop_after=None, debug=()):
    nc = bass.Bass("TRN2", target_bir_lowering=False)

    def inp(name, shape):
        return nc.dram_tensor(name, list(shape), F32, kind="ExternalInput").ap()

    x0 = inp("x0", (D, TOKC))
    ffn_down_w = inp("ffn_down_w", (D, LF))
    ffn_up_w = inp("ffn_up_w", (LF, D))
    ffn_router_w = inp("ffn_router_w", (D, EXN))
    ffn_shared_w1 = inp("ffn_shared_w1", (LF, LF))
    ffn_shared_w2 = inp("ffn_shared_w2", (LF, LF))
    ffn_core_w = inp("ffn_core_w", (D, D))
    e_w1 = [inp(f"e{i}_w1", (LF, HID[i])) for i in range(EXN)]
    e_w2 = [inp(f"e{i}_w2", (HID[i], LF)) for i in range(EXN)]
    down_b128 = inp("down_b128", (128, LF // 128))
    b1c = [inp(f"b1c{i}", (128, HID[i] // 128)) for i in range(EXN)]
    b2all = inp("b2all", (128, (LF // 128) * EXN))
    sb1_128 = inp("sb1_128", (128, LF // 128))
    sb2_128 = inp("sb2_128", (128, LF // 128))
    upb_128 = inp("upb_128", (128, D // 128))
    coreb_128 = inp("coreb_128", (128, D // 128))
    rb_row = inp("rb_row", (1, EXN))
    cost_row = inp("cost_row", (1, EXN))
    ipwT = inp("ipwT", (D, 1408))
    dtwT = inp("dtwT", (DTR, DC))
    opwT = inp("opwT", (DI, D))
    gatewT = inp("gatewT", (D, INTER))
    upwT = inp("upwT", (D, INTER))
    downwT = inp("downwT", (INTER, D))
    alog128 = inp("alog128", (128, (DC // 128) * NST))
    d128 = inp("d128", (128, DC // 128))
    convw128 = inp("convw128", (128, (DC // 128) * DCONV))
    convb128 = inp("convb128", (128, DC // 128))
    dtb128 = inp("dtb128", (128, DC // 128))
    ln1_128 = inp("ln1_128", (128, D // 128))
    ln2_128 = inp("ln2_128", (128, D // 128))
    ones_col = inp("ones_col", (128, 1))
    ones_row = inp("ones_row", (1, 128))
    ident = inp("ident", (128, 128))
    oh8 = inp("oh8", (EXN, EXN * 128))
    selx = inp("selx", (64, 128))
    selB_all = inp("selB_all", (64, NST * 128))
    selC_all = inp("selC_all", (128, NST * 128))

    def scratch(name, shape, dt=F32):
        return nc.dram_tensor(name, list(shape), dt, kind="Internal").ap()

    g_min_in = scratch("g_min_in", (D, TOKC), BF16)
    g_min_out = nc.dram_tensor("g_min_out", [NC * D, TOKC], BF16, kind="Internal",
                               addr_space="Shared").ap()
    g_yg_in = scratch("g_yg_in", (B * DC, T), BF16)
    g_yg_out = nc.dram_tensor("g_yg_out", [NC * B * DC, T], BF16, kind="Internal",
                              addr_space="Shared").ap()
    s_ymy = scratch("s_ymy", (DI, TOKC), BF16)
    s_x1 = scratch("s_x1", (D, TOKC))
    s_u = scratch("s_u", (D, TOKC))
    s_xlnbf = scratch("s_xlnbf", (D, TOKC), BF16)
    s_cx = scratch("s_cx", (D, TOKC), BF16)
    s_x2 = scratch("s_x2", (D, TOKC))
    s_m2 = scratch("s_m2", (D, TOKC), BF16)
    s_dbgx = scratch("s_dbgx", (D, NSUB))
    s_s12 = scratch("s_s12", (D, TOKC))
    p_ew1 = [scratch(f"p_e{i}w1", (LF, HID[i]), BF16) for i in range(EXN)]
    p_ew2 = [scratch(f"p_e{i}w2", (HID[i], LF), BF16) for i in range(EXN)]
    p_fup = scratch("p_fup", (LF, D), BF16)
    p_fcore = scratch("p_fcore", (D, D), BF16)
    p_opw = scratch("p_opw", (DI, D), BF16)
    p_gate = scratch("p_gate", (D, INTER), BF16)
    p_up = scratch("p_up", (D, INTER), BF16)
    p_down = scratch("p_down", (INTER, D), BF16)
    xout = nc.dram_tensor("xout", [D, TOKC], F32, kind="ExternalOutput").ap()
    dbg_sb = (nc.dram_tensor("dbg_sb", [128, 2100], F32, kind="ExternalOutput").ap()
              if "p0" in debug else None)

    dbg_outs = []
    srcs = {"x1": s_x1, "u": s_u, "minin": g_min_in, "minout": g_min_out,
            "yg": g_yg_in, "ygout": g_yg_out, "ymy": s_ymy, "x2": s_x2,
            "xlnbf": s_xlnbf, "cx": s_cx, "dbgx": s_dbgx, "m2": s_m2, "s12": s_s12}
    for nm in debug:
        if nm == "p0":
            continue
        srct = srcs[nm]
        dbg_outs.append((nc.dram_tensor("dbg_" + nm, list(srct.shape),
                                        srct.dtype, kind="ExternalOutput").ap(), srct))

    _dbgx_done = []
    es = ExitStack()
    with es:
        block = es.enter_context(nc.Block())
        k = KB(nc, block, es)
        ps = [es.enter_context(nc.psum_tensor(f"ps{i}", [128, 512], F32))
              for i in range(8)]

        def sb(name, p, f, dt=F32, st=es):
            return st.enter_context(nc.sbuf_tensor(name, [p, f], dt))

        c_ones_col = sb("c_ones_col", 128, 1)
        c_ones_row = sb("c_ones_row", 1, 128)
        c_ident = sb("c_ident", 128, 128)
        c_oh8 = sb("c_oh8", EXN, EXN * 128)
        c_selx = sb("c_selx", 64, 128)
        c_selB = sb("c_selB", 64, NST * 128, BF16)
        c_selC = sb("c_selC", 128, NST * 128, BF16)
        c_downb = sb("c_downb", 128, LF // 128)
        c_b1 = [sb(f"c_b1_{i}", 128, HID[i] // 128) for i in range(EXN)]
        c_gb1 = [sb(f"c_gb1_{i}", 128, HID[i] // 128, BF16) for i in range(EXN)]
        c_sb1 = sb("c_sb1", 128, LF // 128)
        c_rbb = sb("c_rbb", 128, EXN)
        c_cc = sb("c_cc", 128, LF // 128)
        c_kneg = sb("c_kneg", EXN, LF)
        c_ucbh = sb("c_ucbh", 128, D // 128)
        c_aneg = sb("c_aneg", 128, (DC // 128) * NST)
        c_d128 = sb("c_d128", 128, DC // 128)
        c_convw = sb("c_convw", 128, (DC // 128) * DCONV)
        c_convb = sb("c_convb", 128, DC // 128)
        c_dtb = sb("c_dtb", 128, DC // 128)
        c_ln1 = sb("c_ln1", 128, D // 128)
        c_ln2 = sb("c_ln2", 128, D // 128)
        c_eps = sb("c_eps", 128, 1)
        c_mhalf = sb("c_mhalf", 128, 1)

        def phase0():
            v_op(k, lambda v: v.memset(c_eps[:, :], EPS))
            v_op(k, lambda v: v.memset(c_mhalf[:, :], -0.5))
            def precast(dst, srcap, rows):
                step = max(1, (4 << 20) // max(1, int(np.prod(srcap.shape[1:]) * 4)))
                r = 0
                while r < rows:
                    rr = min(step, rows - r)
                    k.op("gpsimd", lambda e, d=dst, s=srcap, r=r, rr=rr:
                         e.dma_start(d[r:r + rr, :], s[r:r + rr, :]), inc="pc", n=16)
                    r += rr
            for i in range(EXN):
                precast(p_ew1[i], e_w1[i], LF)
                precast(p_ew2[i], e_w2[i], HID[i])
            precast(p_fup, ffn_up_w, LF)
            precast(p_fcore, ffn_core_w, D)
            precast(p_opw, opwT, DI)
            precast(p_gate, gatewT, D)
            precast(p_up, upwT, D)
            precast(p_down, downwT, INTER)

            loads = [
                (c_ones_col, ones_col), (c_ones_row, ones_row), (c_ident, ident),
                (c_selx, selx), (c_downb, down_b128), (c_sb1, sb1_128),
                (c_d128, d128), (c_convw, convw128), (c_convb, convb128),
                (c_dtb, dtb128), (c_ln1, ln1_128), (c_ln2, ln2_128),
            ]
            for dst, srcd in loads:
                k.op("sync", lambda e, d=dst, s=srcd: e.dma_start(d[:, :], s[:, :]),
                     inc="ld", n=16)
            for i in range(EXN):
                k.op("sync", lambda e, d=c_b1[i], s=b1c[i]: e.dma_start(d[:, :], s[:, :]),
                     inc="ld", n=16)
            for dst, srcd in [(c_selB, selB_all), (c_selC, selC_all), (c_oh8, oh8)]:
                if dst.dtype == BF16:
                    k.op("gpsimd", lambda e, d=dst, s=srcd: e.dma_start(d[:, :], s[:, :]),
                         inc="gld", n=16)
                else:
                    k.op("sync", lambda e, d=dst, s=srcd: e.dma_start(d[:, :], s[:, :]),
                         inc="ld", n=16)
            p0es = ExitStack()
            t_rb = sb("t_rb", 1, EXN, F32, p0es)
            t_cost = sb("t_cost", 1, EXN, F32, p0es)
            t_rbeff = sb("t_rbeff", 1, EXN, F32, p0es)
            t_upb = sb("t_upb", 128, D // 128, F32, p0es)
            t_coreb = sb("t_coreb", 128, D // 128, F32, p0es)
            t_ucs = sb("t_ucs", 128, D // 128, F32, p0es)
            t_b2all = sb("t_b2all", 128, (LF // 128) * EXN, F32, p0es)
            t_sb2 = sb("t_sb2", 128, LF // 128, F32, p0es)
            t_sb2t = sb("t_sb2t", 128, LF // 128, F32, p0es)
            t_ksum = sb("t_ksum", 1, LF, F32, p0es)
            t_krow = sb("t_krow", 1, LF, F32, p0es)
            t_alog = sb("t_alog", 128, (DC // 128) * NST, F32, p0es)
            t_aexp = sb("t_aexp", 128, (DC // 128) * NST, F32, p0es)
            for dst, srcd in [(t_rb, rb_row), (t_cost, cost_row), (t_upb, upb_128),
                              (t_coreb, coreb_128), (t_b2all, b2all),
                              (t_sb2, sb2_128), (t_alog, alog128)]:
                k.op("sync", lambda e, d=dst, s=srcd: e.dma_start(d[:, :], s[:, :]),
                     inc="ld", n=16)

            k.wait("vector", "ld")
            k.wait("vector", "gld")
            v_op(k, lambda v: v.tensor_sub(t_rbeff[:, :], t_rb[:, :], t_cost[:, :]))
            k.wait("tensor", "vq")
            k.op("tensor", lambda t: t.matmul(ps[0][:, 0:EXN], c_ones_row[:, :],
                                              t_rbeff[:, :], start=True, stop=True),
                 inc="mm")
            k.wait("vector", "mm")
            v_op(k, lambda v: v.tensor_copy(c_rbb[:, :], ps[0][:, 0:EXN]))
            v_op(k, lambda v: v.tensor_add(t_ucs[:, :], t_upb[:, :], t_coreb[:, :]))
            k.wait("vector", "vq")
            v_op(k, lambda v: v.tensor_scalar(c_ucbh[:, :], t_ucs[:, :], 0.5,
                                              None, op0=ALU.mult))
            k.wait("scalar", "ld")
            for i in range(EXN):
                s_op(k, lambda s, i=i: s.activation(c_gb1[i][:, :], c_b1[i][:, :],
                                                    AF.Gelu))
            s_op(k, lambda s: s.activation(t_aexp[:, :], t_alog[:, :], AF.Exp))
            s_op(k, lambda s: s.mul(c_aneg[:, :], t_aexp[:, :], -1.0), dep=True)
            v_op(k, lambda v: v.memset(t_ksum[:, :], 0.0))
            wbuf0 = sb("wbuf0", 128, LF, BF16, p0es)
            gb_done = k.val("sq")
            for ei in range(EXN):
                ke = HID[ei] // 128
                for mb in range(ke):
                    gv = k.op("gpsimd", lambda e, ei=ei, mb=mb: e.dma_start(
                        wbuf0[:, :], e_w2[ei][mb * 128:(mb + 1) * 128, :]),
                        inc="gld", n=16)
                    k.wait("tensor", "gld", gv)
                    k.wait("tensor", "sq")
                    last = (mb == ke - 1)
                    for half in range(2):
                        k.op("tensor", lambda t, ei=ei, mb=mb, half=half, ke=ke:
                             t.matmul(ps[half][0:1, :],
                                      c_gb1[ei][:, mb:mb + 1],
                                      wbuf0[:, half * 512:(half + 1) * 512],
                                      start=(mb == 0), stop=(mb == ke - 1)),
                             inc="mm" if (last and half == 1) else None)
                    if not last:
                        k.op("tensor", lambda t: t.matmul(
                            ps[2][0:1, 0:1], c_ones_row[0:1, 0:1],
                            c_ones_row[0:1, 0:1], start=True, stop=True), inc="mm")
                    k.wait("gpsimd", "mm")
                k.wait("scalar", "mm")
                k.wait("scalar", "ld")
                k.wait("scalar", "st")
                s_op(k, lambda s: s.activation(t_krow[0:1, 0:512], ps[0][0:1, :],
                                               AF.Copy, bias=0.0, scale=-1.0))
                s_op(k, lambda s: s.activation(t_krow[0:1, 512:1024], ps[1][0:1, :],
                                               AF.Copy, bias=0.0, scale=-1.0))
                k.wait("vector", "sq")
                v_op(k, lambda v: v.tensor_sub(t_ksum[:, :], t_ksum[:, :],
                                               t_krow[:, :]), dep=True)
                k.wait("sync", "sq")
                k.op("sync", lambda e, ei=ei: e.dma_start(
                    c_kneg[ei:ei + 1, :], t_krow[:, :]), inc="st", n=16)
            v_op(k, lambda v: v.tensor_reduce(
                c_cc[:, :].rearrange("p (m one) -> p m one", one=1),
                t_b2all[:, :].rearrange("p (m e) -> p m e", e=EXN),
                op=ALU.add, axis=AX.X), dep=True)
            v_op(k, lambda v: v.tensor_scalar(t_sb2t[:, :], t_sb2[:, :], 0.1,
                                              None, op0=ALU.mult), dep=True)
            k.wait("vector", "sq")
            v_op(k, lambda v: v.tensor_add(c_cc[:, :], c_cc[:, :], t_sb2t[:, :]),
                 dep=True)
            for mo in range(LF // 128):
                k.wait("tensor", "vq")
                k.op("tensor", lambda t, mo=mo: t.matmul(
                    ps[3][:, 0:1], t_ksum[0:1, mo * 128:(mo + 1) * 128],
                    c_ones_row[0:1, 0:1], start=True, stop=True), inc="mm")
                k.wait("vector", "mm")
                v_op(k, lambda v, mo=mo: v.tensor_add(
                    c_cc[:, mo:mo + 1], c_cc[:, mo:mo + 1], ps[3][:, 0:1]), dep=True)
            k.flush()
            p0es.close()

        def ffn_phase(src, call):
            out_t = s_x1 if call == 1 else xout
            pes = ExitStack()
            with pes:
                for sup in range(TOKC // SUP):
                    spes = ExitStack()
                    with spes:
                        accb = sb(f"f{call}s{sup}_acc", 128, 8 * SUP, F32, spes)
                        wfm = sb(f"f{call}s{sup}_wfm", EXN, SUP, F32, spes)
                        for _e in ("sync", "gpsimd", "tensor", "scalar", "vector"):
                            for _s in ("mm", "vq", "sq", "st"):
                                k.wait(_e, _s)
                        v_op(k, lambda v: v.memset(accb[:, :], 0.0))
                        for m in range(LF // 128):
                            v_op(k, lambda v, m=m: v.tensor_scalar(
                                accb[:, m * SUP:(m + 1) * SUP],
                                accb[:, m * SUP:(m + 1) * SUP],
                                c_cc[:, m:m + 1], None, op0=ALU.add), dep=True)
                        apes = ExitStack()
                        with apes:
                            gHs = sb(f"f{call}s{sup}_gH", 128, 8 * SUP, BF16, apes)
                            a2 = ExitStack()
                            with a2:
                                dwn_sb = sb(f"fA{call}{sup}_dwn", 128, 16 * LF, BF16, a2)
                                rtr_sb = sb(f"fA{call}{sup}_rtr", 128, 16 * EXN, BF16, a2)
                                k.wait("gpsimd", "mm")
                                k.op("gpsimd", lambda e: e.dma_start(
                                    dwn_sb[:, :].rearrange("p (kb m) -> p kb m", m=LF),
                                    ffn_down_w.rearrange("(kb p) m -> p kb m", p=128)),
                                    inc="gld", n=16)
                                k.op("gpsimd", lambda e: e.dma_start(
                                    rtr_sb[:, :].rearrange("p (kb m) -> p kb m", m=EXN),
                                    ffn_router_w.rearrange("(kb p) m -> p kb m", p=128)),
                                    inc="gld", n=16)
                                x0_sb = sb(f"fA{call}{sup}_x0", 128, 16 * NSUB, F32, a2)
                                xln = sb(f"fA{call}{sup}_xln", 128, 16 * NSUB, BF16, a2)
                                cxl = sb(f"fA{call}{sup}_cx", 128, 16 * NSUB, BF16, a2)
                                tmp = sb(f"fA{call}{sup}_tmp", 128, NSUB, F32, a2)
                                xsq = sb(f"fA{call}{sup}_xsq", 128, NSUB, F32, a2)
                                rmu = sb(f"fA{call}{sup}_rmu", 1, NSUB, F32, a2)
                                rms_ = sb(f"fA{call}{sup}_rms", 1, NSUB, F32, a2)
                                rm2 = sb(f"fA{call}{sup}_rm2", 1, NSUB, F32, a2)
                                rva = sb(f"fA{call}{sup}_rva", 1, NSUB, F32, a2)
                                rrs = sb(f"fA{call}{sup}_rrs", 1, NSUB, F32, a2)
                                rmr = sb(f"fA{call}{sup}_rmr", 1, NSUB, F32, a2)
                                rr = [sb(f"fA{call}{sup}_r{i}", 128,
                                         EXN if i > 1 else 1, F32, a2)
                                      for i in range(12)]
                                r_sc = [sb(f"fA{call}{sup}_s{i}", 128, 1, F32, a2)
                                        for i in range(6)]
                                k.barrier() if sup == 0 and call == 2 else None
                                for ncs in range(SUP // NSUB):
                                    tok0 = sup * SUP + ncs * NSUB
                                    emit_ln_stage(k, nc, ps, src, tok0, x0_sb, xln,
                                                  cxl, tmp, xsq, rmu, rms_, rm2, rva,
                                                  rrs, rmr, c_ones_col, c_ones_row,
                                                  s_xlnbf, s_cx, store_ln=True)
                                    k.wait("tensor", "gld")
                                    k.wait("tensor", "vq")
                                    for m in range(LF // 128):
                                        pb = ps[m % 2]
                                        if m >= 2:
                                            k.wait("tensor", "sq")
                                        for kk in range(16):
                                            k.op("tensor", lambda t, m=m, kk=kk, pb=pb:
                                                 t.matmul(pb[:, :],
                                                          dwn_sb[:, kk * LF + m * 128:
                                                                 kk * LF + m * 128 + 128],
                                                          xln[:, kk * NSUB:(kk + 1) * NSUB],
                                                          start=(kk == 0), stop=(kk == 15)),
                                                 inc="mm" if kk == 15 else None)
                                        k.wait("scalar", "mm")
                                        s_op(k, lambda s, m=m, pb=pb, ncs=ncs: s.activation(
                                            gHs[:, m * SUP + ncs * NSUB:
                                                m * SUP + ncs * NSUB + NSUB],
                                            pb[:, :], AF.Gelu,
                                            bias=c_downb[:, m:m + 1], scale=1.0))
                                    emit_router(k, nc, ps, xln, rtr_sb, c_rbb, c_ident,
                                                c_ones_col, wfm, ncs, rr, r_sc)
                            bpes = ExitStack()
                            with bpes:
                                G_sb = sb(f"fB{call}{sup}_G", 128, 64 * NSUB, BF16, bpes)
                                w1b = [sb(f"fB{call}{sup}_w1{i}", 128, 8 * 128, BF16,
                                          bpes) for i in range(2)]
                                w2b = [sb(f"fB{call}{sup}_w2{i}", 128, 32 * 128, BF16,
                                          bpes) for i in range(2)]
                                wbs = [sb(f"fB{call}{sup}_wb{i}", 128, NSUB, F32, bpes)
                                       for i in range(2)]
                                gtmp = sb(f"fB{call}{sup}_gt", 128, NSUB, F32, bpes)
                                k.wait("gpsimd", "pc")
                                scope_guard(k)
                                bank_sq = {0: k.val("sq"), 1: k.val("sq")}
                                w1_slot_mm = {0: 0, 1: 0}
                                w2_slot_mm = {0: 0, 1: 0}
                                for ei in range(EXN):
                                    ke = HID[ei] // 128
                                    k.wait("tensor", "vq")
                                    k.wait("tensor", "sq")
                                    for h2 in range(2):
                                        k.op("tensor", lambda t, ei=ei, h2=h2: t.matmul(
                                            ps[4 + h2][:, :],
                                            c_oh8[:, ei * 128:(ei + 1) * 128],
                                            wfm[:, h2 * NSUB:(h2 + 1) * NSUB],
                                            start=True, stop=True), inc="mm")
                                    k.wait("vector", "mm")
                                    for h2 in range(2):
                                        v_op(k, lambda v, h2=h2: v.tensor_copy(
                                            wbs[h2][:, :], ps[4 + h2][:, :]))
                                    wbv = k.val("vq")
                                    for mb in range(ke):
                                        slot = mb % 2
                                        k.wait("gpsimd", "mm", w1_slot_mm[slot])
                                        gv = k.op("gpsimd", lambda e, ei=ei, mb=mb,
                                                  slot=slot: e.dma_start(
                                                      w1b[slot][:, :].rearrange(
                                                          "p (kb m) -> p kb m", m=128),
                                                      p_ew1[ei][:, mb * 128:(mb + 1) * 128]
                                                      .rearrange("(kb p) m -> p kb m",
                                                                 p=128)),
                                                  inc="gld", n=16)
                                        k.wait("tensor", "gld", gv)
                                        for h2 in range(2):
                                            pb = ps[h2]
                                            k.wait("tensor", "sq", bank_sq[h2])
                                            for kk in range(8):
                                                k.op("tensor", lambda t, kk=kk, slot=slot,
                                                     pb=pb, h2=h2: t.matmul(
                                                         pb[:, :],
                                                         w1b[slot][:, kk * 128:(kk + 1) * 128],
                                                         gHs[:, kk * SUP + h2 * NSUB:
                                                             kk * SUP + h2 * NSUB + NSUB],
                                                         start=(kk == 0), stop=(kk == 7)),
                                                     inc="mm" if kk == 7 else None)
                                            k.wait("scalar", "mm")
                                            k.wait("scalar", "vq")
                                            s_op(k, lambda s, ei=ei, mb=mb, pb=pb:
                                                 s.activation(gtmp[:, :], pb[:, :],
                                                              AF.Gelu,
                                                              bias=c_b1[ei][:, mb:mb + 1],
                                                              scale=1.0), dep=True)
                                            bank_sq[h2] = k.val("sq")
                                            w1_slot_mm[slot] = k.val("mm")
                                            k.wait("vector", "sq")
                                            v_op(k, lambda v, mb=mb, h2=h2: v.tensor_mul(
                                                G_sb[:, (2 * mb + h2) * NSUB:
                                                     (2 * mb + h2 + 1) * NSUB],
                                                gtmp[:, :], wbs[h2][:, :]))
                                    gdone = k.val("vq")
                                    for mo in range(LF // 128):
                                        slot = mo % 2
                                        k.wait("gpsimd", "mm", w2_slot_mm[slot])
                                        wv2 = k.op("gpsimd", lambda e, ei=ei, mo=mo,
                                                   slot=slot, ke=ke: e.dma_start(
                                                       w2b[slot][:, 0:ke * 128].rearrange(
                                                           "p (kb m) -> p kb m", m=128),
                                                       p_ew2[ei][:, mo * 128:(mo + 1) * 128]
                                                       .rearrange("(kb p) m -> p kb m",
                                                                  p=128)),
                                                   inc="gld", n=16)
                                        k.wait("tensor", "gld", wv2)
                                        k.wait("tensor", "vq", gdone)
                                        if mo >= 1:
                                            k.wait("tensor", "vq")
                                        for h2 in range(2):
                                            pb = ps[2 + h2]
                                            for mb in range(ke):
                                                k.op("tensor", lambda t, mb=mb, slot=slot,
                                                     pb=pb, h2=h2, ke=ke: t.matmul(
                                                         pb[:, :],
                                                         w2b[slot][:, mb * 128:(mb + 1) * 128],
                                                         G_sb[:, (2 * mb + h2) * NSUB:
                                                              (2 * mb + h2 + 1) * NSUB],
                                                         start=(mb == 0), stop=(mb == ke - 1)),
                                                     inc="mm" if mb == ke - 1 else None)
                                        w2_slot_mm[slot] = k.val("mm")
                                        k.wait("vector", "mm")
                                        for h2 in range(2):
                                            v_op(k, lambda v, mo=mo, h2=h2: v.tensor_add(
                                                accb[:, mo * SUP + h2 * NSUB:
                                                     mo * SUP + h2 * NSUB + NSUB],
                                                accb[:, mo * SUP + h2 * NSUB:
                                                     mo * SUP + h2 * NSUB + NSUB],
                                                ps[2 + h2][:, :]), dep=(h2 == 0))
                            cpes = ExitStack()
                            with cpes:
                                s1b = sb(f"fC{call}{sup}_s1", 128, 8 * NSUB, BF16, cpes)
                                shb = [sb(f"fC{call}{sup}_sh{i}", 128, 8 * 128, BF16,
                                          cpes) for i in range(2)]
                                scope_guard(k)
                                sh_slot_mm = {0: 0, 1: 0}
                                for h2 in range(SUP // NSUB):
                                    for mo in range(LF // 128):
                                        slot = mo % 2
                                        k.wait("gpsimd", "mm", sh_slot_mm[slot])
                                        sv = k.op("gpsimd", lambda e, mo=mo, slot=slot:
                                                  e.dma_start(
                                                      shb[slot][:, :].rearrange(
                                                          "p (kb m) -> p kb m", m=128),
                                                      ffn_shared_w1[:, mo * 128:(mo + 1) * 128]
                                                      .rearrange("(kb p) m -> p kb m", p=128)),
                                                  inc="gld", n=16)
                                        k.wait("tensor", "gld", sv)
                                        pb = ps[mo % 2]
                                        k.wait("tensor", "sq")
                                        k.wait("tensor", "vq")
                                        for kk in range(8):
                                            k.op("tensor", lambda t, kk=kk, slot=slot,
                                                 pb=pb, h2=h2: t.matmul(
                                                     pb[:, :],
                                                     shb[slot][:, kk * 128:(kk + 1) * 128],
                                                     gHs[:, kk * SUP + h2 * NSUB:
                                                         kk * SUP + h2 * NSUB + NSUB],
                                                     start=(kk == 0), stop=(kk == 7)),
                                                 inc="mm" if kk == 7 else None)
                                        sh_slot_mm[slot] = k.val("mm")
                                        k.wait("scalar", "mm")
                                        s_op(k, lambda s, mo=mo, pb=pb: s.activation(
                                            s1b[:, mo * NSUB:(mo + 1) * NSUB], pb[:, :],
                                            AF.Gelu, bias=c_sb1[:, mo:mo + 1], scale=1.0))
                                    for mo in range(LF // 128):
                                        slot = mo % 2
                                        k.wait("gpsimd", "mm", sh_slot_mm[slot])
                                        sv = k.op("gpsimd", lambda e, mo=mo, slot=slot:
                                                  e.dma_start(
                                                      shb[slot][:, :].rearrange(
                                                          "p (kb m) -> p kb m", m=128),
                                                      ffn_shared_w2[:, mo * 128:(mo + 1) * 128]
                                                      .rearrange("(kb p) m -> p kb m", p=128)),
                                                  inc="gld", n=16)
                                        k.wait("tensor", "gld", sv)
                                        pb = ps[2 + mo % 2]
                                        if mo >= 2:
                                            k.wait("tensor", "vq")
                                        k.wait("tensor", "sq")
                                        for kk in range(8):
                                            k.op("tensor", lambda t, kk=kk, slot=slot,
                                                 pb=pb: t.matmul(
                                                     pb[:, :],
                                                     shb[slot][:, kk * 128:(kk + 1) * 128],
                                                     s1b[:, kk * NSUB:(kk + 1) * NSUB],
                                                     start=(kk == 0), stop=(kk == 7)),
                                                 inc="mm" if kk == 7 else None)
                                        sh_slot_mm[slot] = k.val("mm")
                                        k.wait("vector", "mm")
                                        v_op(k, lambda v, mo=mo, h2=h2, pb=pb:
                                             v.scalar_tensor_tensor(
                                                 accb[:, mo * SUP + h2 * NSUB:
                                                      mo * SUP + h2 * NSUB + NSUB],
                                                 pb[:, :], 0.1,
                                                 accb[:, mo * SUP + h2 * NSUB:
                                                      mo * SUP + h2 * NSUB + NSUB],
                                                 ALU.mult, ALU.add), dep=True)
                                    k.wait("tensor", "st")
                                    for mo in range(LF // 128):
                                        pb = ps[4 + mo % 2]
                                        if mo >= 2:
                                            k.wait("tensor", "vq")
                                        k.op("tensor", lambda t, mo=mo, h2=h2, pb=pb:
                                             t.matmul(pb[:, :],
                                                      c_kneg[:, mo * 128:(mo + 1) * 128],
                                                      wfm[:, h2 * NSUB:(h2 + 1) * NSUB],
                                                      start=True, stop=True), inc="mm")
                                        k.wait("vector", "mm")
                                        v_op(k, lambda v, mo=mo, h2=h2, pb=pb: v.tensor_add(
                                            accb[:, mo * SUP + h2 * NSUB:
                                                 mo * SUP + h2 * NSUB + NSUB],
                                            accb[:, mo * SUP + h2 * NSUB:
                                                 mo * SUP + h2 * NSUB + NSUB],
                                            pb[:, :]), dep=True)
                        dpes = ExitStack()
                        with dpes:
                            xnew = sb(f"fD{call}{sup}_xn", 128, 16 * NSUB, F32, dpes)
                            u_sb = sb(f"fD{call}{sup}_u", 128, 16 * NSUB, F32, dpes)
                            cxl2 = sb(f"fD{call}{sup}_cx", 128, 16 * NSUB, BF16, dpes)
                            minb = sb(f"fD{call}{sup}_mb", 128, NSUB, BF16, dpes)
                            abf = sb(f"fD{call}{sup}_ab", 128, 8 * NSUB, BF16, dpes)
                            upw2 = [sb(f"fD{call}{sup}_up{i}", 128, 8 * 128, BF16, dpes)
                                    for i in range(2)]
                            cow2 = [sb(f"fD{call}{sup}_co{i}", 128, 16 * 128, BF16, dpes)
                                    for i in range(2)]
                            x0t = sb(f"fD{call}{sup}_x0t", 128, NSUB, F32, dpes)
                            t1t = sb(f"fD{call}{sup}_t1", 128, NSUB, F32, dpes)
                            tmp2 = sb(f"fD{call}{sup}_tp", 128, NSUB, F32, dpes)
                            xsq2 = sb(f"fD{call}{sup}_xq", 128, NSUB, F32, dpes)
                            rmu2 = sb(f"fD{call}{sup}_ru", 1, NSUB, F32, dpes)
                            rms2 = sb(f"fD{call}{sup}_rm", 1, NSUB, F32, dpes)
                            rm22 = sb(f"fD{call}{sup}_r2", 1, NSUB, F32, dpes)
                            rva2 = sb(f"fD{call}{sup}_rv", 1, NSUB, F32, dpes)
                            rrs2 = sb(f"fD{call}{sup}_rr", 1, NSUB, F32, dpes)
                            rmr2 = sb(f"fD{call}{sup}_rx", 1, NSUB, F32, dpes)
                            scope_guard(k)
                            k.wait("vector", "mm")
                            k.wait("vector", "st")
                            d_slot_mm = {0: 0, 1: 0}
                            for ncs in range(SUP // NSUB):
                                tok0 = sup * SUP + ncs * NSUB
                                k.wait("vector", "vq")
                                k.wait("vector", "mm")
                                for m in range(LF // 128):
                                    v_op(k, lambda v, m=m, ncs=ncs: v.tensor_copy(
                                        abf[:, m * NSUB:(m + 1) * NSUB],
                                        accb[:, m * SUP + ncs * NSUB:
                                             m * SUP + ncs * NSUB + NSUB]))
                                abv = k.val("vq")
                                cv = None
                                k.wait("sync", "mm")
                                for kk in range(16):
                                    cv = k.op("sync", lambda e, kk=kk, tok0=tok0:
                                              e.dma_start(
                                                  cxl2[:, kk * NSUB:(kk + 1) * NSUB],
                                                  s_cx[kk * 128:(kk + 1) * 128,
                                                       tok0:tok0 + NSUB]),
                                              inc="ld", n=16)
                                for mo in range(D // 128):
                                    slot = mo % 2
                                    k.wait("gpsimd", "mm", d_slot_mm[slot])
                                    uv = k.op("gpsimd", lambda e, mo=mo, slot=slot:
                                              e.dma_start(
                                                  upw2[slot][:, :].rearrange(
                                                      "p (kb m) -> p kb m", m=128),
                                                  p_fup[:, mo * 128:(mo + 1) * 128]
                                                  .rearrange("(kb p) m -> p kb m", p=128)),
                                              inc="gld", n=16)
                                    cv2 = k.op("gpsimd", lambda e, mo=mo, slot=slot:
                                               e.dma_start(
                                                   cow2[slot][:, :].rearrange(
                                                       "p (kb m) -> p kb m", m=128),
                                                   p_fcore[:, mo * 128:(mo + 1) * 128]
                                                   .rearrange("(kb p) m -> p kb m", p=128)),
                                               inc="gld", n=16)
                                    k.wait("tensor", "gld", cv2)
                                    k.wait("tensor", "ld", cv)
                                    k.wait("tensor", "vq", abv)
                                    pb = ps[mo % 2]
                                    if mo >= 2:
                                        k.wait("tensor", "vq")
                                    for kk in range(8):
                                        k.op("tensor", lambda t, kk=kk, slot=slot, pb=pb:
                                             t.matmul(pb[:, :],
                                                      upw2[slot][:, kk * 128:(kk + 1) * 128],
                                                      abf[:, kk * NSUB:(kk + 1) * NSUB],
                                                      start=(kk == 0), stop=False))
                                    for kk in range(16):
                                        k.op("tensor", lambda t, kk=kk, slot=slot, pb=pb:
                                             t.matmul(pb[:, :],
                                                      cow2[slot][:, kk * 128:(kk + 1) * 128],
                                                      cxl2[:, kk * NSUB:(kk + 1) * NSUB],
                                                      start=False, stop=(kk == 15)),
                                             inc="mm" if kk == 15 else None)
                                    d_slot_mm[slot] = k.val("mm")
                                    k.wait("sync", "vq")
                                    xv = k.op("sync", lambda e, mo=mo, tok0=tok0:
                                              e.dma_start(
                                                  x0t[:, :],
                                                  src[mo * 128:(mo + 1) * 128,
                                                      tok0:tok0 + NSUB]),
                                              inc="ld", n=16)
                                    k.wait("vector", "mm")
                                    k.wait("vector", "vq")
                                    v_op(k, lambda v, mo=mo, pb=pb: v.tensor_scalar(
                                        t1t[:, :], pb[:, :], 0.5,
                                        c_ucbh[:, mo:mo + 1], op0=ALU.mult,
                                        op1=ALU.add))
                                    k.wait("vector", "ld", xv)
                                    v_op(k, lambda v, mo=mo: v.tensor_add(
                                        xnew[:, mo * NSUB:(mo + 1) * NSUB],
                                        x0t[:, :], t1t[:, :]), dep=True)
                                xnv = k.val("vq")
                                k.wait("sync", "vq", xnv)
                                for kk in range(16):
                                    k.op("sync", lambda e, kk=kk, tok0=tok0:
                                         e.dma_start(
                                             out_t[kk * 128:(kk + 1) * 128,
                                                   tok0:tok0 + NSUB],
                                             xnew[:, kk * NSUB:(kk + 1) * NSUB]),
                                         inc="st", n=16)
                                if call == 1:
                                    emit_ln_simple(k, nc, ps, xnew, u_sb, tmp2, xsq2,
                                                   rmu2, rms2, rm22, rva2, rrs2, rmr2,
                                                   c_ones_col, c_ones_row, D)
                                    uv2 = k.val("vq")
                                    k.wait("sync", "vq", uv2)
                                    for kk in range(16):
                                        k.op("sync", lambda e, kk=kk, tok0=tok0:
                                             e.dma_start(
                                                 s_u[kk * 128:(kk + 1) * 128,
                                                     tok0:tok0 + NSUB],
                                                 u_sb[:, kk * NSUB:(kk + 1) * NSUB]),
                                             inc="st", n=16)
                                    emit_rms_prep(k, nc, ps, u_sb, xsq2, rms2, rrs2,
                                                  c_ones_col, c_ones_row, D)
                                    for kk in range(16):
                                        k.wait("vector", "st")
                                        v_op(k, lambda v, kk=kk: v.scalar_tensor_tensor(
                                            minb[:, :],
                                            u_sb[:, kk * NSUB:(kk + 1) * NSUB],
                                            c_ln1[:, kk:kk + 1], ps[4][:, :],
                                            ALU.mult, ALU.mult), dep=True)
                                        mv = k.val("vq")
                                        k.wait("sync", "vq", mv)
                                        k.op("sync", lambda e, kk=kk, tok0=tok0:
                                             e.dma_start(
                                                 g_min_in[kk * 128:(kk + 1) * 128,
                                                          tok0:tok0 + NSUB],
                                                 minb[:, :]), inc="st", n=16)
                                k.wait("vector", "st")
                    k.flush()

        def emit_ln_stage(k, nc, ps, src, tok0, x0_sb, xln, cxl, tmp, xsq,
                          rmu, rms_, rm2, rva, rrs, rmr, ones_c, ones_r,
                          st_ln, st_cx, store_ln):
            k.wait("sync", "vq")
            k.wait("sync", "sq")
            k.wait("sync", "mm")
            k.wait("vector", "st")
            k.wait("scalar", "st")
            ldv = None
            for kk in range(16):
                ldv = k.op("sync", lambda e, kk=kk: e.dma_start(
                    x0_sb[:, kk * NSUB:(kk + 1) * NSUB],
                    src[kk * 128:(kk + 1) * 128, tok0:tok0 + NSUB]), inc="ld", n=16)
            k.wait("tensor", "ld", ldv)
            k.wait("tensor", "vq")
            k.wait("tensor", "sq")
            for kk in range(16):
                k.op("tensor", lambda t, kk=kk: t.matmul(
                    ps[6][0:1, :], ones_c[:, :], x0_sb[:, kk * NSUB:(kk + 1) * NSUB],
                    start=(kk == 0), stop=(kk == 15)), inc="mm" if kk == 15 else None)
            k.wait("scalar", "ld", ldv)
            for kk in range(16):
                s_op(k, lambda s, kk=kk: s.activation(
                    xsq[:, :], x0_sb[:, kk * NSUB:(kk + 1) * NSUB], AF.Square),
                    dep=True)
                k.wait("tensor", "sq")
                k.op("tensor", lambda t, kk=kk: t.matmul(
                    ps[7][0:1, :], ones_c[:, :], xsq[:, :],
                    start=(kk == 0), stop=(kk == 15)), inc="mm" if kk == 15 else None)
                if kk < 15:
                    k.op("tensor", lambda t: t.matmul(
                        ps[5][0:1, 0:1], ones_r[0:1, 0:1], ones_r[0:1, 0:1],
                        start=True, stop=True), inc="mm")
                    k.wait("scalar", "mm")
            k.wait("vector", "mm")
            v_op(k, lambda v: v.tensor_scalar(rmu[:, :], ps[6][0:1, :], 1.0 / D,
                                              None, op0=ALU.mult))
            v_op(k, lambda v: v.tensor_scalar(rms_[:, :], ps[7][0:1, :], 1.0 / D,
                                              None, op0=ALU.mult))
            k.wait("vector", "vq")
            v_op(k, lambda v: v.tensor_mul(rm2[:, :], rmu[:, :], rmu[:, :]))
            k.wait("vector", "vq")
            v_op(k, lambda v: v.tensor_sub(rva[:, :], rms_[:, :], rm2[:, :]))
            k.wait("scalar", "vq")
            s_op(k, lambda s: s.activation(rva[:, :], rva[:, :], AF.Ln, bias=c_eps[0:1, :],
                                           scale=1.0), dep=True)
            s_op(k, lambda s: s.activation(rrs[:, :], rva[:, :], AF.Exp, bias=0.0,
                                           scale=c_mhalf[0:1, :]), dep=True)
            k.wait("vector", "sq")
            v_op(k, lambda v: v.tensor_mul(rmr[:, :], rmu[:, :], rrs[:, :]))
            k.wait("tensor", "vq")
            k.op("tensor", lambda t: t.matmul(ps[4][:, :], ones_r[:, :], rrs[:, :],
                                              start=True, stop=True), inc="mm")
            k.op("tensor", lambda t: t.matmul(ps[5][:, :], ones_r[:, :], rmr[:, :],
                                              start=True, stop=True), inc="mm")
            k.wait("vector", "mm")
            k.wait("scalar", "mm")
            if tok0 == 0 and not _dbgx_done:
                _dbgx_done.append(1)
                k.wait("sync", "mm")
                for kk in range(16):
                    k.op("sync", lambda e, kk=kk: e.dma_start(
                        s_dbgx[kk * 128:(kk + 1) * 128, :],
                        x0_sb[:, kk * NSUB:(kk + 1) * NSUB]), inc="st", n=16)
            for kk in range(16):
                v_op(k, lambda v, kk=kk: v.tensor_mul(
                    tmp[:, :], x0_sb[:, kk * NSUB:(kk + 1) * NSUB], ps[4][:, :]),
                    dep=True)
                k.wait("vector", "vq")
                v_op(k, lambda v, kk=kk: v.tensor_sub(
                    xln[:, kk * NSUB:(kk + 1) * NSUB], tmp[:, :], ps[5][:, :]))
                k.wait("scalar", "vq")
                s_op(k, lambda s, kk=kk: s.activation(
                    cxl[:, kk * NSUB:(kk + 1) * NSUB],
                    xln[:, kk * NSUB:(kk + 1) * NSUB], AF.Gelu))
            xlv = k.val("vq")
            clv = k.val("sq")
            if store_ln:
                k.wait("sync", "vq", xlv)
                k.wait("sync", "sq", clv)
                for kk in range(16):
                    k.op("sync", lambda e, kk=kk: e.dma_start(
                        st_ln[kk * 128:(kk + 1) * 128, tok0:tok0 + NSUB],
                        xln[:, kk * NSUB:(kk + 1) * NSUB]), inc="st", n=16)
                    k.op("sync", lambda e, kk=kk: e.dma_start(
                        st_cx[kk * 128:(kk + 1) * 128, tok0:tok0 + NSUB],
                        cxl[:, kk * NSUB:(kk + 1) * NSUB]), inc="st", n=16)

        def emit_ln_simple(k, nc, ps, xin_sb, out_sb, tmp, xsq, rmu, rms_, rm2,
                           rva, rrs, rmr, ones_c, ones_r, DD):
            nt = 16
            k.wait("tensor", "vq")
            k.wait("tensor", "sq")
            k.wait("scalar", "vq")
            for kk in range(nt):
                k.op("tensor", lambda t, kk=kk: t.matmul(
                    ps[6][0:1, :], ones_c[:, :], xin_sb[:, kk * NSUB:(kk + 1) * NSUB],
                    start=(kk == 0), stop=(kk == nt - 1)),
                    inc="mm" if kk == nt - 1 else None)
            for kk in range(nt):
                s_op(k, lambda s, kk=kk: s.activation(
                    xsq[:, :], xin_sb[:, kk * NSUB:(kk + 1) * NSUB], AF.Square),
                    dep=True)
                k.wait("tensor", "sq")
                k.op("tensor", lambda t, kk=kk: t.matmul(
                    ps[7][0:1, :], ones_c[:, :], xsq[:, :],
                    start=(kk == 0), stop=(kk == nt - 1)),
                    inc="mm" if kk == nt - 1 else None)
                if kk < nt - 1:
                    k.op("tensor", lambda t: t.matmul(
                        ps[5][0:1, 0:1], ones_r[0:1, 0:1], ones_r[0:1, 0:1],
                        start=True, stop=True), inc="mm")
                    k.wait("scalar", "mm")
            k.wait("vector", "mm")
            v_op(k, lambda v: v.tensor_scalar(rmu[:, :], ps[6][0:1, :], 1.0 / DD,
                                              None, op0=ALU.mult))
            v_op(k, lambda v: v.tensor_scalar(rms_[:, :], ps[7][0:1, :], 1.0 / DD,
                                              None, op0=ALU.mult))
            k.wait("vector", "vq")
            v_op(k, lambda v: v.tensor_mul(rm2[:, :], rmu[:, :], rmu[:, :]))
            k.wait("vector", "vq")
            v_op(k, lambda v: v.tensor_sub(rva[:, :], rms_[:, :], rm2[:, :]))
            k.wait("scalar", "vq")
            s_op(k, lambda s: s.activation(rva[:, :], rva[:, :], AF.Ln, bias=c_eps[0:1, :],
                                           scale=1.0), dep=True)
            s_op(k, lambda s: s.activation(rrs[:, :], rva[:, :], AF.Exp, bias=0.0,
                                           scale=c_mhalf[0:1, :]), dep=True)
            k.wait("vector", "sq")
            v_op(k, lambda v: v.tensor_mul(rmr[:, :], rmu[:, :], rrs[:, :]))
            k.wait("tensor", "vq")
            k.op("tensor", lambda t: t.matmul(ps[4][:, :], ones_r[:, :], rrs[:, :],
                                              start=True, stop=True), inc="mm")
            k.op("tensor", lambda t: t.matmul(ps[5][:, :], ones_r[:, :], rmr[:, :],
                                              start=True, stop=True), inc="mm")
            k.wait("vector", "mm")
            for kk in range(nt):
                v_op(k, lambda v, kk=kk: v.tensor_mul(
                    tmp[:, :], xin_sb[:, kk * NSUB:(kk + 1) * NSUB], ps[4][:, :]),
                    dep=True)
                k.wait("vector", "vq")
                v_op(k, lambda v, kk=kk: v.tensor_sub(
                    out_sb[:, kk * NSUB:(kk + 1) * NSUB], tmp[:, :], ps[5][:, :]))

        def emit_rms_prep(k, nc, ps, xin_sb, xsq, rms_, rrs, ones_c, ones_r, DD):
            nt = 16
            k.wait("tensor", "vq")
            k.wait("tensor", "sq")
            k.wait("scalar", "vq")
            for kk in range(nt):
                s_op(k, lambda s, kk=kk: s.activation(
                    xsq[:, :], xin_sb[:, kk * NSUB:(kk + 1) * NSUB], AF.Square),
                    dep=True)
                k.wait("tensor", "sq")
                k.op("tensor", lambda t, kk=kk: t.matmul(
                    ps[7][0:1, :], ones_c[:, :], xsq[:, :],
                    start=(kk == 0), stop=(kk == nt - 1)),
                    inc="mm" if kk == nt - 1 else None)
                if kk < nt - 1:
                    k.op("tensor", lambda t: t.matmul(
                        ps[5][0:1, 0:1], ones_r[0:1, 0:1], ones_r[0:1, 0:1],
                        start=True, stop=True), inc="mm")
                    k.wait("scalar", "mm")
            k.wait("vector", "mm")
            v_op(k, lambda v: v.tensor_scalar(rms_[:, :], ps[7][0:1, :], 1.0 / DD,
                                              None, op0=ALU.mult))
            k.wait("scalar", "vq")
            s_op(k, lambda s: s.activation(rms_[:, :], rms_[:, :], AF.Ln, bias=c_eps[0:1, :],
                                           scale=1.0), dep=True)
            s_op(k, lambda s: s.activation(rrs[:, :], rms_[:, :], AF.Exp, bias=0.0,
                                           scale=c_mhalf[0:1, :]), dep=True)
            k.wait("tensor", "sq")
            k.op("tensor", lambda t: t.matmul(ps[4][:, :], ones_r[:, :], rrs[:, :],
                                              start=True, stop=True), inc="mm")
            k.wait("vector", "mm")

        def emit_rms_scale(k, nc, ps, xin_sb, out_sb, xsq, rms_, rrs, ones_c,
                           ones_r, lnw, DD):
            emit_rms_prep(k, nc, ps, xin_sb, xsq, rms_, rrs, ones_c, ones_r, DD)
            for kk in range(16):
                v_op(k, lambda v, kk=kk: v.scalar_tensor_tensor(
                    out_sb[:, kk * NSUB:(kk + 1) * NSUB],
                    xin_sb[:, kk * NSUB:(kk + 1) * NSUB],
                    lnw[:, kk:kk + 1], ps[4][:, :], ALU.mult, ALU.mult))

        def emit_router(k, nc, ps, xln, rtr_sb, rbb, ident_sb, ones_c, wfm, ncs,
                        rr, r_sc):
            (r_pm, r_npm, r_ex, r_pr, r_k1, r_pk, r_k2, r_w0, r_w, r_lg,
             _a, _b) = rr
            (r_m1, r_mm2, r_dd, r_e2, r_g1, r_g2) = r_sc
            for ts in range(4):
                k.wait("tensor", "vq")
                k.wait("tensor", "sq")
                for kk in range(16):
                    k.op("tensor", lambda t, ts=ts, kk=kk: t.matmul(
                        ps[2][0:128, 0:EXN],
                        xln[:, kk * NSUB + ts * 128: kk * NSUB + ts * 128 + 128],
                        rtr_sb[:, kk * EXN:(kk + 1) * EXN],
                        start=(kk == 0), stop=(kk == 15)),
                        inc="mm" if kk == 15 else None)
                k.wait("vector", "mm")
                v_op(k, lambda v: v.tensor_add(r_lg[:, :], ps[2][0:128, 0:EXN],
                                               rbb[:, :]))
                v_op(k, lambda v: v.reduce_max(r_pm[:, 0:1], r_lg[:, :], axis=AX.X),
                     dep=True)
                k.wait("scalar", "vq")
                s_op(k, lambda s: s.mul(r_npm[:, 0:1], r_pm[:, 0:1], -1.0))
                k.wait("scalar", "sq")
                s_op(k, lambda s: s.activation(r_ex[:, :], r_lg[:, :], AF.Exp,
                                               bias=r_npm[:, 0:1], scale=1.0))
                k.wait("vector", "sq")
                v_op(k, lambda v: v.reduce_sum(r_m1[:, :], r_ex[:, :], axis=AX.X))
                v_op(k, lambda v: v.reciprocal(r_mm2[:, :], r_m1[:, :]), dep=True)
                v_op(k, lambda v: v.tensor_scalar(r_pr[:, :], r_ex[:, :],
                                                  r_mm2[:, :], None, op0=ALU.mult),
                     dep=True)
                v_op(k, lambda v: v.reduce_max(r_m1[:, :], r_pr[:, :], axis=AX.X),
                     dep=True)
                v_op(k, lambda v: v.tensor_scalar(r_k1[:, :], r_pr[:, :],
                                                  r_m1[:, :], None, op0=ALU.is_ge),
                     dep=True)
                v_op(k, lambda v: v.scalar_tensor_tensor(
                    r_pk[:, :], r_k1[:, :], -1e30, r_pr[:, :], ALU.mult, ALU.add),
                    dep=True)
                v_op(k, lambda v: v.reduce_max(r_mm2[:, :], r_pk[:, :], axis=AX.X),
                     dep=True)
                v_op(k, lambda v: v.tensor_scalar(r_k2[:, :], r_pk[:, :],
                                                  r_mm2[:, :], None, op0=ALU.is_ge),
                     dep=True)
                v_op(k, lambda v: v.tensor_sub(r_dd[:, :], r_mm2[:, :], r_m1[:, :]),
                     dep=True)
                k.wait("scalar", "vq")
                s_op(k, lambda s: s.activation(r_e2[:, :], r_dd[:, :], AF.Exp))
                k.wait("vector", "sq")
                v_op(k, lambda v: v.tensor_scalar(r_g2[:, :], r_e2[:, :], 1.0,
                                                  None, op0=ALU.add))
                v_op(k, lambda v: v.reciprocal(r_g1[:, :], r_g2[:, :]), dep=True)
                v_op(k, lambda v: v.scalar_tensor_tensor(
                    r_g2[:, :], r_g1[:, :], -1.0, ones_c[:, :], ALU.mult, ALU.add),
                    dep=True)
                v_op(k, lambda v: v.tensor_scalar(r_w0[:, :], r_k1[:, :],
                                                  r_g1[:, :], None, op0=ALU.mult))
                k.wait("vector", "vq")
                v_op(k, lambda v: v.scalar_tensor_tensor(
                    r_w[:, :], r_k2[:, :], r_g2[:, :], r_w0[:, :],
                    ALU.mult, ALU.add))
                k.wait("tensor", "vq")
                k.op("tensor", lambda t: t.transpose(
                    ps[3][0:8, 0:128], r_w[:, :], ident_sb[:, :]), inc="mm")
                k.wait("vector", "mm")
                v_op(k, lambda v, ncs=ncs, ts=ts: v.tensor_copy(
                    wfm[:, ncs * NSUB + ts * 128: ncs * NSUB + ts * 128 + 128],
                    ps[3][0:8, 0:128]))

        def mixer_phase():
            pes = ExitStack()
            with pes:
                ipw_sb = sb("mx_ipw", 128, 16 * 1408, BF16, pes)
                dtw_sb = sb("mx_dtw", 128, DC, BF16, pes)
                rhs_sb = [sb(f"mx_rhs{i}", 128, 16 * CH, BF16, pes) for i in range(2)]
                siluz = sb("mx_siluz", 128, 4 * CH, F32, pes)
                xsrcA = sb("mx_xsA", 64, CH, F32, pes)
                xsrcB = sb("mx_xsB", 64, CH, F32, pes)
                BsbA = sb("mx_BA", 64, CH, BF16, pes)
                BsbB = sb("mx_BB", 64, CH, BF16, pes)
                Csb = [sb(f"mx_C{j}", 128, CH, BF16, pes) for j in range(4)]
                dtp = sb("mx_dtp", 128, CH, BF16, pes)
                dtf = sb("mx_dtf", 128, 4 * CH, F32, pes)
                relu_t = sb("mx_rel", 128, CH, F32, pes)
                ab_t = sb("mx_ab", 128, CH, F32, pes)
                xs_sb = sb("mx_xs", 128, 4 * CH, F32, pes)
                dtx = sb("mx_dtx", 128, 4 * CH, F32, pes)
                xrep = [sb(f"mx_xr{j}", 128, CH + 4, F32, pes) for j in range(4)]
                xc_t = sb("mx_xc", 128, CH, F32, pes)
                dA = [sb(f"mx_dA{i}", 128, CH, F32, pes) for i in range(2)]
                dbx = [sb(f"mx_db{i}", 128, CH, F32, pes) for i in range(2)]
                hbuf = [sb(f"mx_h{i}", 128, CH, F32, pes) for i in range(2)]
                prodb = [sb(f"mx_p{i}", 128, CH, F32, pes) for i in range(2)]
                crepb = [sb(f"mx_cr{i}", 128, CH, F32, pes) for i in range(2)]
                yac2 = [sb(f"mx_y2{i}", 128, CH, F32, pes) for i in range(2)]
                yac = sb("mx_yac", 128, CH, F32, pes)
                states = sb("mx_st", 128, 4 * NST, F32, pes)
                ygb = [sb(f"mx_yg{i}", 128, CH, BF16, pes) for i in range(2)]

                k.op("gpsimd", lambda e: e.dma_start(
                    ipw_sb[:, :].rearrange("p (kb m) -> p kb m", m=1408),
                    ipwT.rearrange("(kb p) m -> p kb m", p=128)), inc="gld", n=16)
                k.op("gpsimd", lambda e: e.dma_start(
                    dtw_sb[:, :], dtwT[:, :]), inc="gld", n=16)
                k.wait("gpsimd", "st")
                k.op("gpsimd", lambda e: e.collective_compute(
                    "AllGather", ALU.bypass, replica_groups=[list(range(NC))],
                    ins=[g_min_in.opt()], outs=[g_min_out.opt()]), inc="cc")
                k.wait("sync", "cc")
                k.wait("tensor", "gld")

                NCHUNK = (B * T) // CH
                rhs_slot_mm = {0: 0, 1: 0}
                mxw = {"pb": [0, 0], "pc": [0, 0], "dA": [0, 0], "dbx": [0, 0],
                       "hb": [0, 0], "crep": [0, 0], "prodb": [0, 0],
                       "prodv": [0, 0], "acc": [0, 0], "ygst": [0, 0],
                       "selb": [0, 0], "dAsq": [0, 0], "crepsq": [0, 0],
                       "state": [0] * (4 * NST)}
                for ch in range(NCHUNK):
                    bb = ch // (T // CH)
                    t0 = (ch % (T // CH)) * CH
                    rank = bb * 4 + t0 // TOKC
                    col0 = t0 % TOKC
                    slot = ch % 2
                    k.wait("sync", "mm", rhs_slot_mm[slot])
                    rv = k.op("sync", lambda e, slot=slot, rank=rank, col0=col0:
                              e.dma_start(
                                  rhs_sb[slot][:, :].rearrange(
                                      "p (kb m) -> p kb m", m=CH),
                                  g_min_out[rank * D:(rank + 1) * D, col0:col0 + CH]
                                  .rearrange("(kb p) m -> p kb m", p=128)),
                              inc="ld", n=16)
                    k.wait("tensor", "ld", rv)
                    k.wait("scalar", "vq")
                    k.wait("scalar", "mm")
                    for m in range(11):
                        pb = ps[m % 3]
                        k.wait("tensor", "ev")
                        for kk in range(16):
                            k.op("tensor", lambda t, m=m, kk=kk, pb=pb, slot=slot:
                                 t.matmul(pb[:, :],
                                          ipw_sb[:, kk * 1408 + m * 128:
                                                 kk * 1408 + m * 128 + 128],
                                          rhs_sb[slot][:, kk * CH:(kk + 1) * CH],
                                          start=(kk == 0), stop=(kk == 15)),
                                 inc="mm" if kk == 15 else None)
                        k.wait("scalar", "mm")
                        if m < 4:
                            k.op("scalar", lambda s, m=m, pb=pb: s.activation(
                                siluz[:, m * CH:(m + 1) * CH], pb[:, :], AF.Silu),
                                inc="ev")
                        elif m == 4:
                            k.op("scalar", lambda s, pb=pb: s.copy(
                                xsrcA[:, :], pb[0:64, :]), inc="ev")
                            k.op("scalar", lambda s, pb=pb: s.copy(
                                xsrcB[:, :], pb[64:128, :]), inc="ev")
                        elif m == 5:
                            k.op("scalar", lambda s, pb=pb: s.copy(
                                BsbA[:, :], pb[0:64, :]), inc="ev")
                            k.op("scalar", lambda s, pb=pb: s.copy(
                                BsbB[:, :], pb[64:128, :]), inc="ev")
                        elif m < 10:
                            k.op("scalar", lambda s, m=m, pb=pb: s.copy(
                                Csb[m - 6][:, :], pb[:, :]), inc="ev")
                        else:
                            k.op("scalar", lambda s, pb=pb: s.copy(
                                dtp[:, :], pb[:, :]), inc="ev")
                    rhs_slot_mm[slot] = k.val("mm")
                    k.wait("tensor", "ev")
                    for j in range(4):
                        pb = ps[4 + j % 2]
                        if j >= 2:
                            k.wait("tensor", "sq")
                        k.op("tensor", lambda t, j=j, pb=pb: t.matmul(
                            pb[:, :], dtw_sb[:, j * 128:(j + 1) * 128], dtp[:, :],
                            start=True, stop=True), inc="mm")
                        k.wait("scalar", "mm")
                        k.wait("scalar", "vq")
                        s_op(k, lambda s, j=j, pb=pb: s.activation(
                            relu_t[:, :], pb[:, :], AF.Relu,
                            bias=c_dtb[:, j:j + 1], scale=1.0), dep=True)
                        s_op(k, lambda s, j=j, pb=pb: s.activation(
                            ab_t[:, :], pb[:, :], AF.Abs,
                            bias=c_dtb[:, j:j + 1], scale=1.0))
                        s_op(k, lambda s: s.activation(
                            ab_t[:, :], ab_t[:, :], AF.Exp, bias=0.0, scale=-1.0),
                            dep=True)
                        s_op(k, lambda s: s.activation(
                            ab_t[:, :], ab_t[:, :], AF.Ln, bias=1.0, scale=1.0),
                            dep=True)
                        k.wait("vector", "sq")
                        v_op(k, lambda v, j=j: v.tensor_add(
                            dtf[:, j * CH:(j + 1) * CH], relu_t[:, :], ab_t[:, :]),
                            dep=True)
                    k.wait("tensor", "sq")
                    for j in range(4):
                        pb = ps[6]
                        k.wait("tensor", "vq")
                        src_t = xsrcA if j < 2 else xsrcB
                        k.op("tensor", lambda t, j=j, pb=pb, src_t=src_t: t.matmul(
                            pb[:, :],
                            c_selx[32 * (j % 2):32 * (j % 2) + 32, :],
                            src_t[32 * (j % 2):32 * (j % 2) + 32, :],
                            start=True, stop=True), inc="mm")
                        if t0 == 0:
                            v_op(k, lambda v, j=j: v.memset(xrep[j][:, 0:4], 0.0),
                                 dep=True)
                        else:
                            v_op(k, lambda v, j=j: v.tensor_copy(
                                xrep[j][:, 0:4], xrep[j][:, CH:CH + 4]), dep=True)
                        k.wait("vector", "mm")
                        k.wait("vector", "vq")
                        v_op(k, lambda v, j=j, pb=pb: v.tensor_copy(
                            xrep[j][:, 4:CH + 4], pb[:, :]))
                        k.wait("vector", "vq")
                        v_op(k, lambda v, j=j: v.tensor_scalar(
                            xc_t[:, :], xrep[j][:, 1:CH + 1],
                            c_convw[:, j * 4:j * 4 + 1], None, op0=ALU.mult))
                        for tau in range(1, 4):
                            k.wait("vector", "vq")
                            v_op(k, lambda v, j=j, tau=tau: v.scalar_tensor_tensor(
                                xc_t[:, :], xrep[j][:, 1 + tau:CH + 1 + tau],
                                c_convw[:, j * 4 + tau:j * 4 + tau + 1], xc_t[:, :],
                                ALU.mult, ALU.add))
                        k.wait("scalar", "vq")
                        s_op(k, lambda s, j=j: s.activation(
                            xs_sb[:, j * CH:(j + 1) * CH], xc_t[:, :], AF.Silu,
                            bias=c_convb[:, j:j + 1], scale=1.0), dep=True)
                        k.wait("vector", "sq")
                        k.wait("vector", "vq")
                        v_op(k, lambda v, j=j: v.tensor_mul(
                            dtx[:, j * CH:(j + 1) * CH],
                            dtf[:, j * CH:(j + 1) * CH],
                            xs_sb[:, j * CH:(j + 1) * CH]))
                    if t0 == 0:
                        k.wait("vector", "vq")
                        v_op(k, lambda v: v.memset(states[:, :], 0.0))
                    k.wait("tensor", "sq")
                    k.wait("tensor", "vq")
                    def mx_producers(idx):
                        j2, n2 = idx // NST, idx % NST
                        q2 = idx % 2
                        pb2, pc2 = ps[4 + q2], ps[6 + q2]
                        Bs2 = BsbA if j2 < 2 else BsbB
                        k.wait("tensor", "vq", mxw["pb"][q2])
                        k.op("tensor", lambda t, j2=j2, n2=n2, pb2=pb2, Bs2=Bs2:
                             t.matmul(pb2[:, :],
                                      c_selB[32 * (j2 % 2):32 * (j2 % 2) + 32,
                                             n2 * 128:(n2 + 1) * 128],
                                      Bs2[32 * (j2 % 2):32 * (j2 % 2) + 32, :],
                                      start=True, stop=True), inc="mm")
                        mxw["selb"][q2] = k.val("mm")
                        k.wait("tensor", "sq", mxw["pc"][q2])
                        k.op("tensor", lambda t, n2=n2, j2=j2, pc2=pc2: t.matmul(
                            pc2[:, :], c_selC[:, n2 * 128:(n2 + 1) * 128],
                            Csb[j2][:, :], start=True, stop=True), inc="mm")
                        selc_mm = k.val("mm")
                        k.wait("scalar", "vq", mxw["dA"][q2])
                        s_op(k, lambda sc_, j2=j2, n2=n2, q2=q2: sc_.activation(
                            dA[q2][:, :], dtf[:, j2 * CH:(j2 + 1) * CH], AF.Exp,
                            bias=0.0,
                            scale=c_aneg[:, j2 * NST + n2:j2 * NST + n2 + 1]))
                        mxw["dAsq"][q2] = k.val("sq")
                        k.wait("scalar", "mm", selc_mm)
                        k.wait("scalar", "gq", mxw["crep"][q2])
                        s_op(k, lambda sc_, q2=q2, pc2=pc2: sc_.copy(
                            crepb[q2][:, :], pc2[:, :]))
                        mxw["crepsq"][q2] = k.val("sq")
                        mxw["pc"][q2] = k.val("sq")

                    def mx_acc(idx):
                        qp = idx % 2
                        k.wait("vector", "gq", mxw["prodv"][qp])
                        if idx % NST < 2:
                            v_op(k, lambda v, qp=qp: v.tensor_copy(
                                yac2[qp][:, :], prodb[qp][:, :]))
                        else:
                            k.wait("vector", "vq", mxw["acc"][qp])
                            v_op(k, lambda v, qp=qp: v.tensor_add(
                                yac2[qp][:, :], yac2[qp][:, :], prodb[qp][:, :]))
                        mxw["acc"][qp] = k.val("vq")
                        mxw["prodb"][qp] = k.val("vq")
                        if idx % NST == NST - 1:
                            mx_gate(idx // NST)

                    def mx_gate(j2):
                        k.wait("vector", "vq")
                        v_op(k, lambda v: v.tensor_add(yac[:, :], yac2[0][:, :],
                                                       yac2[1][:, :]))
                        k.wait("vector", "vq")
                        v_op(k, lambda v, j2=j2: v.scalar_tensor_tensor(
                            yac[:, :], xs_sb[:, j2 * CH:(j2 + 1) * CH],
                            c_d128[:, j2:j2 + 1], yac[:, :], ALU.mult, ALU.add))
                        k.wait("vector", "ev")
                        k.wait("vector", "vq")
                        k.wait("vector", "st", mxw["ygst"][j2 % 2])
                        v_op(k, lambda v, j2=j2: v.tensor_mul(
                            ygb[j2 % 2][:, :], yac[:, :],
                            siluz[:, j2 * CH:(j2 + 1) * CH]))
                        yv = k.val("vq")
                        k.wait("sync", "vq", yv)
                        k.op("sync", lambda e, j2=j2, bb=bb, t0=t0: e.dma_start(
                            g_yg_in[bb * DC + j2 * 128: bb * DC + (j2 + 1) * 128,
                                    t0:t0 + CH], ygb[j2 % 2][:, :]), inc="st", n=16)
                        mxw["ygst"][j2 % 2] = k.val("st")

                    def mx_consumers(idx):
                        j2, n2 = idx // NST, idx % NST
                        q2 = idx % 2
                        pb2 = ps[4 + q2]
                        k.wait("vector", "mm", mxw["selb"][q2])
                        k.wait("vector", "vq", mxw["dbx"][q2])
                        v_op(k, lambda v, j2=j2, q2=q2, pb2=pb2: v.tensor_mul(
                            dbx[q2][:, :], dtx[:, j2 * CH:(j2 + 1) * CH],
                            pb2[:, :]))
                        mxw["pb"][q2] = k.val("vq")
                        k.wait("vector", "sq", mxw["dAsq"][q2])
                        k.wait("vector", "gq", mxw["hb"][q2])
                        k.wait("vector", "sq", mxw["state"][idx])
                        k.wait("vector", "vq")
                        v_op(k, lambda v, j2=j2, n2=n2, q2=q2: v.tensor_tensor_scan(
                            hbuf[q2][:, :], dA[q2][:, :], dbx[q2][:, :],
                            states[:, j2 * NST + n2:j2 * NST + n2 + 1],
                            ALU.mult, ALU.add))
                        scan_vq = k.val("vq")
                        mxw["dA"][q2] = scan_vq
                        mxw["dbx"][q2] = scan_vq
                        k.wait("scalar", "vq", scan_vq)
                        s_op(k, lambda sc_, j2=j2, n2=n2, q2=q2: sc_.copy(
                            states[:, j2 * NST + n2:j2 * NST + n2 + 1],
                            hbuf[q2][:, CH - 1:CH]))
                        mxw["state"][idx] = k.val("sq")
                        k.wait("gpsimd", "vq", scan_vq)
                        k.wait("gpsimd", "sq", mxw["crepsq"][q2])
                        k.wait("gpsimd", "vq", mxw["prodb"][q2])
                        k.op("gpsimd", lambda g, q2=q2: g.tensor_mul(
                            prodb[q2][:, :], hbuf[q2][:, :], crepb[q2][:, :]),
                            inc="gq")
                        gq_v = k.val("gq")
                        mxw["prodv"][q2] = gq_v
                        mxw["hb"][q2] = gq_v
                        mxw["crep"][q2] = gq_v

                    mx_producers(0)
                    for idx in range(4 * NST):
                        if idx + 1 < 4 * NST:
                            mx_producers(idx + 1)
                        mx_consumers(idx)
                        if idx >= 1:
                            mx_acc(idx - 1)
                    mx_acc(4 * NST - 1)
                    if ch % 8 == 7:
                        k.flush()
                k.flush()

        def outproj_phase():
            pes = ExitStack()
            with pes:
                yrhs = sb("op_yrhs", 128, 32 * NSUB, BF16, pes)
                wop = [sb(f"op_w{i}", 128, 32 * 128, BF16, pes) for i in range(2)]
                h1 = sb("op_h1", 128, 16 * NSUB, F32, pes)
                m2 = sb("op_m2", 128, 16 * NSUB, BF16, pes)
                s12t = sb("op_s12t", 128, NSUB, F32, pes)
                xsq = sb("op_xsq", 128, NSUB, F32, pes)
                rms_ = sb("op_rms", 1, NSUB, F32, pes)
                rrs = sb("op_rrs", 1, NSUB, F32, pes)
                u_t = sb("op_ut", 128, NSUB, F32, pes)
                x1_t = sb("op_x1t", 128, NSUB, F32, pes)

                k.wait("gpsimd", "st")
                k.op("gpsimd", lambda e: e.collective_compute(
                    "AllGather", ALU.bypass, replica_groups=[list(range(NC))],
                    ins=[g_yg_in.opt()], outs=[g_yg_out.opt()]), inc="cc")
                k.wait("sync", "cc")
                k.S("ld")[1] += 16

                def sw_copy(e):
                    for arm in nc.switch_core_id(engines=[nc.sync], n=128):
                        c = arm.logical if hasattr(arm, "logical") else int(arm)
                        c = c % NC
                        bbq, ttq = c // 4, (c % 4) * TOKC
                        srcap = bass.AP(
                            g_yg_out.tensor, (bbq * DC) * T + ttq,
                            [[B * DC * T, NC], [T, DC], [1, TOKC]])
                        dstap = s_ymy.rearrange("(r d) t -> r d t", d=DC)
                        e.dma_start(dstap, srcap).then_inc(k.S("ld")[0], 16)
                k.raw("sync", sw_copy)
                swv = k.val("ld")
                k.wait("sync", "ld", swv)

                op_slot_mm = {0: 0, 1: 0}
                for ncs in range(TOKC // NSUB):
                    tok0 = ncs * NSUB
                    rv = None
                    k.wait("sync", "mm")
                    for kb_ in range(32):
                        rv = k.op("sync", lambda e, kb_=kb_, tok0=tok0: e.dma_start(
                            yrhs[:, kb_ * NSUB:(kb_ + 1) * NSUB],
                            s_ymy[kb_ * 128:(kb_ + 1) * 128, tok0:tok0 + NSUB]),
                            inc="ld", n=16)
                    k.wait("tensor", "ld", rv)
                    for mo in range(D // 128):
                        slot = mo % 2
                        k.wait("gpsimd", "mm", op_slot_mm[slot])
                        wv = k.op("gpsimd", lambda e, mo=mo, slot=slot: e.dma_start(
                            wop[slot][:, :].rearrange("p (kb m) -> p kb m", m=128),
                            p_opw[:, mo * 128:(mo + 1) * 128]
                            .rearrange("(kb p) m -> p kb m", p=128)),
                            inc="gld", n=16)
                        k.wait("tensor", "gld", wv)
                        pb = ps[mo % 2]
                        if mo >= 2:
                            k.wait("tensor", "vq")
                        for kb_ in range(32):
                            k.op("tensor", lambda t, kb_=kb_, slot=slot, pb=pb:
                                 t.matmul(pb[:, :],
                                          wop[slot][:, kb_ * 128:(kb_ + 1) * 128],
                                          yrhs[:, kb_ * NSUB:(kb_ + 1) * NSUB],
                                          start=(kb_ == 0), stop=(kb_ == 31)),
                                 inc="mm" if kb_ == 31 else None)
                        op_slot_mm[slot] = k.val("mm")
                        k.wait("sync", "vq")
                        uv = k.op("sync", lambda e, mo=mo, tok0=tok0: e.dma_start(
                            u_t[:, :], s_u[mo * 128:(mo + 1) * 128, tok0:tok0 + NSUB]),
                            inc="ld", n=16)
                        xv = k.op("sync", lambda e, mo=mo, tok0=tok0: e.dma_start(
                            x1_t[:, :], s_x1[mo * 128:(mo + 1) * 128,
                                             tok0:tok0 + NSUB]), inc="ld", n=16)
                        k.wait("vector", "mm")
                        k.wait("vector", "ld", xv)
                        v_op(k, lambda v, mo=mo, pb=pb: v.tensor_add(
                            h1[:, mo * NSUB:(mo + 1) * NSUB], pb[:, :], u_t[:, :]),
                            dep=True)
                        k.wait("vector", "vq")
                        k.wait("vector", "st")
                        v_op(k, lambda v, mo=mo: v.tensor_add(
                            s12t[:, :], h1[:, mo * NSUB:(mo + 1) * NSUB], x1_t[:, :]))
                        sv = k.val("vq")
                        k.wait("sync", "vq", sv)
                        k.op("sync", lambda e, mo=mo, tok0=tok0: e.dma_start(
                            s_s12[mo * 128:(mo + 1) * 128, tok0:tok0 + NSUB],
                            s12t[:, :]), inc="st", n=16)
                    emit_rms_scale(k, nc, ps, h1, m2, xsq, rms_, rrs, c_ones_col,
                                   c_ones_row, c_ln2, D)
                    m2v = k.val("vq")
                    k.wait("sync", "vq", m2v)
                    for kk in range(16):
                        k.op("sync", lambda e, kk=kk, tok0=tok0: e.dma_start(
                            s_m2[kk * 128:(kk + 1) * 128, tok0:tok0 + NSUB],
                            m2[:, kk * NSUB:(kk + 1) * NSUB]), inc="st", n=16)
                    k.wait("vector", "st")
                k.flush()

        def mlp_phase():
            pes = ExitStack()
            with pes:
                m2 = sb("ml_m2", 128, 16 * NSUB, BF16, pes)
                wg = [sb(f"ml_wg{i}", 128, 16 * 128, BF16, pes) for i in range(2)]
                wu = [sb(f"ml_wu{i}", 128, 16 * 128, BF16, pes) for i in range(2)]
                wd = [sb(f"ml_wd{i}", 128, 64 * 128, BF16, pes) for i in range(2)]
                i_sb = sb("ml_i", 128, 64 * NSUB, BF16, pes)
                sg = sb("ml_sg", 128, NSUB, F32, pes)
                s12t = sb("ml_s12t", 128, NSUB, F32, pes)
                x2t = [sb(f"ml_x2t{i}", 128, NSUB, F32, pes) for i in range(2)]

                gu_slot_mm = {0: 0, 1: 0}
                wd_slot_mm = {0: 0, 1: 0}
                for ncs in range(TOKC // NSUB):
                    tok0 = ncs * NSUB
                    mv = None
                    k.wait("sync", "vq")
                    k.wait("sync", "mm")
                    for kk in range(16):
                        mv = k.op("sync", lambda e, kk=kk, tok0=tok0: e.dma_start(
                            m2[:, kk * NSUB:(kk + 1) * NSUB],
                            s_m2[kk * 128:(kk + 1) * 128, tok0:tok0 + NSUB]),
                            inc="ld", n=16)
                    k.wait("tensor", "ld", mv)
                    k.wait("vector", "mm")
                    for mi in range(INTER // 128):
                        slot = mi % 2
                        k.wait("gpsimd", "mm", gu_slot_mm[slot])
                        gv = k.op("gpsimd", lambda e, mi=mi, slot=slot: e.dma_start(
                            wg[slot][:, :].rearrange("p (kb m) -> p kb m", m=128),
                            p_gate[:, mi * 128:(mi + 1) * 128]
                            .rearrange("(kb p) m -> p kb m", p=128)),
                            inc="gld", n=16)
                        uv2 = k.op("gpsimd", lambda e, mi=mi, slot=slot: e.dma_start(
                            wu[slot][:, :].rearrange("p (kb m) -> p kb m", m=128),
                            p_up[:, mi * 128:(mi + 1) * 128]
                            .rearrange("(kb p) m -> p kb m", p=128)),
                            inc="gld", n=16)
                        k.wait("tensor", "gld", uv2)
                        pg = ps[mi % 2]
                        pu = ps[2 + mi % 2]
                        if mi >= 2:
                            k.wait("tensor", "vq")
                        for kk in range(16):
                            k.op("tensor", lambda t, kk=kk, slot=slot, pg=pg: t.matmul(
                                pg[:, :], wg[slot][:, kk * 128:(kk + 1) * 128],
                                m2[:, kk * NSUB:(kk + 1) * NSUB],
                                start=(kk == 0), stop=(kk == 15)),
                                inc="mm" if kk == 15 else None)
                        for kk in range(16):
                            k.op("tensor", lambda t, kk=kk, slot=slot, pu=pu: t.matmul(
                                pu[:, :], wu[slot][:, kk * 128:(kk + 1) * 128],
                                m2[:, kk * NSUB:(kk + 1) * NSUB],
                                start=(kk == 0), stop=(kk == 15)),
                                inc="mm" if kk == 15 else None)
                        gu_slot_mm[slot] = k.val("mm")
                        k.wait("scalar", "mm")
                        k.wait("scalar", "vq")
                        s_op(k, lambda s, pg=pg: s.activation(sg[:, :], pg[:, :],
                                                              AF.Silu), dep=True)
                        k.wait("vector", "sq")
                        v_op(k, lambda v, mi=mi, pu=pu: v.tensor_mul(
                            i_sb[:, mi * NSUB:(mi + 1) * NSUB], sg[:, :], pu[:, :]))
                    iv = k.val("vq")
                    k.wait("tensor", "vq", iv)
                    for mo2 in range(D // 128):
                        slot = mo2 % 2
                        k.wait("gpsimd", "mm", wd_slot_mm[slot])
                        dv = k.op("gpsimd", lambda e, mo2=mo2, slot=slot: e.dma_start(
                            wd[slot][:, :].rearrange("p (kb m) -> p kb m", m=128),
                            p_down[:, mo2 * 128:(mo2 + 1) * 128]
                            .rearrange("(kb p) m -> p kb m", p=128)),
                            inc="gld", n=16)
                        k.wait("tensor", "gld", dv)
                        pb = ps[4 + mo2 % 2]
                        if mo2 >= 2:
                            k.wait("tensor", "vq")
                        for mi in range(64):
                            k.op("tensor", lambda t, mi=mi, slot=slot, pb=pb: t.matmul(
                                pb[:, :], wd[slot][:, mi * 128:(mi + 1) * 128],
                                i_sb[:, mi * NSUB:(mi + 1) * NSUB],
                                start=(mi == 0), stop=(mi == 63)),
                                inc="mm" if mi == 63 else None)
                        wd_slot_mm[slot] = k.val("mm")
                        k.wait("sync", "vq")
                        sv12 = k.op("sync", lambda e, mo2=mo2, tok0=tok0: e.dma_start(
                            s12t[:, :], s_s12[mo2 * 128:(mo2 + 1) * 128,
                                              tok0:tok0 + NSUB]), inc="ld", n=16)
                        k.wait("vector", "mm")
                        k.wait("vector", "ld", sv12)
                        k.wait("vector", "st")
                        v_op(k, lambda v, mo2=mo2, pb=pb: v.tensor_add(
                            x2t[mo2 % 2][:, :], s12t[:, :], pb[:, :]), dep=True)
                        xv2 = k.val("vq")
                        k.wait("sync", "vq", xv2)
                        k.op("sync", lambda e, mo2=mo2, tok0=tok0: e.dma_start(
                            s_x2[mo2 * 128:(mo2 + 1) * 128, tok0:tok0 + NSUB],
                            x2t[mo2 % 2][:, :]), inc="st", n=16)
                k.flush()

        phase0()
        if dbg_sb is not None:
            k.barrier(exclude=("pc",))
            k.op("sync", lambda e: e.dma_start(dbg_sb[0:128, 0:8], c_cc[:, :]),
                 inc="st", n=16)
            k.op("sync", lambda e: e.dma_start(dbg_sb[0:8, 8:1032], c_kneg[:, :]),
                 inc="st", n=16)
            k.op("sync", lambda e: e.dma_start(dbg_sb[0:128, 1032:1040], c_rbb[:, :]),
                 inc="st", n=16)
            k.op("sync", lambda e: e.dma_start(dbg_sb[0:128, 1040:1056], c_ucbh[:, :]),
                 inc="st", n=16)
            k.op("sync", lambda e: e.dma_start(dbg_sb[0:128, 1056:1120],
                                               c_aneg[:, :]), inc="st", n=16)
        k.barrier(exclude=("pc",))
        ffn_phase(x0, call=1)
        if stop_after != "p1":
            k.barrier()
            mixer_phase()
            if stop_after != "p3":
                k.barrier()
                outproj_phase()
                k.barrier()
                mlp_phase()
                if stop_after != "p5":
                    k.barrier()
                    ffn_phase(s_x2, call=2)
        for dst, srct in dbg_outs:
            k.barrier()
            rows = srct.shape[0]
            r = 0
            while r < rows:
                rr = min(2048, rows - r)
                k.op("sync", lambda e, d=dst, s=srct, r=r, rr=rr: e.dma_start(
                    d[r:r + rr, :], s[r:r + rr, :]), inc="st", n=16)
                r += rr
        k.wait("sync", "st")
        k.wait("sync", "ld")
        k.wait("gpsimd", "gld")
        k.wait("gpsimd", "pc")
        k.flush()

    return nc


def _host_prep(inputs):
    gi = lambda name: np.asarray(inputs[name])
    x = gi("L0_attn_hidden_states")

    base = {}
    base["ffn_down_w"] = np.ascontiguousarray(gi("L0_ffn_down_w"))
    base["ffn_up_w"] = np.ascontiguousarray(gi("L0_ffn_up_w"))
    base["ffn_router_w"] = np.ascontiguousarray(gi("L0_ffn_router_w"))
    base["ffn_shared_w1"] = np.ascontiguousarray(gi("L0_ffn_shared_w1"))
    base["ffn_shared_w2"] = np.ascontiguousarray(gi("L0_ffn_shared_w2"))
    base["ffn_core_w"] = np.ascontiguousarray(gi("L0_ffn_core_w"))
    for i in range(EXN):
        base[f"e{i}_w1"] = np.ascontiguousarray(gi(f"L0_ffn_e{i}_w1"))
        base[f"e{i}_w2"] = np.ascontiguousarray(gi(f"L0_ffn_e{i}_w2"))

    def col128(vec):
        return np.ascontiguousarray(np.asarray(vec).reshape(-1, 128).T)

    base["down_b128"] = col128(gi("L0_ffn_down_b"))
    for i in range(EXN):
        base[f"b1c{i}"] = col128(gi(f"L0_ffn_e{i}_b1"))
    b2all = np.zeros((128, (LF // 128) * EXN), np.float32)
    for e in range(EXN):
        c = col128(gi(f"L0_ffn_e{e}_b2"))
        for m in range(LF // 128):
            b2all[:, m * EXN + e] = c[:, m]
    base["b2all"] = b2all
    base["sb1_128"] = col128(gi("L0_ffn_shared_b1"))
    base["sb2_128"] = col128(gi("L0_ffn_shared_b2"))
    base["upb_128"] = col128(gi("L0_ffn_up_b"))
    base["coreb_128"] = col128(gi("L0_ffn_core_b"))
    base["rb_row"] = np.ascontiguousarray(
        gi("L0_ffn_router_b").reshape(1, EXN).astype(np.float32))
    cost = np.array([2 * HID[i] * LF for i in range(EXN)], np.float32)
    base["cost_row"] = (COST_LAMBDA * cost).reshape(1, EXN).astype(np.float32)

    base["gatewT"] = np.ascontiguousarray(gi("L0_attn_gate_w").T)
    base["upwT"] = np.ascontiguousarray(gi("L0_attn_up_w").T)
    base["downwT"] = np.ascontiguousarray(gi("L0_attn_down_w").T)
    base["opwT"] = np.ascontiguousarray(gi("L0_attn_out_proj_w").T)
    base["ln1_128"] = col128(gi("L0_attn_ln1_w"))
    base["ln2_128"] = col128(gi("L0_attn_ln2_w"))

    base["ones_col"] = np.ones((128, 1), np.float32)
    base["ones_row"] = np.ones((1, 128), np.float32)
    base["ident"] = np.eye(128, dtype=np.float32)
    oh8 = np.zeros((EXN, EXN * 128), np.float32)
    for e in range(EXN):
        oh8[e, e * 128:(e + 1) * 128] = 1.0
    base["oh8"] = oh8
    selx = np.zeros((32, 128), np.float32)
    for p in range(128):
        selx[(p // 64) * 16 + (p % 16), p] = 1.0
    base["selx"] = np.concatenate([selx, selx], axis=0)
    selB = np.zeros((32, NST * 128), np.float32)
    selC = np.zeros((128, NST * 128), np.float32)
    for n in range(NST):
        for p in range(128):
            selB[(p // 64) * 16 + n, n * 128 + p] = 1.0
            selC[(p // 16) * 16 + n, n * 128 + p] = 1.0
    base["selB_all"] = np.concatenate([selB, selB], axis=0)
    base["selC_all"] = selC

    ipw = gi("L0_attn_in_proj_w")
    dtw = gi("L0_attn_dt_proj_w")
    alog = gi("L0_attn_A_log")
    dpar = gi("L0_attn_D_param")
    cw = gi("L0_attn_conv_w")
    cb = gi("L0_attn_conv_b")
    dtb = gi("L0_attn_dt_proj_b")

    in_maps = []
    for c in range(NC):
        m = dict(base)
        bb, tt = c // 4, (c % 4) * TOKC
        m["x0"] = np.ascontiguousarray(x[bb, tt:tt + TOKC, :].T)
        d0 = c * DC
        rows = np.concatenate([
            np.arange(d0, d0 + DC),
            4096 + np.arange(c * 128, (c + 1) * 128),
            5120 + np.arange(c * 128, (c + 1) * 128),
            6144 + np.arange(d0, d0 + DC),
            np.arange(10240, 10368),
        ])
        m["ipwT"] = np.ascontiguousarray(ipw[rows, :].T)
        m["dtwT"] = np.ascontiguousarray(dtw[d0:d0 + DC, :].T)

        def loc128(vec2):
            v = vec2.reshape(DC // 128, 128, -1)
            return np.ascontiguousarray(v.transpose(1, 0, 2).reshape(128, -1))
        m["alog128"] = loc128(alog[d0:d0 + DC, :])
        m["d128"] = loc128(dpar[d0:d0 + DC].reshape(-1, 1))
        m["convw128"] = loc128(cw[d0:d0 + DC, 0, :])
        m["convb128"] = loc128(cb[d0:d0 + DC].reshape(-1, 1))
        m["dtb128"] = loc128(dtb[d0:d0 + DC].reshape(-1, 1))
        in_maps.append(m)
    return in_maps


def kernel(**inputs):
    in_maps = _host_prep(inputs)
    nc = build_program()
    res = run_bass_kernel_spmd(nc, in_maps, core_ids=list(range(NC)))
    out = np.zeros((B, T, D), np.float32)
    for c in range(NC):
        bb, tt = c // 4, (c % 4) * TOKC
        out[bb, tt:tt + TOKC, :] = np.asarray(res.results[c]["xout"]).T
    return out


# version 04 (final)
# speedup vs baseline: 1.2101x; 1.0423x over previous
import os
os.environ.setdefault("NEURON_SCRATCHPAD_PAGE_SIZE", "2048")

import numpy as np
import ml_dtypes
from contextlib import ExitStack

import concourse.bass as bass
import concourse.mybir as mybir
from concourse.bass_utils import run_bass_kernel_spmd

F32 = mybir.dt.float32
BF16 = mybir.dt.bfloat16
AF = mybir.ActivationFunctionType
ALU = mybir.AluOpType
AX = mybir.AxisListType

D = 2048
DI = 4096
DXB = 1024
NST = 16
DCONV = 4
DTR = 128
INTER = 8192
LF = 1024
EXN = 8
HID = [1024, 2048, 3072, 4096, 1024, 2048, 3072, 4096]
COST_LAMBDA = 1e-7
EPS = 1e-5

B = 2
T = 16384
NC = 8
TOKC = 4096
DC = 512
NSUB = 512
SUP = 1024
CH = 512

ENGS = ("sync", "gpsimd", "tensor", "scalar", "vector")
SEMS = ("ld", "gld", "st", "mm", "ev", "sq", "vq", "gq", "cc", "pc")


class KB:
    def __init__(self, nc, block, es):
        self.nc, self.block, self.es = nc, block, es
        self.q = {e: [] for e in ENGS}
        self.sems = {}

    def S(self, name):
        if name not in self.sems:
            h = self.es.enter_context(self.nc.semaphore(name))
            self.sems[name] = [h, 0]
        return self.sems[name]

    def val(self, name):
        return self.S(name)[1]

    def op(self, eng, fn, inc=None, n=1):
        if inc is not None:
            s = self.S(inc)
            s[1] += n

            def run(h, fn=fn, sh=s[0], n=n):
                fn(h).then_inc(sh, n)
        else:
            def run(h, fn=fn):
                fn(h)
        self.q[eng].append(run)
        return self.val(inc) if inc else None

    def raw(self, eng, fn):
        self.q[eng].append(fn)

    def wait(self, eng, name, val=None):
        s = self.S(name)
        v = s[1] if val is None else val
        self.q[eng].append(lambda h, sh=s[0], v=v: h.wait_ge(sh, v))

    def barrier(self, exclude=()):
        for e in ENGS:
            for snm in SEMS:
                if snm in self.sems and snm not in exclude:
                    self.wait(e, snm)

    def flush(self):
        for e in ENGS:
            fns = self.q[e]
            if not fns:
                continue
            self.q[e] = []

            def runner(h, fns=fns):
                for f in fns:
                    f(h)
            getattr(self.block, e)(runner)


def scope_guard(k, engs=("gpsimd", "sync")):
    for e in engs:
        for snm in ("mm", "vq", "sq", "ev", "st"):
            if snm in k.sems:
                k.wait(e, snm)


def v_op(k, fn, dep=False):
    if dep:
        k.wait("vector", "vq")
    k.op("vector", fn, inc="vq")


def s_op(k, fn, dep=False):
    if dep:
        k.wait("scalar", "sq")
    k.op("scalar", fn, inc="sq")


def build_program(stop_after=None, debug=()):
    nc = bass.Bass("TRN2", target_bir_lowering=False)

    def inp(name, shape):
        return nc.dram_tensor(name, list(shape), F32, kind="ExternalInput").ap()

    x0 = inp("x0", (D, TOKC))
    ffn_down_w = inp("ffn_down_w", (D, LF))
    ffn_up_w = inp("ffn_up_w", (LF, D))
    ffn_router_w = inp("ffn_router_w", (D, EXN))
    ffn_shared_w1 = inp("ffn_shared_w1", (LF, LF))
    ffn_shared_w2 = inp("ffn_shared_w2", (LF, LF))
    ffn_core_w = inp("ffn_core_w", (D, D))
    e_w1 = [inp(f"e{i}_w1", (LF, HID[i])) for i in range(EXN)]
    e_w2 = [inp(f"e{i}_w2", (HID[i], LF)) for i in range(EXN)]
    down_b128 = inp("down_b128", (128, LF // 128))
    b1c = [inp(f"b1c{i}", (128, HID[i] // 128)) for i in range(EXN)]
    b2all = inp("b2all", (128, (LF // 128) * EXN))
    sb1_128 = inp("sb1_128", (128, LF // 128))
    sb2_128 = inp("sb2_128", (128, LF // 128))
    upb_128 = inp("upb_128", (128, D // 128))
    coreb_128 = inp("coreb_128", (128, D // 128))
    rb_row = inp("rb_row", (1, EXN))
    cost_row = inp("cost_row", (1, EXN))
    ipwT = inp("ipwT", (D, 1408))
    dtwT = inp("dtwT", (DTR, DC))
    opwT = inp("opwT", (DI, D))
    gatewT = inp("gatewT", (D, INTER))
    upwT = inp("upwT", (D, INTER))
    downwT = inp("downwT", (INTER, D))
    alog128 = inp("alog128", (128, (DC // 128) * NST))
    d128 = inp("d128", (128, DC // 128))
    convw128 = inp("convw128", (128, (DC // 128) * DCONV))
    convb128 = inp("convb128", (128, DC // 128))
    dtb128 = inp("dtb128", (128, DC // 128))
    ln1_128 = inp("ln1_128", (128, D // 128))
    ln2_128 = inp("ln2_128", (128, D // 128))
    ones_col = inp("ones_col", (128, 1))
    ones_row = inp("ones_row", (1, 128))
    ident = inp("ident", (128, 128))
    oh8 = inp("oh8", (EXN, EXN * 128))
    selx = inp("selx", (64, 128))
    selB_all = inp("selB_all", (64, NST * 128))
    selC_all = inp("selC_all", (128, NST * 128))

    def scratch(name, shape, dt=F32):
        return nc.dram_tensor(name, list(shape), dt, kind="Internal").ap()

    g_min_in = scratch("g_min_in", (D, TOKC), BF16)
    g_min_out = nc.dram_tensor("g_min_out", [NC * D, TOKC], BF16, kind="Internal",
                               addr_space="Shared").ap()
    g_yg_in = scratch("g_yg_in", (B * DC, T), BF16)
    g_yg_out = nc.dram_tensor("g_yg_out", [NC * B * DC, T], BF16, kind="Internal",
                              addr_space="Shared").ap()
    s_ymy = scratch("s_ymy", (DI, TOKC), BF16)
    s_x1 = scratch("s_x1", (D, TOKC))
    s_u = scratch("s_u", (D, TOKC))
    s_xlnbf = scratch("s_xlnbf", (D, TOKC), BF16)
    s_cx = scratch("s_cx", (D, TOKC), BF16)
    s_x2 = scratch("s_x2", (D, TOKC))
    s_m2 = scratch("s_m2", (D, TOKC), BF16)
    s_dbgx = scratch("s_dbgx", (D, NSUB))
    s_kneg = scratch("s_kneg", (EXN, LF))
    s_s12 = scratch("s_s12", (D, TOKC))
    p_ew1 = [scratch(f"p_e{i}w1", (LF, HID[i]), BF16) for i in range(EXN)]
    p_ew2 = [scratch(f"p_e{i}w2", (HID[i], LF), BF16) for i in range(EXN)]
    p_fup = scratch("p_fup", (LF, D), BF16)
    p_fcore = scratch("p_fcore", (D, D), BF16)
    p_opw = scratch("p_opw", (DI, D), BF16)
    p_gate = scratch("p_gate", (D, INTER), BF16)
    p_up = scratch("p_up", (D, INTER), BF16)
    p_down = scratch("p_down", (INTER, D), BF16)
    xout = nc.dram_tensor("xout", [D, TOKC], F32, kind="ExternalOutput").ap()
    dbg_sb = (nc.dram_tensor("dbg_sb", [128, 2100], F32, kind="ExternalOutput").ap()
              if "p0" in debug else None)

    dbg_outs = []
    srcs = {"x1": s_x1, "u": s_u, "minin": g_min_in, "minout": g_min_out,
            "yg": g_yg_in, "ygout": g_yg_out, "ymy": s_ymy, "x2": s_x2,
            "xlnbf": s_xlnbf, "cx": s_cx, "dbgx": s_dbgx, "m2": s_m2, "s12": s_s12}
    for nm in debug:
        if nm == "p0":
            continue
        srct = srcs[nm]
        dbg_outs.append((nc.dram_tensor("dbg_" + nm, list(srct.shape),
                                        srct.dtype, kind="ExternalOutput").ap(), srct))

    _dbgx_done = []
    es = ExitStack()
    with es:
        block = es.enter_context(nc.Block())
        k = KB(nc, block, es)
        ps = [es.enter_context(nc.psum_tensor(f"ps{i}", [128, 512], F32))
              for i in range(8)]

        def sb(name, p, f, dt=F32, st=es):
            return st.enter_context(nc.sbuf_tensor(name, [p, f], dt))

        c_ones_col = sb("c_ones_col", 128, 1)
        c_ones_row = sb("c_ones_row", 1, 128)
        c_selx = sb("c_selx", 64, 128)
        c_selB = sb("c_selB", 64, NST * 128, BF16)
        c_selC = sb("c_selC", 128, NST * 128, BF16)
        c_downb = sb("c_downb", 128, LF // 128)
        c_b1 = [sb(f"c_b1_{i}", 128, HID[i] // 128) for i in range(EXN)]
        c_gb1 = [sb(f"c_gb1_{i}", 128, HID[i] // 128, BF16) for i in range(EXN)]
        c_sb1 = sb("c_sb1", 128, LF // 128)
        c_rbb = sb("c_rbb", 128, EXN)
        c_cc = sb("c_cc", 128, LF // 128)
        c_ucbh = sb("c_ucbh", 128, D // 128)
        c_aneg = sb("c_aneg", 128, (DC // 128) * NST)
        c_d128 = sb("c_d128", 128, DC // 128)
        c_convw = sb("c_convw", 128, (DC // 128) * DCONV)
        c_convb = sb("c_convb", 128, DC // 128)
        c_dtb = sb("c_dtb", 128, DC // 128)
        c_ln1 = sb("c_ln1", 128, D // 128)
        c_ln2 = sb("c_ln2", 128, D // 128)
        c_eps = sb("c_eps", 128, 1)
        c_mhalf = sb("c_mhalf", 128, 1)

        def phase0():
            p0es = ExitStack()
            v_op(k, lambda v: v.memset(c_eps[:, :], EPS))
            v_op(k, lambda v: v.memset(c_mhalf[:, :], -0.5))
            def precast(dst, srcap, rows):
                step = max(1, (4 << 20) // max(1, int(np.prod(srcap.shape[1:]) * 4)))
                r = 0
                while r < rows:
                    rr = min(step, rows - r)
                    k.op("gpsimd", lambda e, d=dst, s=srcap, r=r, rr=rr:
                         e.dma_start(d[r:r + rr, :], s[r:r + rr, :]), inc="pc", n=16)
                    r += rr
            for i in range(EXN):
                precast(p_ew1[i], e_w1[i], LF)
                precast(p_ew2[i], e_w2[i], HID[i])
            precast(p_fup, ffn_up_w, LF)
            precast(p_fcore, ffn_core_w, D)
            precast(p_opw, opwT, DI)
            precast(p_gate, gatewT, D)
            precast(p_up, upwT, D)
            precast(p_down, downwT, INTER)

            p0_ident = sb("p0_ident", 128, 128, F32, p0es)
            loads = [
                (c_ones_col, ones_col), (c_ones_row, ones_row), (p0_ident, ident),
                (c_selx, selx), (c_downb, down_b128), (c_sb1, sb1_128),
                (c_d128, d128), (c_convw, convw128), (c_convb, convb128),
                (c_dtb, dtb128), (c_ln1, ln1_128), (c_ln2, ln2_128),
            ]
            for dst, srcd in loads:
                k.op("sync", lambda e, d=dst, s=srcd: e.dma_start(d[:, :], s[:, :]),
                     inc="ld", n=16)
            for i in range(EXN):
                k.op("sync", lambda e, d=c_b1[i], s=b1c[i]: e.dma_start(d[:, :], s[:, :]),
                     inc="ld", n=16)
            for dst, srcd in [(c_selB, selB_all), (c_selC, selC_all)]:
                if dst.dtype == BF16:
                    k.op("gpsimd", lambda e, d=dst, s=srcd: e.dma_start(d[:, :], s[:, :]),
                         inc="gld", n=16)
                else:
                    k.op("sync", lambda e, d=dst, s=srcd: e.dma_start(d[:, :], s[:, :]),
                         inc="ld", n=16)
            t_rb = sb("t_rb", 1, EXN, F32, p0es)
            t_cost = sb("t_cost", 1, EXN, F32, p0es)
            t_rbeff = sb("t_rbeff", 1, EXN, F32, p0es)
            t_upb = sb("t_upb", 128, D // 128, F32, p0es)
            t_coreb = sb("t_coreb", 128, D // 128, F32, p0es)
            t_ucs = sb("t_ucs", 128, D // 128, F32, p0es)
            t_b2all = sb("t_b2all", 128, (LF // 128) * EXN, F32, p0es)
            t_sb2 = sb("t_sb2", 128, LF // 128, F32, p0es)
            t_sb2t = sb("t_sb2t", 128, LF // 128, F32, p0es)
            t_ksum = sb("t_ksum", 1, LF, F32, p0es)
            t_krow = sb("t_krow", 1, LF, F32, p0es)
            t_alog = sb("t_alog", 128, (DC // 128) * NST, F32, p0es)
            t_aexp = sb("t_aexp", 128, (DC // 128) * NST, F32, p0es)
            for dst, srcd in [(t_rb, rb_row), (t_cost, cost_row), (t_upb, upb_128),
                              (t_coreb, coreb_128), (t_b2all, b2all),
                              (t_sb2, sb2_128), (t_alog, alog128)]:
                k.op("sync", lambda e, d=dst, s=srcd: e.dma_start(d[:, :], s[:, :]),
                     inc="ld", n=16)

            k.wait("vector", "ld")
            k.wait("vector", "gld")
            v_op(k, lambda v: v.tensor_sub(t_rbeff[:, :], t_rb[:, :], t_cost[:, :]))
            k.wait("tensor", "vq")
            k.op("tensor", lambda t: t.matmul(ps[0][:, 0:EXN], c_ones_row[:, :],
                                              t_rbeff[:, :], start=True, stop=True),
                 inc="mm")
            k.wait("vector", "mm")
            v_op(k, lambda v: v.tensor_copy(c_rbb[:, :], ps[0][:, 0:EXN]))
            v_op(k, lambda v: v.tensor_add(t_ucs[:, :], t_upb[:, :], t_coreb[:, :]))
            k.wait("vector", "vq")
            v_op(k, lambda v: v.tensor_scalar(c_ucbh[:, :], t_ucs[:, :], 0.5,
                                              None, op0=ALU.mult))
            k.wait("scalar", "ld")
            for i in range(EXN):
                s_op(k, lambda s, i=i: s.activation(c_gb1[i][:, :], c_b1[i][:, :],
                                                    AF.Gelu))
            s_op(k, lambda s: s.activation(t_aexp[:, :], t_alog[:, :], AF.Exp))
            s_op(k, lambda s: s.mul(c_aneg[:, :], t_aexp[:, :], -1.0), dep=True)
            v_op(k, lambda v: v.memset(t_ksum[:, :], 0.0))
            wbuf0 = sb("wbuf0", 128, LF, BF16, p0es)
            gb_done = k.val("sq")
            for ei in range(EXN):
                ke = HID[ei] // 128
                for mb in range(ke):
                    gv = k.op("gpsimd", lambda e, ei=ei, mb=mb: e.dma_start(
                        wbuf0[:, :], e_w2[ei][mb * 128:(mb + 1) * 128, :]),
                        inc="gld", n=16)
                    k.wait("tensor", "gld", gv)
                    k.wait("tensor", "sq")
                    last = (mb == ke - 1)
                    for half in range(2):
                        k.op("tensor", lambda t, ei=ei, mb=mb, half=half, ke=ke:
                             t.matmul(ps[half][0:1, :],
                                      c_gb1[ei][:, mb:mb + 1],
                                      wbuf0[:, half * 512:(half + 1) * 512],
                                      start=(mb == 0), stop=(mb == ke - 1)),
                             inc="mm" if (last and half == 1) else None)
                    if not last:
                        k.op("tensor", lambda t: t.matmul(
                            ps[2][0:1, 0:1], c_ones_row[0:1, 0:1],
                            c_ones_row[0:1, 0:1], start=True, stop=True), inc="mm")
                    k.wait("gpsimd", "mm")
                k.wait("scalar", "mm")
                k.wait("scalar", "ld")
                k.wait("scalar", "st")
                s_op(k, lambda s: s.activation(t_krow[0:1, 0:512], ps[0][0:1, :],
                                               AF.Copy, bias=0.0, scale=-1.0))
                s_op(k, lambda s: s.activation(t_krow[0:1, 512:1024], ps[1][0:1, :],
                                               AF.Copy, bias=0.0, scale=-1.0))
                k.wait("vector", "sq")
                v_op(k, lambda v: v.tensor_sub(t_ksum[:, :], t_ksum[:, :],
                                               t_krow[:, :]), dep=True)
                k.wait("sync", "sq")
                k.op("sync", lambda e, ei=ei: e.dma_start(
                    s_kneg[ei:ei + 1, :], t_krow[:, :]), inc="st", n=16)
            v_op(k, lambda v: v.tensor_reduce(
                c_cc[:, :].rearrange("p (m one) -> p m one", one=1),
                t_b2all[:, :].rearrange("p (m e) -> p m e", e=EXN),
                op=ALU.add, axis=AX.X), dep=True)
            v_op(k, lambda v: v.tensor_scalar(t_sb2t[:, :], t_sb2[:, :], 0.1,
                                              None, op0=ALU.mult), dep=True)
            k.wait("vector", "sq")
            v_op(k, lambda v: v.tensor_add(c_cc[:, :], c_cc[:, :], t_sb2t[:, :]),
                 dep=True)
            for mo in range(LF // 128):
                k.wait("tensor", "vq")
                k.op("tensor", lambda t, mo=mo: t.matmul(
                    ps[3][:, 0:1], t_ksum[0:1, mo * 128:(mo + 1) * 128],
                    c_ones_row[0:1, 0:1], start=True, stop=True), inc="mm")
                k.wait("vector", "mm")
                v_op(k, lambda v, mo=mo: v.tensor_add(
                    c_cc[:, mo:mo + 1], c_cc[:, mo:mo + 1], ps[3][:, 0:1]), dep=True)
            k.flush()
            p0es.close()

        def ffn_phase(src, call):
            out_t = s_x1 if call == 1 else xout
            pes = ExitStack()
            with pes:
                f_ident = sb(f"f{call}_id", 128, 128, F32, pes)
                f_oh8 = sb(f"f{call}_oh", EXN, EXN * 128, F32, pes)
                f_kneg = sb(f"f{call}_kn", EXN, LF, F32, pes)
                k.op("sync", lambda e: e.dma_start(f_ident[:, :], ident[:, :]),
                     inc="ld", n=16)
                k.op("sync", lambda e: e.dma_start(f_oh8[:, :], oh8[:, :]),
                     inc="ld", n=16)
                k.wait("sync", "st")
                k.op("sync", lambda e: e.dma_start(f_kneg[:, :], s_kneg[:, :]),
                     inc="ld", n=16)
                k.wait("tensor", "ld")
                for sup in range(TOKC // SUP):
                    spes = ExitStack()
                    with spes:
                        accb = sb(f"f{call}s{sup}_acc", 128, 8 * SUP, F32, spes)
                        wfm = sb(f"f{call}s{sup}_wfm", EXN, SUP, F32, spes)
                        for _e in ("sync", "gpsimd", "tensor", "scalar", "vector"):
                            for _s in ("mm", "vq", "sq", "st"):
                                k.wait(_e, _s)
                        v_op(k, lambda v: v.memset(accb[:, :], 0.0))
                        for m in range(LF // 128):
                            v_op(k, lambda v, m=m: v.tensor_scalar(
                                accb[:, m * SUP:(m + 1) * SUP],
                                accb[:, m * SUP:(m + 1) * SUP],
                                c_cc[:, m:m + 1], None, op0=ALU.add), dep=True)
                        apes = ExitStack()
                        with apes:
                            gHs = sb(f"f{call}s{sup}_gH", 128, 8 * SUP, BF16, apes)
                            a2 = ExitStack()
                            with a2:
                                dwn_sb = sb(f"fA{call}{sup}_dwn", 128, 16 * LF, BF16, a2)
                                rtr_sb = sb(f"fA{call}{sup}_rtr", 128, 16 * EXN, BF16, a2)
                                k.wait("gpsimd", "mm")
                                k.op("gpsimd", lambda e: e.dma_start(
                                    dwn_sb[:, :].rearrange("p (kb m) -> p kb m", m=LF),
                                    ffn_down_w.rearrange("(kb p) m -> p kb m", p=128)),
                                    inc="gld", n=16)
                                k.op("gpsimd", lambda e: e.dma_start(
                                    rtr_sb[:, :].rearrange("p (kb m) -> p kb m", m=EXN),
                                    ffn_router_w.rearrange("(kb p) m -> p kb m", p=128)),
                                    inc="gld", n=16)
                                x0_sb = sb(f"fA{call}{sup}_x0", 128, 16 * NSUB, F32, a2)
                                xln = sb(f"fA{call}{sup}_xln", 128, 16 * NSUB, BF16, a2)
                                cxl = sb(f"fA{call}{sup}_cx", 128, 16 * NSUB, BF16, a2)
                                tmp = sb(f"fA{call}{sup}_tmp", 128, NSUB, F32, a2)
                                xsq = sb(f"fA{call}{sup}_xsq", 128, NSUB, F32, a2)
                                rmu = sb(f"fA{call}{sup}_rmu", 1, NSUB, F32, a2)
                                rms_ = sb(f"fA{call}{sup}_rms", 1, NSUB, F32, a2)
                                rm2 = sb(f"fA{call}{sup}_rm2", 1, NSUB, F32, a2)
                                rva = sb(f"fA{call}{sup}_rva", 1, NSUB, F32, a2)
                                rrs = sb(f"fA{call}{sup}_rrs", 1, NSUB, F32, a2)
                                rmr = sb(f"fA{call}{sup}_rmr", 1, NSUB, F32, a2)
                                rr = [sb(f"fA{call}{sup}_r{i}", 128,
                                         EXN if i > 1 else 1, F32, a2)
                                      for i in range(12)]
                                r_sc = [sb(f"fA{call}{sup}_s{i}", 128, 1, F32, a2)
                                        for i in range(6)]
                                k.barrier() if sup == 0 and call == 2 else None
                                for ncs in range(SUP // NSUB):
                                    tok0 = sup * SUP + ncs * NSUB
                                    emit_ln_stage(k, nc, ps, src, tok0, x0_sb, xln,
                                                  cxl, tmp, xsq, rmu, rms_, rm2, rva,
                                                  rrs, rmr, c_ones_col, c_ones_row,
                                                  s_xlnbf, s_cx, store_ln=True)
                                    k.wait("tensor", "gld")
                                    k.wait("tensor", "vq")
                                    for m in range(LF // 128):
                                        pb = ps[m % 2]
                                        if m >= 2:
                                            k.wait("tensor", "sq")
                                        for kk in range(16):
                                            k.op("tensor", lambda t, m=m, kk=kk, pb=pb:
                                                 t.matmul(pb[:, :],
                                                          dwn_sb[:, kk * LF + m * 128:
                                                                 kk * LF + m * 128 + 128],
                                                          xln[:, kk * NSUB:(kk + 1) * NSUB],
                                                          start=(kk == 0), stop=(kk == 15)),
                                                 inc="mm" if kk == 15 else None)
                                        k.wait("scalar", "mm")
                                        s_op(k, lambda s, m=m, pb=pb, ncs=ncs: s.activation(
                                            gHs[:, m * SUP + ncs * NSUB:
                                                m * SUP + ncs * NSUB + NSUB],
                                            pb[:, :], AF.Gelu,
                                            bias=c_downb[:, m:m + 1], scale=1.0))
                                    emit_router(k, nc, ps, xln, rtr_sb, c_rbb, f_ident,
                                                c_ones_col, wfm, ncs, rr, r_sc)
                            bpes = ExitStack()
                            with bpes:
                                G_sb = sb(f"fB{call}{sup}_G", 128, 64 * NSUB, BF16, bpes)
                                w1b = [sb(f"fB{call}{sup}_w1{i}", 128, 8 * 128, BF16,
                                          bpes) for i in range(2)]
                                w2b = [sb(f"fB{call}{sup}_w2{i}", 128, 32 * 128, BF16,
                                          bpes) for i in range(2)]
                                wbs = [sb(f"fB{call}{sup}_wb{i}", 128, NSUB, F32, bpes)
                                       for i in range(2)]
                                gtmp = sb(f"fB{call}{sup}_gt", 128, NSUB, F32, bpes)
                                k.wait("gpsimd", "pc")
                                scope_guard(k)
                                bank_sq = {0: k.val("sq"), 1: k.val("sq")}
                                w1_slot_mm = {0: 0, 1: 0}
                                w2_slot_mm = {0: 0, 1: 0}
                                for ei in range(EXN):
                                    ke = HID[ei] // 128
                                    k.wait("tensor", "vq")
                                    k.wait("tensor", "sq")
                                    for h2 in range(2):
                                        k.op("tensor", lambda t, ei=ei, h2=h2: t.matmul(
                                            ps[4 + h2][:, :],
                                            f_oh8[:, ei * 128:(ei + 1) * 128],
                                            wfm[:, h2 * NSUB:(h2 + 1) * NSUB],
                                            start=True, stop=True), inc="mm")
                                    k.wait("vector", "mm")
                                    for h2 in range(2):
                                        v_op(k, lambda v, h2=h2: v.tensor_copy(
                                            wbs[h2][:, :], ps[4 + h2][:, :]))
                                    wbv = k.val("vq")
                                    for mb in range(ke):
                                        slot = mb % 2
                                        k.wait("gpsimd", "mm", w1_slot_mm[slot])
                                        gv = k.op("gpsimd", lambda e, ei=ei, mb=mb,
                                                  slot=slot: e.dma_start(
                                                      w1b[slot][:, :].rearrange(
                                                          "p (kb m) -> p kb m", m=128),
                                                      p_ew1[ei][:, mb * 128:(mb + 1) * 128]
                                                      .rearrange("(kb p) m -> p kb m",
                                                                 p=128)),
                                                  inc="gld", n=16)
                                        k.wait("tensor", "gld", gv)
                                        for h2 in range(2):
                                            pb = ps[h2]
                                            k.wait("tensor", "sq", bank_sq[h2])
                                            for kk in range(8):
                                                k.op("tensor", lambda t, kk=kk, slot=slot,
                                                     pb=pb, h2=h2: t.matmul(
                                                         pb[:, :],
                                                         w1b[slot][:, kk * 128:(kk + 1) * 128],
                                                         gHs[:, kk * SUP + h2 * NSUB:
                                                             kk * SUP + h2 * NSUB + NSUB],
                                                         start=(kk == 0), stop=(kk == 7)),
                                                     inc="mm" if kk == 7 else None)
                                            k.wait("scalar", "mm")
                                            k.wait("scalar", "vq")
                                            s_op(k, lambda s, ei=ei, mb=mb, pb=pb:
                                                 s.activation(gtmp[:, :], pb[:, :],
                                                              AF.Gelu,
                                                              bias=c_b1[ei][:, mb:mb + 1],
                                                              scale=1.0), dep=True)
                                            bank_sq[h2] = k.val("sq")
                                            w1_slot_mm[slot] = k.val("mm")
                                            k.wait("vector", "sq")
                                            v_op(k, lambda v, mb=mb, h2=h2: v.tensor_mul(
                                                G_sb[:, (2 * mb + h2) * NSUB:
                                                     (2 * mb + h2 + 1) * NSUB],
                                                gtmp[:, :], wbs[h2][:, :]))
                                    gdone = k.val("vq")
                                    for mo in range(LF // 128):
                                        slot = mo % 2
                                        k.wait("gpsimd", "mm", w2_slot_mm[slot])
                                        wv2 = k.op("gpsimd", lambda e, ei=ei, mo=mo,
                                                   slot=slot, ke=ke: e.dma_start(
                                                       w2b[slot][:, 0:ke * 128].rearrange(
                                                           "p (kb m) -> p kb m", m=128),
                                                       p_ew2[ei][:, mo * 128:(mo + 1) * 128]
                                                       .rearrange("(kb p) m -> p kb m",
                                                                  p=128)),
                                                   inc="gld", n=16)
                                        k.wait("tensor", "gld", wv2)
                                        k.wait("tensor", "vq", gdone)
                                        if mo >= 1:
                                            k.wait("tensor", "vq")
                                        for h2 in range(2):
                                            pb = ps[2 + h2]
                                            for mb in range(ke):
                                                k.op("tensor", lambda t, mb=mb, slot=slot,
                                                     pb=pb, h2=h2, ke=ke: t.matmul(
                                                         pb[:, :],
                                                         w2b[slot][:, mb * 128:(mb + 1) * 128],
                                                         G_sb[:, (2 * mb + h2) * NSUB:
                                                              (2 * mb + h2 + 1) * NSUB],
                                                         start=(mb == 0), stop=(mb == ke - 1)),
                                                     inc="mm" if mb == ke - 1 else None)
                                        w2_slot_mm[slot] = k.val("mm")
                                        k.wait("vector", "mm")
                                        for h2 in range(2):
                                            v_op(k, lambda v, mo=mo, h2=h2: v.tensor_add(
                                                accb[:, mo * SUP + h2 * NSUB:
                                                     mo * SUP + h2 * NSUB + NSUB],
                                                accb[:, mo * SUP + h2 * NSUB:
                                                     mo * SUP + h2 * NSUB + NSUB],
                                                ps[2 + h2][:, :]), dep=(h2 == 0))
                            cpes = ExitStack()
                            with cpes:
                                s1b = sb(f"fC{call}{sup}_s1", 128, 8 * NSUB, BF16, cpes)
                                shb = [sb(f"fC{call}{sup}_sh{i}", 128, 8 * 128, BF16,
                                          cpes) for i in range(2)]
                                scope_guard(k)
                                sh_slot_mm = {0: 0, 1: 0}
                                for h2 in range(SUP // NSUB):
                                    for mo in range(LF // 128):
                                        slot = mo % 2
                                        k.wait("gpsimd", "mm", sh_slot_mm[slot])
                                        sv = k.op("gpsimd", lambda e, mo=mo, slot=slot:
                                                  e.dma_start(
                                                      shb[slot][:, :].rearrange(
                                                          "p (kb m) -> p kb m", m=128),
                                                      ffn_shared_w1[:, mo * 128:(mo + 1) * 128]
                                                      .rearrange("(kb p) m -> p kb m", p=128)),
                                                  inc="gld", n=16)
                                        k.wait("tensor", "gld", sv)
                                        pb = ps[mo % 2]
                                        k.wait("tensor", "sq")
                                        k.wait("tensor", "vq")
                                        for kk in range(8):
                                            k.op("tensor", lambda t, kk=kk, slot=slot,
                                                 pb=pb, h2=h2: t.matmul(
                                                     pb[:, :],
                                                     shb[slot][:, kk * 128:(kk + 1) * 128],
                                                     gHs[:, kk * SUP + h2 * NSUB:
                                                         kk * SUP + h2 * NSUB + NSUB],
                                                     start=(kk == 0), stop=(kk == 7)),
                                                 inc="mm" if kk == 7 else None)
                                        sh_slot_mm[slot] = k.val("mm")
                                        k.wait("scalar", "mm")
                                        s_op(k, lambda s, mo=mo, pb=pb: s.activation(
                                            s1b[:, mo * NSUB:(mo + 1) * NSUB], pb[:, :],
                                            AF.Gelu, bias=c_sb1[:, mo:mo + 1], scale=1.0))
                                    for mo in range(LF // 128):
                                        slot = mo % 2
                                        k.wait("gpsimd", "mm", sh_slot_mm[slot])
                                        sv = k.op("gpsimd", lambda e, mo=mo, slot=slot:
                                                  e.dma_start(
                                                      shb[slot][:, :].rearrange(
                                                          "p (kb m) -> p kb m", m=128),
                                                      ffn_shared_w2[:, mo * 128:(mo + 1) * 128]
                                                      .rearrange("(kb p) m -> p kb m", p=128)),
                                                  inc="gld", n=16)
                                        k.wait("tensor", "gld", sv)
                                        pb = ps[2 + mo % 2]
                                        if mo >= 2:
                                            k.wait("tensor", "vq")
                                        k.wait("tensor", "sq")
                                        for kk in range(8):
                                            k.op("tensor", lambda t, kk=kk, slot=slot,
                                                 pb=pb: t.matmul(
                                                     pb[:, :],
                                                     shb[slot][:, kk * 128:(kk + 1) * 128],
                                                     s1b[:, kk * NSUB:(kk + 1) * NSUB],
                                                     start=(kk == 0), stop=(kk == 7)),
                                                 inc="mm" if kk == 7 else None)
                                        sh_slot_mm[slot] = k.val("mm")
                                        k.wait("vector", "mm")
                                        v_op(k, lambda v, mo=mo, h2=h2, pb=pb:
                                             v.scalar_tensor_tensor(
                                                 accb[:, mo * SUP + h2 * NSUB:
                                                      mo * SUP + h2 * NSUB + NSUB],
                                                 pb[:, :], 0.1,
                                                 accb[:, mo * SUP + h2 * NSUB:
                                                      mo * SUP + h2 * NSUB + NSUB],
                                                 ALU.mult, ALU.add), dep=True)
                                    k.wait("tensor", "st")
                                    for mo in range(LF // 128):
                                        pb = ps[4 + mo % 2]
                                        if mo >= 2:
                                            k.wait("tensor", "vq")
                                        k.op("tensor", lambda t, mo=mo, h2=h2, pb=pb:
                                             t.matmul(pb[:, :],
                                                      f_kneg[:, mo * 128:(mo + 1) * 128],
                                                      wfm[:, h2 * NSUB:(h2 + 1) * NSUB],
                                                      start=True, stop=True), inc="mm")
                                        k.wait("vector", "mm")
                                        v_op(k, lambda v, mo=mo, h2=h2, pb=pb: v.tensor_add(
                                            accb[:, mo * SUP + h2 * NSUB:
                                                 mo * SUP + h2 * NSUB + NSUB],
                                            accb[:, mo * SUP + h2 * NSUB:
                                                 mo * SUP + h2 * NSUB + NSUB],
                                            pb[:, :]), dep=True)
                        dpes = ExitStack()
                        with dpes:
                            xnew = sb(f"fD{call}{sup}_xn", 128, 16 * NSUB, F32, dpes)
                            u_sb = sb(f"fD{call}{sup}_u", 128, 16 * NSUB, F32, dpes)
                            cxl2 = sb(f"fD{call}{sup}_cx", 128, 16 * NSUB, BF16, dpes)
                            minb = sb(f"fD{call}{sup}_mb", 128, NSUB, BF16, dpes)
                            abf = sb(f"fD{call}{sup}_ab", 128, 8 * NSUB, BF16, dpes)
                            upw2 = [sb(f"fD{call}{sup}_up{i}", 128, 8 * 128, BF16, dpes)
                                    for i in range(2)]
                            cow2 = [sb(f"fD{call}{sup}_co{i}", 128, 16 * 128, BF16, dpes)
                                    for i in range(2)]
                            x0t = sb(f"fD{call}{sup}_x0t", 128, NSUB, F32, dpes)
                            t1t = sb(f"fD{call}{sup}_t1", 128, NSUB, F32, dpes)
                            tmp2 = sb(f"fD{call}{sup}_tp", 128, NSUB, F32, dpes)
                            xsq2 = sb(f"fD{call}{sup}_xq", 128, NSUB, F32, dpes)
                            rmu2 = sb(f"fD{call}{sup}_ru", 1, NSUB, F32, dpes)
                            rms2 = sb(f"fD{call}{sup}_rm", 1, NSUB, F32, dpes)
                            rm22 = sb(f"fD{call}{sup}_r2", 1, NSUB, F32, dpes)
                            rva2 = sb(f"fD{call}{sup}_rv", 1, NSUB, F32, dpes)
                            rrs2 = sb(f"fD{call}{sup}_rr", 1, NSUB, F32, dpes)
                            rmr2 = sb(f"fD{call}{sup}_rx", 1, NSUB, F32, dpes)
                            scope_guard(k)
                            k.wait("vector", "mm")
                            k.wait("vector", "st")
                            d_slot_mm = {0: 0, 1: 0}
                            for ncs in range(SUP // NSUB):
                                tok0 = sup * SUP + ncs * NSUB
                                k.wait("vector", "vq")
                                k.wait("vector", "mm")
                                for m in range(LF // 128):
                                    v_op(k, lambda v, m=m, ncs=ncs: v.tensor_copy(
                                        abf[:, m * NSUB:(m + 1) * NSUB],
                                        accb[:, m * SUP + ncs * NSUB:
                                             m * SUP + ncs * NSUB + NSUB]))
                                abv = k.val("vq")
                                cv = None
                                k.wait("sync", "mm")
                                for kk in range(16):
                                    cv = k.op("sync", lambda e, kk=kk, tok0=tok0:
                                              e.dma_start(
                                                  cxl2[:, kk * NSUB:(kk + 1) * NSUB],
                                                  s_cx[kk * 128:(kk + 1) * 128,
                                                       tok0:tok0 + NSUB]),
                                              inc="ld", n=16)
                                for mo in range(D // 128):
                                    slot = mo % 2
                                    k.wait("gpsimd", "mm", d_slot_mm[slot])
                                    uv = k.op("gpsimd", lambda e, mo=mo, slot=slot:
                                              e.dma_start(
                                                  upw2[slot][:, :].rearrange(
                                                      "p (kb m) -> p kb m", m=128),
                                                  p_fup[:, mo * 128:(mo + 1) * 128]
                                                  .rearrange("(kb p) m -> p kb m", p=128)),
                                              inc="gld", n=16)
                                    cv2 = k.op("gpsimd", lambda e, mo=mo, slot=slot:
                                               e.dma_start(
                                                   cow2[slot][:, :].rearrange(
                                                       "p (kb m) -> p kb m", m=128),
                                                   p_fcore[:, mo * 128:(mo + 1) * 128]
                                                   .rearrange("(kb p) m -> p kb m", p=128)),
                                               inc="gld", n=16)
                                    k.wait("tensor", "gld", cv2)
                                    k.wait("tensor", "ld", cv)
                                    k.wait("tensor", "vq", abv)
                                    pb = ps[mo % 2]
                                    if mo >= 2:
                                        k.wait("tensor", "vq")
                                    for kk in range(8):
                                        k.op("tensor", lambda t, kk=kk, slot=slot, pb=pb:
                                             t.matmul(pb[:, :],
                                                      upw2[slot][:, kk * 128:(kk + 1) * 128],
                                                      abf[:, kk * NSUB:(kk + 1) * NSUB],
                                                      start=(kk == 0), stop=False))
                                    for kk in range(16):
                                        k.op("tensor", lambda t, kk=kk, slot=slot, pb=pb:
                                             t.matmul(pb[:, :],
                                                      cow2[slot][:, kk * 128:(kk + 1) * 128],
                                                      cxl2[:, kk * NSUB:(kk + 1) * NSUB],
                                                      start=False, stop=(kk == 15)),
                                             inc="mm" if kk == 15 else None)
                                    d_slot_mm[slot] = k.val("mm")
                                    k.wait("sync", "vq")
                                    xv = k.op("sync", lambda e, mo=mo, tok0=tok0:
                                              e.dma_start(
                                                  x0t[:, :],
                                                  src[mo * 128:(mo + 1) * 128,
                                                      tok0:tok0 + NSUB]),
                                              inc="ld", n=16)
                                    k.wait("vector", "mm")
                                    k.wait("vector", "vq")
                                    v_op(k, lambda v, mo=mo, pb=pb: v.tensor_scalar(
                                        t1t[:, :], pb[:, :], 0.5,
                                        c_ucbh[:, mo:mo + 1], op0=ALU.mult,
                                        op1=ALU.add))
                                    k.wait("vector", "ld", xv)
                                    v_op(k, lambda v, mo=mo: v.tensor_add(
                                        xnew[:, mo * NSUB:(mo + 1) * NSUB],
                                        x0t[:, :], t1t[:, :]), dep=True)
                                xnv = k.val("vq")
                                k.wait("sync", "vq", xnv)
                                for kk in range(16):
                                    k.op("sync", lambda e, kk=kk, tok0=tok0:
                                         e.dma_start(
                                             out_t[kk * 128:(kk + 1) * 128,
                                                   tok0:tok0 + NSUB],
                                             xnew[:, kk * NSUB:(kk + 1) * NSUB]),
                                         inc="st", n=16)
                                if call == 1:
                                    emit_ln_simple(k, nc, ps, xnew, u_sb, tmp2, xsq2,
                                                   rmu2, rms2, rm22, rva2, rrs2, rmr2,
                                                   c_ones_col, c_ones_row, D)
                                    uv2 = k.val("vq")
                                    k.wait("sync", "vq", uv2)
                                    for kk in range(16):
                                        k.op("sync", lambda e, kk=kk, tok0=tok0:
                                             e.dma_start(
                                                 s_u[kk * 128:(kk + 1) * 128,
                                                     tok0:tok0 + NSUB],
                                                 u_sb[:, kk * NSUB:(kk + 1) * NSUB]),
                                             inc="st", n=16)
                                    emit_rms_prep(k, nc, ps, u_sb, xsq2, rms2, rrs2,
                                                  c_ones_col, c_ones_row, D)
                                    for kk in range(16):
                                        k.wait("vector", "st")
                                        v_op(k, lambda v, kk=kk: v.scalar_tensor_tensor(
                                            minb[:, :],
                                            u_sb[:, kk * NSUB:(kk + 1) * NSUB],
                                            c_ln1[:, kk:kk + 1], ps[4][:, :],
                                            ALU.mult, ALU.mult), dep=True)
                                        mv = k.val("vq")
                                        k.wait("sync", "vq", mv)
                                        k.op("sync", lambda e, kk=kk, tok0=tok0:
                                             e.dma_start(
                                                 g_min_in[kk * 128:(kk + 1) * 128,
                                                          tok0:tok0 + NSUB],
                                                 minb[:, :]), inc="st", n=16)
                                k.wait("vector", "st")
                    k.flush()

        def emit_ln_stage(k, nc, ps, src, tok0, x0_sb, xln, cxl, tmp, xsq,
                          rmu, rms_, rm2, rva, rrs, rmr, ones_c, ones_r,
                          st_ln, st_cx, store_ln):
            k.wait("sync", "vq")
            k.wait("sync", "sq")
            k.wait("sync", "mm")
            k.wait("vector", "st")
            k.wait("scalar", "st")
            ldv = None
            for kk in range(16):
                ldv = k.op("sync", lambda e, kk=kk: e.dma_start(
                    x0_sb[:, kk * NSUB:(kk + 1) * NSUB],
                    src[kk * 128:(kk + 1) * 128, tok0:tok0 + NSUB]), inc="ld", n=16)
            k.wait("tensor", "ld", ldv)
            k.wait("tensor", "vq")
            k.wait("tensor", "sq")
            for kk in range(16):
                k.op("tensor", lambda t, kk=kk: t.matmul(
                    ps[6][0:1, :], ones_c[:, :], x0_sb[:, kk * NSUB:(kk + 1) * NSUB],
                    start=(kk == 0), stop=(kk == 15)), inc="mm" if kk == 15 else None)
            k.wait("scalar", "ld", ldv)
            for kk in range(16):
                s_op(k, lambda s, kk=kk: s.activation(
                    xsq[:, :], x0_sb[:, kk * NSUB:(kk + 1) * NSUB], AF.Square),
                    dep=True)
                k.wait("tensor", "sq")
                k.op("tensor", lambda t, kk=kk: t.matmul(
                    ps[7][0:1, :], ones_c[:, :], xsq[:, :],
                    start=(kk == 0), stop=(kk == 15)), inc="mm" if kk == 15 else None)
                if kk < 15:
                    k.op("tensor", lambda t: t.matmul(
                        ps[5][0:1, 0:1], ones_r[0:1, 0:1], ones_r[0:1, 0:1],
                        start=True, stop=True), inc="mm")
                    k.wait("scalar", "mm")
            k.wait("vector", "mm")
            v_op(k, lambda v: v.tensor_scalar(rmu[:, :], ps[6][0:1, :], 1.0 / D,
                                              None, op0=ALU.mult))
            v_op(k, lambda v: v.tensor_scalar(rms_[:, :], ps[7][0:1, :], 1.0 / D,
                                              None, op0=ALU.mult))
            k.wait("vector", "vq")
            v_op(k, lambda v: v.tensor_mul(rm2[:, :], rmu[:, :], rmu[:, :]))
            k.wait("vector", "vq")
            v_op(k, lambda v: v.tensor_sub(rva[:, :], rms_[:, :], rm2[:, :]))
            k.wait("scalar", "vq")
            s_op(k, lambda s: s.activation(rva[:, :], rva[:, :], AF.Ln, bias=c_eps[0:1, :],
                                           scale=1.0), dep=True)
            s_op(k, lambda s: s.activation(rrs[:, :], rva[:, :], AF.Exp, bias=0.0,
                                           scale=c_mhalf[0:1, :]), dep=True)
            k.wait("vector", "sq")
            v_op(k, lambda v: v.tensor_mul(rmr[:, :], rmu[:, :], rrs[:, :]))
            k.wait("tensor", "vq")
            k.op("tensor", lambda t: t.matmul(ps[4][:, :], ones_r[:, :], rrs[:, :],
                                              start=True, stop=True), inc="mm")
            k.op("tensor", lambda t: t.matmul(ps[5][:, :], ones_r[:, :], rmr[:, :],
                                              start=True, stop=True), inc="mm")
            k.wait("vector", "mm")
            k.wait("scalar", "mm")
            if tok0 == 0 and not _dbgx_done:
                _dbgx_done.append(1)
                k.wait("sync", "mm")
                for kk in range(16):
                    k.op("sync", lambda e, kk=kk: e.dma_start(
                        s_dbgx[kk * 128:(kk + 1) * 128, :],
                        x0_sb[:, kk * NSUB:(kk + 1) * NSUB]), inc="st", n=16)
            for kk in range(16):
                v_op(k, lambda v, kk=kk: v.tensor_mul(
                    tmp[:, :], x0_sb[:, kk * NSUB:(kk + 1) * NSUB], ps[4][:, :]),
                    dep=True)
                k.wait("vector", "vq")
                v_op(k, lambda v, kk=kk: v.tensor_sub(
                    xln[:, kk * NSUB:(kk + 1) * NSUB], tmp[:, :], ps[5][:, :]))
                k.wait("scalar", "vq")
                s_op(k, lambda s, kk=kk: s.activation(
                    cxl[:, kk * NSUB:(kk + 1) * NSUB],
                    xln[:, kk * NSUB:(kk + 1) * NSUB], AF.Gelu))
            xlv = k.val("vq")
            clv = k.val("sq")
            if store_ln:
                k.wait("sync", "vq", xlv)
                k.wait("sync", "sq", clv)
                for kk in range(16):
                    k.op("sync", lambda e, kk=kk: e.dma_start(
                        st_ln[kk * 128:(kk + 1) * 128, tok0:tok0 + NSUB],
                        xln[:, kk * NSUB:(kk + 1) * NSUB]), inc="st", n=16)
                    k.op("sync", lambda e, kk=kk: e.dma_start(
                        st_cx[kk * 128:(kk + 1) * 128, tok0:tok0 + NSUB],
                        cxl[:, kk * NSUB:(kk + 1) * NSUB]), inc="st", n=16)

        def emit_ln_simple(k, nc, ps, xin_sb, out_sb, tmp, xsq, rmu, rms_, rm2,
                           rva, rrs, rmr, ones_c, ones_r, DD):
            nt = 16
            k.wait("tensor", "vq")
            k.wait("tensor", "sq")
            k.wait("scalar", "vq")
            for kk in range(nt):
                k.op("tensor", lambda t, kk=kk: t.matmul(
                    ps[6][0:1, :], ones_c[:, :], xin_sb[:, kk * NSUB:(kk + 1) * NSUB],
                    start=(kk == 0), stop=(kk == nt - 1)),
                    inc="mm" if kk == nt - 1 else None)
            for kk in range(nt):
                s_op(k, lambda s, kk=kk: s.activation(
                    xsq[:, :], xin_sb[:, kk * NSUB:(kk + 1) * NSUB], AF.Square),
                    dep=True)
                k.wait("tensor", "sq")
                k.op("tensor", lambda t, kk=kk: t.matmul(
                    ps[7][0:1, :], ones_c[:, :], xsq[:, :],
                    start=(kk == 0), stop=(kk == nt - 1)),
                    inc="mm" if kk == nt - 1 else None)
                if kk < nt - 1:
                    k.op("tensor", lambda t: t.matmul(
                        ps[5][0:1, 0:1], ones_r[0:1, 0:1], ones_r[0:1, 0:1],
                        start=True, stop=True), inc="mm")
                    k.wait("scalar", "mm")
            k.wait("vector", "mm")
            v_op(k, lambda v: v.tensor_scalar(rmu[:, :], ps[6][0:1, :], 1.0 / DD,
                                              None, op0=ALU.mult))
            v_op(k, lambda v: v.tensor_scalar(rms_[:, :], ps[7][0:1, :], 1.0 / DD,
                                              None, op0=ALU.mult))
            k.wait("vector", "vq")
            v_op(k, lambda v: v.tensor_mul(rm2[:, :], rmu[:, :], rmu[:, :]))
            k.wait("vector", "vq")
            v_op(k, lambda v: v.tensor_sub(rva[:, :], rms_[:, :], rm2[:, :]))
            k.wait("scalar", "vq")
            s_op(k, lambda s: s.activation(rva[:, :], rva[:, :], AF.Ln, bias=c_eps[0:1, :],
                                           scale=1.0), dep=True)
            s_op(k, lambda s: s.activation(rrs[:, :], rva[:, :], AF.Exp, bias=0.0,
                                           scale=c_mhalf[0:1, :]), dep=True)
            k.wait("vector", "sq")
            v_op(k, lambda v: v.tensor_mul(rmr[:, :], rmu[:, :], rrs[:, :]))
            k.wait("tensor", "vq")
            k.op("tensor", lambda t: t.matmul(ps[4][:, :], ones_r[:, :], rrs[:, :],
                                              start=True, stop=True), inc="mm")
            k.op("tensor", lambda t: t.matmul(ps[5][:, :], ones_r[:, :], rmr[:, :],
                                              start=True, stop=True), inc="mm")
            k.wait("vector", "mm")
            for kk in range(nt):
                v_op(k, lambda v, kk=kk: v.tensor_mul(
                    tmp[:, :], xin_sb[:, kk * NSUB:(kk + 1) * NSUB], ps[4][:, :]),
                    dep=True)
                k.wait("vector", "vq")
                v_op(k, lambda v, kk=kk: v.tensor_sub(
                    out_sb[:, kk * NSUB:(kk + 1) * NSUB], tmp[:, :], ps[5][:, :]))

        def emit_rms_prep(k, nc, ps, xin_sb, xsq, rms_, rrs, ones_c, ones_r, DD):
            nt = 16
            k.wait("tensor", "vq")
            k.wait("tensor", "sq")
            k.wait("scalar", "vq")
            for kk in range(nt):
                s_op(k, lambda s, kk=kk: s.activation(
                    xsq[:, :], xin_sb[:, kk * NSUB:(kk + 1) * NSUB], AF.Square),
                    dep=True)
                k.wait("tensor", "sq")
                k.op("tensor", lambda t, kk=kk: t.matmul(
                    ps[7][0:1, :], ones_c[:, :], xsq[:, :],
                    start=(kk == 0), stop=(kk == nt - 1)),
                    inc="mm" if kk == nt - 1 else None)
                if kk < nt - 1:
                    k.op("tensor", lambda t: t.matmul(
                        ps[5][0:1, 0:1], ones_r[0:1, 0:1], ones_r[0:1, 0:1],
                        start=True, stop=True), inc="mm")
                    k.wait("scalar", "mm")
            k.wait("vector", "mm")
            v_op(k, lambda v: v.tensor_scalar(rms_[:, :], ps[7][0:1, :], 1.0 / DD,
                                              None, op0=ALU.mult))
            k.wait("scalar", "vq")
            s_op(k, lambda s: s.activation(rms_[:, :], rms_[:, :], AF.Ln, bias=c_eps[0:1, :],
                                           scale=1.0), dep=True)
            s_op(k, lambda s: s.activation(rrs[:, :], rms_[:, :], AF.Exp, bias=0.0,
                                           scale=c_mhalf[0:1, :]), dep=True)
            k.wait("tensor", "sq")
            k.op("tensor", lambda t: t.matmul(ps[4][:, :], ones_r[:, :], rrs[:, :],
                                              start=True, stop=True), inc="mm")
            k.wait("vector", "mm")

        def emit_rms_scale(k, nc, ps, xin_sb, out_sb, xsq, rms_, rrs, ones_c,
                           ones_r, lnw, DD):
            emit_rms_prep(k, nc, ps, xin_sb, xsq, rms_, rrs, ones_c, ones_r, DD)
            for kk in range(16):
                v_op(k, lambda v, kk=kk: v.scalar_tensor_tensor(
                    out_sb[:, kk * NSUB:(kk + 1) * NSUB],
                    xin_sb[:, kk * NSUB:(kk + 1) * NSUB],
                    lnw[:, kk:kk + 1], ps[4][:, :], ALU.mult, ALU.mult))

        def emit_router(k, nc, ps, xln, rtr_sb, rbb, ident_sb, ones_c, wfm, ncs,
                        rr, r_sc):
            (r_pm, r_npm, r_ex, r_pr, r_k1, r_pk, r_k2, r_w0, r_w, r_lg,
             _a, _b) = rr
            (r_m1, r_mm2, r_dd, r_e2, r_g1, r_g2) = r_sc
            for ts in range(4):
                k.wait("tensor", "vq")
                k.wait("tensor", "sq")
                for kk in range(16):
                    k.op("tensor", lambda t, ts=ts, kk=kk: t.matmul(
                        ps[2][0:128, 0:EXN],
                        xln[:, kk * NSUB + ts * 128: kk * NSUB + ts * 128 + 128],
                        rtr_sb[:, kk * EXN:(kk + 1) * EXN],
                        start=(kk == 0), stop=(kk == 15)),
                        inc="mm" if kk == 15 else None)
                k.wait("vector", "mm")
                v_op(k, lambda v: v.tensor_add(r_lg[:, :], ps[2][0:128, 0:EXN],
                                               rbb[:, :]))
                v_op(k, lambda v: v.reduce_max(r_pm[:, 0:1], r_lg[:, :], axis=AX.X),
                     dep=True)
                k.wait("scalar", "vq")
                s_op(k, lambda s: s.mul(r_npm[:, 0:1], r_pm[:, 0:1], -1.0))
                k.wait("scalar", "sq")
                s_op(k, lambda s: s.activation(r_ex[:, :], r_lg[:, :], AF.Exp,
                                               bias=r_npm[:, 0:1], scale=1.0))
                k.wait("vector", "sq")
                v_op(k, lambda v: v.reduce_sum(r_m1[:, :], r_ex[:, :], axis=AX.X))
                v_op(k, lambda v: v.reciprocal(r_mm2[:, :], r_m1[:, :]), dep=True)
                v_op(k, lambda v: v.tensor_scalar(r_pr[:, :], r_ex[:, :],
                                                  r_mm2[:, :], None, op0=ALU.mult),
                     dep=True)
                v_op(k, lambda v: v.reduce_max(r_m1[:, :], r_pr[:, :], axis=AX.X),
                     dep=True)
                v_op(k, lambda v: v.tensor_scalar(r_k1[:, :], r_pr[:, :],
                                                  r_m1[:, :], None, op0=ALU.is_ge),
                     dep=True)
                v_op(k, lambda v: v.scalar_tensor_tensor(
                    r_pk[:, :], r_k1[:, :], -1e30, r_pr[:, :], ALU.mult, ALU.add),
                    dep=True)
                v_op(k, lambda v: v.reduce_max(r_mm2[:, :], r_pk[:, :], axis=AX.X),
                     dep=True)
                v_op(k, lambda v: v.tensor_scalar(r_k2[:, :], r_pk[:, :],
                                                  r_mm2[:, :], None, op0=ALU.is_ge),
                     dep=True)
                v_op(k, lambda v: v.tensor_sub(r_dd[:, :], r_mm2[:, :], r_m1[:, :]),
                     dep=True)
                k.wait("scalar", "vq")
                s_op(k, lambda s: s.activation(r_e2[:, :], r_dd[:, :], AF.Exp))
                k.wait("vector", "sq")
                v_op(k, lambda v: v.tensor_scalar(r_g2[:, :], r_e2[:, :], 1.0,
                                                  None, op0=ALU.add))
                v_op(k, lambda v: v.reciprocal(r_g1[:, :], r_g2[:, :]), dep=True)
                v_op(k, lambda v: v.scalar_tensor_tensor(
                    r_g2[:, :], r_g1[:, :], -1.0, ones_c[:, :], ALU.mult, ALU.add),
                    dep=True)
                v_op(k, lambda v: v.tensor_scalar(r_w0[:, :], r_k1[:, :],
                                                  r_g1[:, :], None, op0=ALU.mult))
                k.wait("vector", "vq")
                v_op(k, lambda v: v.scalar_tensor_tensor(
                    r_w[:, :], r_k2[:, :], r_g2[:, :], r_w0[:, :],
                    ALU.mult, ALU.add))
                k.wait("tensor", "vq")
                k.op("tensor", lambda t: t.transpose(
                    ps[3][0:8, 0:128], r_w[:, :], ident_sb[:, :]), inc="mm")
                k.wait("vector", "mm")
                v_op(k, lambda v, ncs=ncs, ts=ts: v.tensor_copy(
                    wfm[:, ncs * NSUB + ts * 128: ncs * NSUB + ts * 128 + 128],
                    ps[3][0:8, 0:128]))

        def mixer_phase():
            pes = ExitStack()
            with pes:
                ipw_sb = sb("mx_ipw", 128, 16 * 1408, BF16, pes)
                dtw_sb = sb("mx_dtw", 128, DC, BF16, pes)
                rhs_sb = [sb(f"mx_rhs{i}", 128, 16 * CH, BF16, pes) for i in range(2)]
                siluz = [sb(f"mx_siluz{i}", 128, 4 * CH, BF16, pes) for i in range(2)]
                xsrcA = [sb(f"mx_xsA{i}", 64, CH, F32, pes) for i in range(2)]
                xsrcB = [sb(f"mx_xsB{i}", 64, CH, F32, pes) for i in range(2)]
                BsbA = [sb(f"mx_BA{i}", 64, CH, BF16, pes) for i in range(2)]
                BsbB = [sb(f"mx_BB{i}", 64, CH, BF16, pes) for i in range(2)]
                Csb = [[sb(f"mx_C{j}_{i}", 128, CH, BF16, pes) for j in range(4)]
                       for i in range(2)]
                dtp = [sb(f"mx_dtp{i}", 128, CH, BF16, pes) for i in range(2)]
                dtf = [sb(f"mx_dtf{i}", 128, 4 * CH, F32, pes) for i in range(2)]

                xs_sb = [sb(f"mx_xs{i}", 128, 4 * CH, F32, pes) for i in range(2)]
                dtx = [sb(f"mx_dtx{i}", 128, 4 * CH, F32, pes) for i in range(2)]
                xrep = [sb(f"mx_xr{j}", 128, CH + 4, F32, pes) for j in range(4)]
                xc_t = sb("mx_xc", 128, CH, F32, pes)
                ab_t = xc_t
                dA = [sb(f"mx_dA{i}", 128, CH, F32, pes) for i in range(2)]
                dbx = [sb(f"mx_db{i}", 128, CH, F32, pes) for i in range(2)]
                hbuf = [sb(f"mx_h{i}", 128, CH, F32, pes) for i in range(2)]
                prodb = [sb(f"mx_p{i}", 128, CH, F32, pes) for i in range(2)]
                crepb = [sb(f"mx_cr{i}", 128, CH, F32, pes) for i in range(2)]
                yac2 = [sb(f"mx_y2{i}", 128, CH, F32, pes) for i in range(2)]
                states = sb("mx_st", 128, 4 * NST, F32, pes)
                ygb = [sb(f"mx_yg{i}", 128, CH, BF16, pes) for i in range(2)]

                k.op("gpsimd", lambda e: e.dma_start(
                    ipw_sb[:, :].rearrange("p (kb m) -> p kb m", m=1408),
                    ipwT.rearrange("(kb p) m -> p kb m", p=128)), inc="gld", n=16)
                k.op("gpsimd", lambda e: e.dma_start(
                    dtw_sb[:, :], dtwT[:, :]), inc="gld", n=16)
                k.wait("gpsimd", "st")
                k.op("gpsimd", lambda e: e.collective_compute(
                    "AllGather", ALU.bypass, replica_groups=[list(range(NC))],
                    ins=[g_min_in.opt()], outs=[g_min_out.opt()]), inc="cc")
                k.wait("sync", "cc")
                k.wait("tensor", "gld")

                NCHUNK = (B * T) // CH
                rhs_slot_mm = {0: 0, 1: 0}
                mxw = {"pb": [0, 0], "pc": [0, 0], "dA": [0, 0], "dbx": [0, 0],
                       "hb": [0, 0], "crep": [0, 0], "prodb": [0, 0],
                       "prodv": [0, 0], "acc": [0, 0], "ygst": [0, 0],
                       "selb": [0, 0], "dAsq": [0, 0], "crepsq": [0, 0],
                       "hdvq": [0, 0], "hdmm": [0, 0],
                       "dtbank": [0], "convbank": [0],
                       "state": [0] * (4 * NST)}

                def head_gen(ch):
                    par = ch % 2
                    bb2 = ch // (T // CH)
                    t02 = (ch % (T // CH)) * CH
                    rank = bb2 * 4 + t02 // TOKC
                    col0 = t02 % TOKC
                    k.wait("sync", "mm", rhs_slot_mm[par])
                    rv = k.op("sync", lambda e, par=par, rank=rank, col0=col0:
                              e.dma_start(
                                  rhs_sb[par][:, :].rearrange(
                                      "p (kb m) -> p kb m", m=CH),
                                  g_min_out[rank * D:(rank + 1) * D,
                                            col0:col0 + CH]
                                  .rearrange("(kb p) m -> p kb m", p=128)),
                              inc="ld", n=16)
                    yield
                    k.wait("scalar", "vq", mxw["hdvq"][par])
                    k.wait("scalar", "mm", mxw["hdmm"][par])
                    for m in range(11):
                        pb = ps[m % 3]
                        k.wait("tensor", "ev")
                        k.wait("tensor", "ld", rv)
                        for kk in range(16):
                            k.op("tensor", lambda t, m=m, kk=kk, pb=pb, par=par:
                                 t.matmul(pb[:, :],
                                          ipw_sb[:, kk * 1408 + m * 128:
                                                 kk * 1408 + m * 128 + 128],
                                          rhs_sb[par][:, kk * CH:(kk + 1) * CH],
                                          start=(kk == 0), stop=(kk == 15)),
                                 inc="mm" if kk == 15 else None)
                        k.wait("scalar", "mm")
                        if m < 4:
                            k.op("scalar", lambda s, m=m, pb=pb, par=par:
                                 s.activation(siluz[par][:, m * CH:(m + 1) * CH],
                                              pb[:, :], AF.Silu), inc="ev")
                        elif m == 4:
                            k.op("scalar", lambda s, pb=pb, par=par: s.copy(
                                xsrcA[par][:, :], pb[0:64, :]), inc="ev")
                            k.op("scalar", lambda s, pb=pb, par=par: s.copy(
                                xsrcB[par][:, :], pb[64:128, :]), inc="ev")
                        elif m == 5:
                            k.op("scalar", lambda s, pb=pb, par=par: s.copy(
                                BsbA[par][:, :], pb[0:64, :]), inc="ev")
                            k.op("scalar", lambda s, pb=pb, par=par: s.copy(
                                BsbB[par][:, :], pb[64:128, :]), inc="ev")
                        elif m < 10:
                            k.op("scalar", lambda s, m=m, pb=pb, par=par: s.copy(
                                Csb[par][m - 6][:, :], pb[:, :]), inc="ev")
                        else:
                            k.op("scalar", lambda s, pb=pb, par=par: s.copy(
                                dtp[par][:, :], pb[:, :]), inc="ev")
                        yield
                    rhs_slot_mm[par] = k.val("mm")
                    for j in range(4):
                        k.wait("tensor", "ev")
                        k.wait("tensor", "sq", mxw["dtbank"][0])
                        k.wait("tensor", "vq", mxw["convbank"][0])
                        k.op("tensor", lambda t, j=j, par=par: t.matmul(
                            ps[3][:, :], dtw_sb[:, j * 128:(j + 1) * 128],
                            dtp[par][:, :], start=True, stop=True), inc="mm")
                        k.wait("scalar", "mm")
                        k.wait("scalar", "vq")
                        s_op(k, lambda s, j=j, par=par: s.activation(
                            dtf[par][:, j * CH:(j + 1) * CH], ps[3][:, :], AF.Relu,
                            bias=c_dtb[:, j:j + 1], scale=1.0), dep=True)
                        s_op(k, lambda s, j=j: s.activation(
                            ab_t[:, :], ps[3][:, :], AF.Abs,
                            bias=c_dtb[:, j:j + 1], scale=1.0))
                        mxw["dtbank"][0] = k.val("sq")
                        s_op(k, lambda s: s.activation(
                            ab_t[:, :], ab_t[:, :], AF.Exp, bias=0.0, scale=-1.0),
                            dep=True)
                        s_op(k, lambda s: s.activation(
                            ab_t[:, :], ab_t[:, :], AF.Ln, bias=1.0, scale=1.0),
                            dep=True)
                        k.wait("vector", "sq")
                        v_op(k, lambda v, j=j, par=par: v.tensor_add(
                            dtf[par][:, j * CH:(j + 1) * CH],
                            dtf[par][:, j * CH:(j + 1) * CH],
                            ab_t[:, :]), dep=True)
                        yield
                    for j in range(4):
                        src_t = xsrcA[par] if j < 2 else xsrcB[par]
                        k.wait("tensor", "sq", mxw["dtbank"][0])
                        k.wait("tensor", "vq", mxw["convbank"][0])
                        k.op("tensor", lambda t, j=j, src_t=src_t: t.matmul(
                            ps[3][:, :],
                            c_selx[32 * (j % 2):32 * (j % 2) + 32, :],
                            src_t[32 * (j % 2):32 * (j % 2) + 32, :],
                            start=True, stop=True), inc="mm")
                        selx_mm = k.val("mm")
                        if t02 == 0:
                            v_op(k, lambda v, j=j: v.memset(
                                xrep[j][:, 0:4], 0.0), dep=True)
                        else:
                            v_op(k, lambda v, j=j: v.tensor_copy(
                                xrep[j][:, 0:4],
                                xrep[j][:, CH:CH + 4]), dep=True)
                        k.wait("vector", "mm", selx_mm)
                        k.wait("vector", "vq")
                        v_op(k, lambda v, j=j: v.tensor_copy(
                            xrep[j][:, 4:CH + 4], ps[3][:, :]))
                        mxw["convbank"][0] = k.val("vq")
                        k.wait("vector", "vq")
                        v_op(k, lambda v, j=j: v.tensor_scalar(
                            xc_t[:, :], xrep[j][:, 1:CH + 1],
                            c_convw[:, j * 4 + 0:j * 4 + 1], None, op0=ALU.mult))
                        for tau in range(1, 4):
                            k.wait("vector", "vq")
                            v_op(k, lambda v, j=j, tau=tau:
                                 v.scalar_tensor_tensor(
                                     xc_t[:, :],
                                     xrep[j][:, 1 + tau:CH + 1 + tau],
                                     c_convw[:, j * 4 + tau:j * 4 + tau + 1],
                                     xc_t[:, :], ALU.mult, ALU.add))
                        k.wait("scalar", "vq")
                        s_op(k, lambda s, j=j, par=par: s.activation(
                            xs_sb[par][:, j * CH:(j + 1) * CH], xc_t[:, :],
                            AF.Silu, bias=c_convb[:, j:j + 1], scale=1.0),
                            dep=True)
                        k.wait("vector", "sq")
                        k.wait("vector", "vq")
                        v_op(k, lambda v, j=j, par=par: v.tensor_mul(
                            dtx[par][:, j * CH:(j + 1) * CH],
                            dtf[par][:, j * CH:(j + 1) * CH],
                            xs_sb[par][:, j * CH:(j + 1) * CH]))
                        yield

                def mx_producers(idx, par):
                    j2, n2 = idx // NST, idx % NST
                    q2 = idx % 2
                    pb2, pc2 = ps[4 + q2], ps[6 + q2]
                    Bs2 = BsbA[par] if j2 < 2 else BsbB[par]
                    k.wait("tensor", "vq", mxw["pb"][q2])
                    k.op("tensor", lambda t, j2=j2, n2=n2, pb2=pb2, Bs2=Bs2:
                         t.matmul(pb2[:, :],
                                  c_selB[32 * (j2 % 2):32 * (j2 % 2) + 32,
                                         n2 * 128:(n2 + 1) * 128],
                                  Bs2[32 * (j2 % 2):32 * (j2 % 2) + 32, :],
                                  start=True, stop=True), inc="mm")
                    mxw["selb"][q2] = k.val("mm")
                    k.wait("tensor", "sq", mxw["pc"][q2])
                    k.op("tensor", lambda t, n2=n2, j2=j2, pc2=pc2, par=par:
                         t.matmul(pc2[:, :], c_selC[:, n2 * 128:(n2 + 1) * 128],
                                  Csb[par][j2][:, :], start=True, stop=True),
                         inc="mm")
                    selc_mm = k.val("mm")
                    k.wait("scalar", "vq", mxw["dA"][q2])
                    s_op(k, lambda sc_, j2=j2, n2=n2, q2=q2, par=par:
                         sc_.activation(
                             dA[q2][:, :], dtf[par][:, j2 * CH:(j2 + 1) * CH],
                             AF.Exp, bias=0.0,
                             scale=c_aneg[:, j2 * NST + n2:j2 * NST + n2 + 1]))
                    mxw["dAsq"][q2] = k.val("sq")
                    k.wait("scalar", "mm", selc_mm)
                    k.wait("scalar", "gq", mxw["crep"][q2])
                    s_op(k, lambda sc_, q2=q2, pc2=pc2: sc_.copy(
                        crepb[q2][:, :], pc2[:, :]))
                    mxw["crepsq"][q2] = k.val("sq")
                    mxw["pc"][q2] = k.val("sq")

                def mx_gate(j2, par, bb2, t02):
                    k.wait("vector", "ev")
                    k.wait("vector", "vq")
                    v_op(k, lambda v: v.tensor_add(yac2[0][:, :], yac2[0][:, :],
                                                   yac2[1][:, :]))
                    k.wait("vector", "vq")
                    v_op(k, lambda v, j2=j2, par=par: v.scalar_tensor_tensor(
                        yac2[0][:, :], xs_sb[par][:, j2 * CH:(j2 + 1) * CH],
                        c_d128[:, j2:j2 + 1], yac2[0][:, :], ALU.mult, ALU.add))
                    k.wait("vector", "vq")
                    k.wait("vector", "st", mxw["ygst"][j2 % 2])
                    v_op(k, lambda v, j2=j2, par=par: v.tensor_mul(
                        ygb[j2 % 2][:, :], yac2[0][:, :],
                        siluz[par][:, j2 * CH:(j2 + 1) * CH]))
                    yv = k.val("vq")
                    k.wait("sync", "vq", yv)
                    k.op("sync", lambda e, j2=j2, bb2=bb2, t02=t02: e.dma_start(
                        g_yg_in[bb2 * DC + j2 * 128: bb2 * DC + (j2 + 1) * 128,
                                t02:t02 + CH], ygb[j2 % 2][:, :]), inc="st", n=16)
                    mxw["ygst"][j2 % 2] = k.val("st")

                def mx_acc(idx, par, bb2, t02):
                    qp = idx % 2
                    k.wait("vector", "gq", mxw["prodv"][qp])
                    if idx % NST < 2:
                        v_op(k, lambda v, qp=qp: v.tensor_copy(
                            yac2[qp][:, :], prodb[qp][:, :]))
                    else:
                        k.wait("vector", "vq", mxw["acc"][qp])
                        v_op(k, lambda v, qp=qp: v.tensor_add(
                            yac2[qp][:, :], yac2[qp][:, :], prodb[qp][:, :]))
                    mxw["acc"][qp] = k.val("vq")
                    mxw["prodb"][qp] = k.val("vq")
                    if idx % NST == NST - 1:
                        mx_gate(idx // NST, par, bb2, t02)

                def mx_consumers(idx, par):
                    j2, n2 = idx // NST, idx % NST
                    q2 = idx % 2
                    pb2 = ps[4 + q2]
                    k.wait("vector", "mm", mxw["selb"][q2])
                    k.wait("vector", "vq", mxw["dbx"][q2])
                    v_op(k, lambda v, j2=j2, q2=q2, pb2=pb2, par=par: v.tensor_mul(
                        dbx[q2][:, :], dtx[par][:, j2 * CH:(j2 + 1) * CH],
                        pb2[:, :]))
                    mxw["pb"][q2] = k.val("vq")
                    k.wait("vector", "sq", mxw["dAsq"][q2])
                    k.wait("vector", "gq", mxw["hb"][q2])
                    k.wait("vector", "sq", mxw["state"][idx])
                    k.wait("vector", "vq")
                    v_op(k, lambda v, j2=j2, n2=n2, q2=q2: v.tensor_tensor_scan(
                        hbuf[q2][:, :], dA[q2][:, :], dbx[q2][:, :],
                        states[:, j2 * NST + n2:j2 * NST + n2 + 1],
                        ALU.mult, ALU.add))
                    scan_vq = k.val("vq")
                    mxw["dA"][q2] = scan_vq
                    mxw["dbx"][q2] = scan_vq
                    k.wait("scalar", "vq", scan_vq)
                    s_op(k, lambda sc_, j2=j2, n2=n2, q2=q2: sc_.copy(
                        states[:, j2 * NST + n2:j2 * NST + n2 + 1],
                        hbuf[q2][:, CH - 1:CH]))
                    mxw["state"][idx] = k.val("sq")
                    k.wait("gpsimd", "vq", scan_vq)
                    k.wait("gpsimd", "sq", mxw["crepsq"][q2])
                    k.wait("gpsimd", "vq", mxw["prodb"][q2])
                    k.op("gpsimd", lambda g, q2=q2: g.tensor_mul(
                        prodb[q2][:, :], hbuf[q2][:, :], crepb[q2][:, :]),
                        inc="gq")
                    gq_v = k.val("gq")
                    mxw["prodv"][q2] = gq_v
                    mxw["hb"][q2] = gq_v
                    mxw["crep"][q2] = gq_v

                g0 = head_gen(0)
                for _ in g0:
                    pass
                for ch in range(NCHUNK):
                    par = ch % 2
                    bb2 = ch // (T // CH)
                    t02 = (ch % (T // CH)) * CH
                    if t02 == 0:
                        k.wait("vector", "vq")
                        k.wait("vector", "sq")
                        v_op(k, lambda v: v.memset(states[:, :], 0.0))
                    k.wait("scalar", "vq")
                    k.wait("tensor", "ev")
                    gen = head_gen(ch + 1) if ch + 1 < NCHUNK else None
                    mx_producers(0, par)
                    for idx in range(4 * NST):
                        if idx + 1 < 4 * NST:
                            mx_producers(idx + 1, par)
                        mx_consumers(idx, par)
                        if idx >= 1:
                            mx_acc(idx - 1, par, bb2, t02)
                        if gen is not None:
                            next(gen, None)
                    mx_acc(4 * NST - 1, par, bb2, t02)
                    if gen is not None:
                        for _ in gen:
                            pass
                    mxw["hdvq"][par] = k.val("vq")
                    mxw["hdmm"][par] = k.val("mm")
                    if ch % 8 == 7:
                        k.flush()
                k.flush()

        def outproj_phase():
            pes = ExitStack()
            with pes:
                yrhs = sb("op_yrhs", 128, 32 * NSUB, BF16, pes)
                wop = [sb(f"op_w{i}", 128, 32 * 128, BF16, pes) for i in range(2)]
                h1 = sb("op_h1", 128, 16 * NSUB, F32, pes)
                m2 = sb("op_m2", 128, 16 * NSUB, BF16, pes)
                s12t = sb("op_s12t", 128, NSUB, F32, pes)
                xsq = sb("op_xsq", 128, NSUB, F32, pes)
                rms_ = sb("op_rms", 1, NSUB, F32, pes)
                rrs = sb("op_rrs", 1, NSUB, F32, pes)
                u_t = sb("op_ut", 128, NSUB, F32, pes)
                x1_t = sb("op_x1t", 128, NSUB, F32, pes)

                k.wait("gpsimd", "st")
                k.op("gpsimd", lambda e: e.collective_compute(
                    "AllGather", ALU.bypass, replica_groups=[list(range(NC))],
                    ins=[g_yg_in.opt()], outs=[g_yg_out.opt()]), inc="cc")
                k.wait("sync", "cc")
                k.S("ld")[1] += 16

                def sw_copy(e):
                    for arm in nc.switch_core_id(engines=[nc.sync], n=128):
                        c = arm.logical if hasattr(arm, "logical") else int(arm)
                        c = c % NC
                        bbq, ttq = c // 4, (c % 4) * TOKC
                        srcap = bass.AP(
                            g_yg_out.tensor, (bbq * DC) * T + ttq,
                            [[B * DC * T, NC], [T, DC], [1, TOKC]])
                        dstap = s_ymy.rearrange("(r d) t -> r d t", d=DC)
                        e.dma_start(dstap, srcap).then_inc(k.S("ld")[0], 16)
                k.raw("sync", sw_copy)
                swv = k.val("ld")
                k.wait("sync", "ld", swv)

                op_slot_mm = {0: 0, 1: 0}
                for ncs in range(TOKC // NSUB):
                    tok0 = ncs * NSUB
                    rv = None
                    k.wait("sync", "mm")
                    for kb_ in range(32):
                        rv = k.op("sync", lambda e, kb_=kb_, tok0=tok0: e.dma_start(
                            yrhs[:, kb_ * NSUB:(kb_ + 1) * NSUB],
                            s_ymy[kb_ * 128:(kb_ + 1) * 128, tok0:tok0 + NSUB]),
                            inc="ld", n=16)
                    k.wait("tensor", "ld", rv)
                    for mo in range(D // 128):
                        slot = mo % 2
                        k.wait("gpsimd", "mm", op_slot_mm[slot])
                        wv = k.op("gpsimd", lambda e, mo=mo, slot=slot: e.dma_start(
                            wop[slot][:, :].rearrange("p (kb m) -> p kb m", m=128),
                            p_opw[:, mo * 128:(mo + 1) * 128]
                            .rearrange("(kb p) m -> p kb m", p=128)),
                            inc="gld", n=16)
                        k.wait("tensor", "gld", wv)
                        pb = ps[mo % 2]
                        if mo >= 2:
                            k.wait("tensor", "vq")
                        for kb_ in range(32):
                            k.op("tensor", lambda t, kb_=kb_, slot=slot, pb=pb:
                                 t.matmul(pb[:, :],
                                          wop[slot][:, kb_ * 128:(kb_ + 1) * 128],
                                          yrhs[:, kb_ * NSUB:(kb_ + 1) * NSUB],
                                          start=(kb_ == 0), stop=(kb_ == 31)),
                                 inc="mm" if kb_ == 31 else None)
                        op_slot_mm[slot] = k.val("mm")
                        k.wait("sync", "vq")
                        uv = k.op("sync", lambda e, mo=mo, tok0=tok0: e.dma_start(
                            u_t[:, :], s_u[mo * 128:(mo + 1) * 128, tok0:tok0 + NSUB]),
                            inc="ld", n=16)
                        xv = k.op("sync", lambda e, mo=mo, tok0=tok0: e.dma_start(
                            x1_t[:, :], s_x1[mo * 128:(mo + 1) * 128,
                                             tok0:tok0 + NSUB]), inc="ld", n=16)
                        k.wait("vector", "mm")
                        k.wait("vector", "ld", xv)
                        v_op(k, lambda v, mo=mo, pb=pb: v.tensor_add(
                            h1[:, mo * NSUB:(mo + 1) * NSUB], pb[:, :], u_t[:, :]),
                            dep=True)
                        k.wait("vector", "vq")
                        k.wait("vector", "st")
                        v_op(k, lambda v, mo=mo: v.tensor_add(
                            s12t[:, :], h1[:, mo * NSUB:(mo + 1) * NSUB], x1_t[:, :]))
                        sv = k.val("vq")
                        k.wait("sync", "vq", sv)
                        k.op("sync", lambda e, mo=mo, tok0=tok0: e.dma_start(
                            s_s12[mo * 128:(mo + 1) * 128, tok0:tok0 + NSUB],
                            s12t[:, :]), inc="st", n=16)
                    emit_rms_scale(k, nc, ps, h1, m2, xsq, rms_, rrs, c_ones_col,
                                   c_ones_row, c_ln2, D)
                    m2v = k.val("vq")
                    k.wait("sync", "vq", m2v)
                    for kk in range(16):
                        k.op("sync", lambda e, kk=kk, tok0=tok0: e.dma_start(
                            s_m2[kk * 128:(kk + 1) * 128, tok0:tok0 + NSUB],
                            m2[:, kk * NSUB:(kk + 1) * NSUB]), inc="st", n=16)
                    k.wait("vector", "st")
                k.flush()

        def mlp_phase():
            pes = ExitStack()
            with pes:
                m2 = sb("ml_m2", 128, 16 * NSUB, BF16, pes)
                wg = [sb(f"ml_wg{i}", 128, 16 * 128, BF16, pes) for i in range(2)]
                wu = [sb(f"ml_wu{i}", 128, 16 * 128, BF16, pes) for i in range(2)]
                wd = [sb(f"ml_wd{i}", 128, 64 * 128, BF16, pes) for i in range(2)]
                i_sb = sb("ml_i", 128, 64 * NSUB, BF16, pes)
                sg = sb("ml_sg", 128, NSUB, F32, pes)
                s12t = sb("ml_s12t", 128, NSUB, F32, pes)
                x2t = [sb(f"ml_x2t{i}", 128, NSUB, F32, pes) for i in range(2)]

                gu_slot_mm = {0: 0, 1: 0}
                wd_slot_mm = {0: 0, 1: 0}
                for ncs in range(TOKC // NSUB):
                    tok0 = ncs * NSUB
                    mv = None
                    k.wait("sync", "vq")
                    k.wait("sync", "mm")
                    for kk in range(16):
                        mv = k.op("sync", lambda e, kk=kk, tok0=tok0: e.dma_start(
                            m2[:, kk * NSUB:(kk + 1) * NSUB],
                            s_m2[kk * 128:(kk + 1) * 128, tok0:tok0 + NSUB]),
                            inc="ld", n=16)
                    k.wait("tensor", "ld", mv)
                    k.wait("vector", "mm")
                    for mi in range(INTER // 128):
                        slot = mi % 2
                        k.wait("gpsimd", "mm", gu_slot_mm[slot])
                        gv = k.op("gpsimd", lambda e, mi=mi, slot=slot: e.dma_start(
                            wg[slot][:, :].rearrange("p (kb m) -> p kb m", m=128),
                            p_gate[:, mi * 128:(mi + 1) * 128]
                            .rearrange("(kb p) m -> p kb m", p=128)),
                            inc="gld", n=16)
                        uv2 = k.op("gpsimd", lambda e, mi=mi, slot=slot: e.dma_start(
                            wu[slot][:, :].rearrange("p (kb m) -> p kb m", m=128),
                            p_up[:, mi * 128:(mi + 1) * 128]
                            .rearrange("(kb p) m -> p kb m", p=128)),
                            inc="gld", n=16)
                        k.wait("tensor", "gld", uv2)
                        pg = ps[mi % 2]
                        pu = ps[2 + mi % 2]
                        if mi >= 2:
                            k.wait("tensor", "vq")
                        for kk in range(16):
                            k.op("tensor", lambda t, kk=kk, slot=slot, pg=pg: t.matmul(
                                pg[:, :], wg[slot][:, kk * 128:(kk + 1) * 128],
                                m2[:, kk * NSUB:(kk + 1) * NSUB],
                                start=(kk == 0), stop=(kk == 15)),
                                inc="mm" if kk == 15 else None)
                        for kk in range(16):
                            k.op("tensor", lambda t, kk=kk, slot=slot, pu=pu: t.matmul(
                                pu[:, :], wu[slot][:, kk * 128:(kk + 1) * 128],
                                m2[:, kk * NSUB:(kk + 1) * NSUB],
                                start=(kk == 0), stop=(kk == 15)),
                                inc="mm" if kk == 15 else None)
                        gu_slot_mm[slot] = k.val("mm")
                        k.wait("scalar", "mm")
                        k.wait("scalar", "vq")
                        s_op(k, lambda s, pg=pg: s.activation(sg[:, :], pg[:, :],
                                                              AF.Silu), dep=True)
                        k.wait("vector", "sq")
                        v_op(k, lambda v, mi=mi, pu=pu: v.tensor_mul(
                            i_sb[:, mi * NSUB:(mi + 1) * NSUB], sg[:, :], pu[:, :]))
                    iv = k.val("vq")
                    k.wait("tensor", "vq", iv)
                    for mo2 in range(D // 128):
                        slot = mo2 % 2
                        k.wait("gpsimd", "mm", wd_slot_mm[slot])
                        dv = k.op("gpsimd", lambda e, mo2=mo2, slot=slot: e.dma_start(
                            wd[slot][:, :].rearrange("p (kb m) -> p kb m", m=128),
                            p_down[:, mo2 * 128:(mo2 + 1) * 128]
                            .rearrange("(kb p) m -> p kb m", p=128)),
                            inc="gld", n=16)
                        k.wait("tensor", "gld", dv)
                        pb = ps[4 + mo2 % 2]
                        if mo2 >= 2:
                            k.wait("tensor", "vq")
                        for mi in range(64):
                            k.op("tensor", lambda t, mi=mi, slot=slot, pb=pb: t.matmul(
                                pb[:, :], wd[slot][:, mi * 128:(mi + 1) * 128],
                                i_sb[:, mi * NSUB:(mi + 1) * NSUB],
                                start=(mi == 0), stop=(mi == 63)),
                                inc="mm" if mi == 63 else None)
                        wd_slot_mm[slot] = k.val("mm")
                        k.wait("sync", "vq")
                        sv12 = k.op("sync", lambda e, mo2=mo2, tok0=tok0: e.dma_start(
                            s12t[:, :], s_s12[mo2 * 128:(mo2 + 1) * 128,
                                              tok0:tok0 + NSUB]), inc="ld", n=16)
                        k.wait("vector", "mm")
                        k.wait("vector", "ld", sv12)
                        k.wait("vector", "st")
                        v_op(k, lambda v, mo2=mo2, pb=pb: v.tensor_add(
                            x2t[mo2 % 2][:, :], s12t[:, :], pb[:, :]), dep=True)
                        xv2 = k.val("vq")
                        k.wait("sync", "vq", xv2)
                        k.op("sync", lambda e, mo2=mo2, tok0=tok0: e.dma_start(
                            s_x2[mo2 * 128:(mo2 + 1) * 128, tok0:tok0 + NSUB],
                            x2t[mo2 % 2][:, :]), inc="st", n=16)
                k.flush()

        phase0()
        if dbg_sb is not None:
            k.barrier(exclude=("pc",))
            k.op("sync", lambda e: e.dma_start(dbg_sb[0:128, 0:8], c_cc[:, :]),
                 inc="st", n=16)
            k.op("sync", lambda e: e.dma_start(dbg_sb[0:8, 8:1032], s_kneg[:, :]),
                 inc="st", n=16)
            k.op("sync", lambda e: e.dma_start(dbg_sb[0:128, 1032:1040], c_rbb[:, :]),
                 inc="st", n=16)
            k.op("sync", lambda e: e.dma_start(dbg_sb[0:128, 1040:1056], c_ucbh[:, :]),
                 inc="st", n=16)
            k.op("sync", lambda e: e.dma_start(dbg_sb[0:128, 1056:1120],
                                               c_aneg[:, :]), inc="st", n=16)
        k.barrier(exclude=("pc",))
        ffn_phase(x0, call=1)
        if stop_after != "p1":
            k.barrier()
            mixer_phase()
            if stop_after != "p3":
                k.barrier()
                outproj_phase()
                k.barrier()
                mlp_phase()
                if stop_after != "p5":
                    k.barrier()
                    ffn_phase(s_x2, call=2)
        for dst, srct in dbg_outs:
            k.barrier()
            rows = srct.shape[0]
            r = 0
            while r < rows:
                rr = min(2048, rows - r)
                k.op("sync", lambda e, d=dst, s=srct, r=r, rr=rr: e.dma_start(
                    d[r:r + rr, :], s[r:r + rr, :]), inc="st", n=16)
                r += rr
        k.wait("sync", "st")
        k.wait("sync", "ld")
        k.wait("gpsimd", "gld")
        k.wait("gpsimd", "pc")
        k.flush()

    return nc


def _host_prep(inputs):
    gi = lambda name: np.asarray(inputs[name])
    x = gi("L0_attn_hidden_states")

    base = {}
    base["ffn_down_w"] = np.ascontiguousarray(gi("L0_ffn_down_w"))
    base["ffn_up_w"] = np.ascontiguousarray(gi("L0_ffn_up_w"))
    base["ffn_router_w"] = np.ascontiguousarray(gi("L0_ffn_router_w"))
    base["ffn_shared_w1"] = np.ascontiguousarray(gi("L0_ffn_shared_w1"))
    base["ffn_shared_w2"] = np.ascontiguousarray(gi("L0_ffn_shared_w2"))
    base["ffn_core_w"] = np.ascontiguousarray(gi("L0_ffn_core_w"))
    for i in range(EXN):
        base[f"e{i}_w1"] = np.ascontiguousarray(gi(f"L0_ffn_e{i}_w1"))
        base[f"e{i}_w2"] = np.ascontiguousarray(gi(f"L0_ffn_e{i}_w2"))

    def col128(vec):
        return np.ascontiguousarray(np.asarray(vec).reshape(-1, 128).T)

    base["down_b128"] = col128(gi("L0_ffn_down_b"))
    for i in range(EXN):
        base[f"b1c{i}"] = col128(gi(f"L0_ffn_e{i}_b1"))
    b2all = np.zeros((128, (LF // 128) * EXN), np.float32)
    for e in range(EXN):
        c = col128(gi(f"L0_ffn_e{e}_b2"))
        for m in range(LF // 128):
            b2all[:, m * EXN + e] = c[:, m]
    base["b2all"] = b2all
    base["sb1_128"] = col128(gi("L0_ffn_shared_b1"))
    base["sb2_128"] = col128(gi("L0_ffn_shared_b2"))
    base["upb_128"] = col128(gi("L0_ffn_up_b"))
    base["coreb_128"] = col128(gi("L0_ffn_core_b"))
    base["rb_row"] = np.ascontiguousarray(
        gi("L0_ffn_router_b").reshape(1, EXN).astype(np.float32))
    cost = np.array([2 * HID[i] * LF for i in range(EXN)], np.float32)
    base["cost_row"] = (COST_LAMBDA * cost).reshape(1, EXN).astype(np.float32)

    base["gatewT"] = np.ascontiguousarray(gi("L0_attn_gate_w").T)
    base["upwT"] = np.ascontiguousarray(gi("L0_attn_up_w").T)
    base["downwT"] = np.ascontiguousarray(gi("L0_attn_down_w").T)
    base["opwT"] = np.ascontiguousarray(gi("L0_attn_out_proj_w").T)
    base["ln1_128"] = col128(gi("L0_attn_ln1_w"))
    base["ln2_128"] = col128(gi("L0_attn_ln2_w"))

    base["ones_col"] = np.ones((128, 1), np.float32)
    base["ones_row"] = np.ones((1, 128), np.float32)
    base["ident"] = np.eye(128, dtype=np.float32)
    oh8 = np.zeros((EXN, EXN * 128), np.float32)
    for e in range(EXN):
        oh8[e, e * 128:(e + 1) * 128] = 1.0
    base["oh8"] = oh8
    selx = np.zeros((32, 128), np.float32)
    for p in range(128):
        selx[(p // 64) * 16 + (p % 16), p] = 1.0
    base["selx"] = np.concatenate([selx, selx], axis=0)
    selB = np.zeros((32, NST * 128), np.float32)
    selC = np.zeros((128, NST * 128), np.float32)
    for n in range(NST):
        for p in range(128):
            selB[(p // 64) * 16 + n, n * 128 + p] = 1.0
            selC[(p // 16) * 16 + n, n * 128 + p] = 1.0
    base["selB_all"] = np.concatenate([selB, selB], axis=0)
    base["selC_all"] = selC

    ipw = gi("L0_attn_in_proj_w")
    dtw = gi("L0_attn_dt_proj_w")
    alog = gi("L0_attn_A_log")
    dpar = gi("L0_attn_D_param")
    cw = gi("L0_attn_conv_w")
    cb = gi("L0_attn_conv_b")
    dtb = gi("L0_attn_dt_proj_b")

    in_maps = []
    for c in range(NC):
        m = dict(base)
        bb, tt = c // 4, (c % 4) * TOKC
        m["x0"] = np.ascontiguousarray(x[bb, tt:tt + TOKC, :].T)
        d0 = c * DC
        rows = np.concatenate([
            np.arange(d0, d0 + DC),
            4096 + np.arange(c * 128, (c + 1) * 128),
            5120 + np.arange(c * 128, (c + 1) * 128),
            6144 + np.arange(d0, d0 + DC),
            np.arange(10240, 10368),
        ])
        m["ipwT"] = np.ascontiguousarray(ipw[rows, :].T)
        m["dtwT"] = np.ascontiguousarray(dtw[d0:d0 + DC, :].T)

        def loc128(vec2):
            v = vec2.reshape(DC // 128, 128, -1)
            return np.ascontiguousarray(v.transpose(1, 0, 2).reshape(128, -1))
        m["alog128"] = loc128(alog[d0:d0 + DC, :])
        m["d128"] = loc128(dpar[d0:d0 + DC].reshape(-1, 1))
        m["convw128"] = loc128(cw[d0:d0 + DC, 0, :])
        m["convb128"] = loc128(cb[d0:d0 + DC].reshape(-1, 1))
        m["dtb128"] = loc128(dtb[d0:d0 + DC].reshape(-1, 1))
        in_maps.append(m)
    return in_maps


def kernel(**inputs):
    in_maps = _host_prep(inputs)
    nc = build_program()
    res = run_bass_kernel_spmd(nc, in_maps, core_ids=list(range(NC)))
    out = np.zeros((B, T, D), np.float32)
    for c in range(NC):
        bb, tt = c // 4, (c % 4) * TOKC
        out[bb, tt:tt + TOKC, :] = np.asarray(res.results[c]["xout"]).T
    return out
